# version 15
# baseline (speedup 1.0000x reference)
import functools
import operator
import numpy as np
import ml_dtypes
from contextlib import ExitStack

import concourse.bass as bass
import concourse.mybir as mybir
from concourse.bass_utils import run_bass_kernel_spmd

F32 = mybir.dt.float32
BF16 = mybir.dt.bfloat16
I8 = mybir.dt.int8
U8 = mybir.dt.uint8
ALU = mybir.AluOpType
ACT = mybir.ActivationFunctionType
AX = mybir.AxisListType

NCORES = 8
C = 1024
I3 = 3072
II = 2816
NIC = II // 128
HEADS = 8
QMAX = 127.0
EPS_LN = 1e-5
NEG = -30000.0
ZEPS = 1e-30
QT = 124
DEBUG_SPILLS = False
DEBUG_RES = []


def build_nc(T):
    nfull, rem = T // QT, T % QT
    ntiles = nfull + (1 if rem else 0)
    qts = [QT] * nfull + ([rem] if rem else [])
    SLAB = 4 * QT
    nslab_full, srem = T // SLAB, T % SLAB
    slabs = [SLAB] * nslab_full + ([srem] if srem else [])
    nslabs = len(slabs)
    sw = [slabs[s] + 4 for s in range(nslabs)]
    last_s = nslabs - 1
    sw[last_s] = max(sw[last_s], (ntiles - 1) * QT - SLAB * last_s + 128)
    SLOTW = max(sw)
    nxt = (T + 4 + 127) // 128
    XROWS = nxt * 128
    FRW = XROWS
    NTT = T // 128
    assert T % 128 == 0
    tiles_of = [[] for _ in range(nslabs)]
    for j in range(ntiles):
        tiles_of[min(j // 4, nslabs - 1)].append(j)

    nc = bass.Bass("TRN2", target_bir_lowering=False, detect_race_conditions=False)

    ap = lambda *a, **k: nc.dram_tensor(*a, **k).ap()
    xh = ap("xh", [XROWS, C], F32, kind="ExternalInput")
    wqkv = ap("wqkv", [C, I3], F32, kind="ExternalInput")
    wout = ap("wout", [C, C], F32, kind="ExternalInput")
    bout = ap("bout", [1, C], F32, kind="ExternalInput")
    clip2 = ap("clip2", [1, 8], F32, kind="ExternalInput")
    wg = ap("wg", [II, C], F32, kind="ExternalInput")
    wu = ap("wu", [II, C], F32, kind="ExternalInput")
    wd = ap("wd", [C, II], F32, kind="ExternalInput")
    mask1a_d = ap("mask1a", [128, 1024], F32, kind="ExternalInput")
    mask2a_d = ap("mask2a", [128, 1024], F32, kind="ExternalInput")
    mask1z_d = ap("mask1z", [128, 1024], F32, kind="ExternalInput")
    mask2z_d = ap("mask2z", [128, 1024], F32, kind="ExternalInput")
    bmul_d = ap("bmul", [1, ntiles * QT], F32, kind="ExternalInput")
    out = ap("out", [T, C], F32, kind="ExternalOutput")

    spk = dict(kind="ExternalOutput") if DEBUG_SPILLS else {}
    xn_sp = ap("xn_sp", [XROWS, C], F32, **spk)
    s_sp = ap("s_sp", [ntiles * QT, C], BF16, **spk)
    t_sp = ap("t_sp", [NIC, 128, T], F32, **spk)
    lg_sp = ap("lg_sp", [NIC, 128, 8 * 128], BF16)
    lu_sp = ap("lu_sp", [NIC, 128, 8 * 128], BF16)
    mrow_sp = ap("mrow_sp", [1, 128], F32)
    cc1_in = nc.dram_tensor("cc1_in", [1, 8], F32)
    cc1_out = nc.dram_tensor("cc1_out", [1, 8], F32)
    cc2_in = nc.dram_tensor("cc2_in", [1, 8], F32)
    cc2_out = nc.dram_tensor("cc2_out", [1, 8], F32)

    prog = {e: [] for e in ("sync", "vector", "scalar", "tensor", "gpsimd")}
    cnt = {}

    def inc(s, n=1):
        cnt[s] = cnt.get(s, 0) + n
        return cnt[s]

    with ExitStack() as stk:
        E = stk.enter_context

        R1B, R2B, R3B, R4B = 83968, 48000, 47104, 22528
        arena = E(nc.sbuf_tensor("arena", [128, R1B + R2B + R3B + R4B], U8))
        base = nc.lookup_mloc(arena).addr
        OFF = {"R1": 0, "R2": R1B, "R3": R1B + R2B, "R4": R1B + R2B + R3B}
        SZ = {"R1": R1B, "R2": R2B, "R3": R3B, "R4": R4B}
        _n = [0]

        def ovl(region, off, shape, dtype):
            _n[0] += 1
            name = f"ov{_n[0]}"
            hnd = nc.virtual_sbuf_tensor(name, shape, dtype)
            nc._tensor(hnd.name, hnd.shape, hnd.dtype, type="SB")
            mloc = nc.lookup_mloc(hnd)
            mloc.allocated = True
            mloc.addr = base + OFF[region] + off
            nb = functools.reduce(operator.mul, shape[1:], 1) * mybir.dt.size(dtype)
            assert off % 32 == 0 and off + nb <= SZ[region], (region, off, nb)
            return hnd

        wqkv_b = ovl("R1", 0, [128, 8 * I3], BF16)
        xn_fm = ovl("R1", 49152, [128, 8 * FRW], BF16)
        wstg_gu = [ovl("R4", 8192 + i * 4096, [128, C], F32) for i in range(2)]
        wnb_gu = [ovl("R4", 16384 + i * 2048, [128, C], BF16) for i in range(2)]
        wi8_gu = [ovl("R4", 20480 + i * 1024, [128, C], I8) for i in range(2)]
        lhs_stg = ovl("R3", 20480, [128, 8 * 128], BF16)
        xq_fm = ovl("R1", 0, [128, 8 * T], BF16)
        lgs = [ovl("R1", 32768 + i * 2048, [128, 8 * 128], BF16) for i in range(2)]
        lus = [ovl("R1", 36864 + i * 2048, [128, 8 * 128], BF16) for i in range(2)]
        tpre = [ovl("R1", 40960 + i * (T * 4), [128, T], F32) for i in range(2)]
        qk_r = [ovl("R2", i * 16 * SLOTW * 2, [128, 16 * SLOTW], BF16) for i in range(2)]
        vf_r = [ovl("R2", 64 * SLOTW + i * 16 * SLOTW, [128, 8 * SLOTW], BF16)
                for i in range(2)]
        wstg_d = ovl("R2", 0, [128, II], F32)
        wi8_d = ovl("R2", 11264, [128, II], I8)
        wnb_d = ovl("R2", 14112, [128, II], BF16)
        h_cb = [ovl("R2", 21856 + i * 1024, [128, SLAB], BF16) for i in range(2)]
        trd = [ovl("R2", 43712 + i * 2016, [128, SLAB], F32) for i in range(2)]
        wout_b = ovl("R3", 0, [128, 8 * C], BF16)
        e1 = ovl("R3", 16384, [128, 1024], BF16)
        e2 = ovl("R3", 18432, [128, 1024], BF16)
        zb1 = ovl("R3", 20480, [128, 1024], BF16)
        zb2 = ovl("R3", 22528, [128, 1024], BF16)
        r1 = ovl("R3", 24576, [128, 1024], BF16)
        r2 = ovl("R3", 26624, [128, 1024], BF16)
        ww = ovl("R3", 28672, [128, 1024], BF16)
        o_sb = ovl("R3", 30720, [128, 1024], BF16)
        s_t = ovl("R3", 32768, [128, 1024], BF16)
        v_tm = [ovl("R3", 34816 + i * 2048, [128, 1024], BF16) for i in range(2)]
        m1a = ovl("R3", 38912, [128, 1024], BF16)
        m2a = ovl("R3", 40960, [128, 1024], BF16)
        m1z = ovl("R3", 43008, [128, 1024], BF16)
        m2z = ovl("R3", 45056, [128, 1024], BF16)
        rhs_d = ovl("R3", 0, [128, NIC * C], BF16)
        x_t = [ovl("R4", i * 4096, [128, C], F32) for i in range(2)]
        xn_t = [ovl("R4", 8192 + i * 4096, [128, C], F32) for i in range(2)]
        xnb_t = [ovl("R4", 16384 + i * 2048, [128, C], BF16) for i in range(2)]
        a1 = ovl("R4", 0, [128, 1024], F32)
        a2 = ovl("R4", 4096, [128, 1024], F32)
        xq_t = [ovl("R4", i * 2048, [128, C], BF16) for i in range(2)]
        xq_i8 = ovl("R4", 4096, [128, C], I8)
        uq = ovl("R4", 5120, [128, C], F32)
        silu_g = [ovl("R4", 5120 + i * 2048, [128, 512], F32) for i in range(2)]
        xrd_d = [ovl("R4", 9216 + i * 4096, [128, C], F32) for i in range(2)]
        out_t = [ovl("R4", i * 4096, [128, C], F32) for i in range(2)]
        x_rd = [ovl("R4", 8192 + i * 4096, [128, C], F32) for i in range(2)]
        g_t = [ovl("R4", 16384, [128, C], BF16), ovl("R3", 45056, [128, C], BF16)]
        s_rd = [ovl("R4", 18432 + i * 2048, [128, C], BF16) for i in range(2)]

        sb = lambda name, shape, dt: E(nc.sbuf_tensor(name, shape, dt))
        bmul_s = sb("bmul_s", [1, ntiles * QT], BF16)
        bout_b = sb("bout_b", [1, C], BF16)
        clip_s = sb("clip_s", [1, 8], F32)
        onesK = sb("onesK", [128, 128], BF16)
        ones1f = sb("ones1f", [1, 128], F32)
        epsrow = sb("epsrow", [1, 512], F32)
        ln_s = sb("ln_s", [128, 8], F32)
        amax_x = sb("amax_x", [128, 1], F32)
        amax_t = sb("amax_t", [128, 1], F32)
        red1 = sb("red1", [128, 1], F32)
        mrow = sb("mrow", [1, 128], F32)
        m11 = sb("m11", [1, 1], F32)
        mbias = sb("mbias", [128, 1], F32)
        gmax1 = sb("gmax1", [1, 8], F32)
        gmax2 = sb("gmax2", [1, 8], F32)
        sc_s = sb("sc_s", [1, 8], F32)
        sx_b = sb("sx_b", [128, 1], F32)
        invsx_b = sb("invsx_b", [128, 1], F32)
        sh_b = sb("sh_b", [128, 1], F32)
        invsh_b = sb("invsh_b", [128, 1], F32)
        winv = sb("winv", [128, 8], F32)
        h_i8 = sb("h_i8", [128, SLAB], I8)
        epsb = sb("epsb", [128, 1], F32)
        cneg_b = sb("cneg_b", [128, 1], F32)

        pA = E(nc.psum_tensor("pA", [128, 1024], F32))
        pB = E(nc.psum_tensor("pB", [128, 1024], F32))
        pC = E(nc.psum_tensor("pC", [128, 1024], F32))
        pD = E(nc.psum_tensor("pD", [128, 512], F32))
        pE = E(nc.psum_tensor("pE", [128, 512], F32))

        sem = {}

        def S(n):
            if n not in sem:
                sem[n] = E(nc.semaphore(n))
            return sem[n]

        for _nm in ("CV", "CS", "CT", "CG"):
            S(_nm)

        def emit(eng, f):
            prog[eng].append(f)

        def dma(eng, out_ap, in_ap, s, transpose=False):
            S(s)
            n = inc(s, 16)
            def f(e, out_ap=out_ap, in_ap=in_ap, s=s, transpose=transpose):
                if transpose:
                    e.dma_start(out=out_ap, in_=in_ap, transpose=True).then_inc(sem[s], 16)
                else:
                    e.dma_start(out=out_ap, in_=in_ap).then_inc(sem[s], 16)
            emit(eng, f)
            return n

        def wait(eng, s, val):
            S(s)
            if val > 0:
                emit(eng, lambda e, s=s, val=val: e.wait_ge(sem[s], val))

        ENG_SEM = {"vector": "CV", "scalar": "CS", "tensor": "CT", "gpsimd": "CG"}

        def op(eng, s, fmaker):
            assert s == ENG_SEM[eng]
            if eng in ("vector", "scalar"):
                prev = cnt.get(s, 0)
                if prev > 0:
                    emit(eng, lambda e, s=s, prev=prev: e.wait_ge(sem[s], prev))
            n = inc(s)
            emit(eng, lambda e, fmaker=fmaker, s=s: fmaker(e).then_inc(sem[s], 1))
            return n

        def opq(eng, fmaker):
            if eng in ("vector", "scalar"):
                op(eng, ENG_SEM[eng], fmaker)
            else:
                emit(eng, lambda e, fmaker=fmaker: fmaker(e))

        def opw(eng, fmaker):
            return op(eng, ENG_SEM[eng], fmaker)

        st3 = lambda t, qn: t[:, :].rearrange("p (h q) -> p h q", h=8)[:, :, 0:qn]

        dma("sync", clip_s[:], clip2, s="LC")
        dma("gpsimd", bmul_s[:], bmul_d, s="LCG")
        dma("gpsimd", bout_b[:], bout, s="LCG")
        lcg_of = [0] * 24
        wqkv_b3 = wqkv_b[:, :].rearrange("p (c n) -> p c n", c=8)
        wqkv_s3 = wqkv.rearrange("(c p) n -> p c n", p=128)

        def ld_wqkv_chunk(jj):
            lcg_of[jj] = dma("gpsimd", wqkv_b3[:, :, jj * 128:(jj + 1) * 128],
                             wqkv_s3[:, :, jj * 128:(jj + 1) * 128], s="LCG")

        for jj in range(4):
            ld_wqkv_chunk(jj)
        for (dst, srcd) in ((m1a, mask1a_d), (m2a, mask2a_d), (m1z, mask1z_d),
                            (m2z, mask2z_d)):
            dma("gpsimd", dst[:], srcd, s="LCG")
        ld_masks = cnt["LCG"]
        for jj in range(4, 10):
            ld_wqkv_chunk(jj)
        for ccn in range(8):
            dma("gpsimd", wout_b[:, ccn * C:(ccn + 1) * C],
                wout[ccn * 128:(ccn + 1) * 128, :], s="LCG")
        ld_wout = cnt["LCG"]
        for jj in range(10, 24):
            ld_wqkv_chunk(jj)
        ld_wq = cnt["LCG"]
        ld_const = cnt["LC"]

        wait("vector", "LC", ld_const)
        opq("vector", lambda e: e.memset(onesK[:], 1.0))
        opq("vector", lambda e: e.memset(ones1f[:], 1.0))
        opq("vector", lambda e: e.memset(epsrow[:], ZEPS))
        opq("vector", lambda e: e.memset(amax_x[:], 0.0))
        opq("vector", lambda e: e.memset(amax_t[:], 0.0))
        opq("vector", lambda e: e.memset(epsb[:], EPS_LN))
        opq("vector", lambda e: e.memset(cneg_b[:], -128.0))
        opq("vector", lambda e: e.memset(pA[:, :], 0.0))
        pa_init = op("vector", "CV", lambda e: e.memset(pB[:, :], 0.0))

        tr_done = [0] * nxt
        sp_done = [0] * nxt
        xa_done = [0] * nxt
        for i in range(nxt):
            b = i % 2
            if i >= 2:
                wait("scalar", "CV", xa_done[i - 2])
            ldx = dma("scalar", x_t[b][:], xh[i * 128:(i + 1) * 128, :], s=f"LXX{b}")
            wait("vector", f"LXX{b}", ldx)
            if i >= 2:
                wait("vector", f"LGX{b}", sp_done[i - 2])
            opw("vector", lambda e, b=b: e.reduce_sum(out=ln_s[:, 0:1], in_=x_t[b][:], axis=AX.X))
            wait("scalar", f"LXX{b}", ldx)
            if i >= 2:
                wait("scalar", f"LTA{b}", tr_done[i - 2])
            sq = op("scalar", "CS", lambda e, b=b: e.activation(
                out=xnb_t[b][:], in_=x_t[b][:], func=ACT.Square, accum_out=ln_s[:, 1:2]))
            wait("vector", "CS", sq)
            opw("vector", lambda e: e.tensor_scalar(out=ln_s[:, 2:3], in0=ln_s[:, 0:1],
                                                    scalar1=1.0 / C, scalar2=None, op0=ALU.mult))
            opq("vector", lambda e: e.tensor_scalar(out=ln_s[:, 3:4], in0=ln_s[:, 1:2],
                                                    scalar1=1.0 / C, scalar2=None, op0=ALU.mult))
            opw("vector", lambda e: e.tensor_tensor(out=ln_s[:, 4:5], in0=ln_s[:, 2:3],
                                                    in1=ln_s[:, 2:3], op=ALU.mult))
            vv = op("vector", "CV", lambda e: e.tensor_tensor(out=ln_s[:, 4:5], in0=ln_s[:, 3:4],
                                                              in1=ln_s[:, 4:5], op=ALU.subtract))
            wait("scalar", "CV", vv)
            sq2 = op("scalar", "CS", lambda e: e.activation(out=ln_s[:, 5:6], in_=ln_s[:, 4:5],
                                                            func=ACT.Sqrt, bias=epsb[:]))
            wait("vector", "CS", sq2)
            opw("vector", lambda e: e.reciprocal(out=ln_s[:, 6:7], in_=ln_s[:, 5:6]))
            opw("vector", lambda e, b=b: e.scalar_tensor_tensor(
                out=xn_t[b][:], in0=x_t[b][:], scalar=ln_s[:, 2:3],
                in1=ln_s[:, 6:7].broadcast_to([128, C]),
                op0=ALU.subtract, op1=ALU.mult))
            opw("vector", lambda e, b=b: e.reduce_max(out=red1[:], in_=xn_t[b][:], axis=AX.X,
                                                      apply_absolute_value=True))
            opq("vector", lambda e: e.tensor_tensor(out=amax_x[:], in0=amax_x[:],
                                                    in1=red1[:], op=ALU.max))
            xa_done[i] = op("vector", "CV",
                            lambda e, b=b: e.tensor_copy(out=xnb_t[b][:], in_=xn_t[b][:]))
            wait("gpsimd", "CV", xa_done[i])
            sp_done[i] = dma("gpsimd", xn_sp[i * 128:(i + 1) * 128, :], xn_t[b][:], s=f"LGX{b}")
            wait("sync", "CV", xa_done[i])
            tr_done[i] = dma("sync",
                             xn_fm[:, :].rearrange("p (c f) -> p c f", c=8)[:, :, i * 128:(i + 1) * 128],
                             xnb_t[b][:, :].rearrange("p (c q) -> p c q", c=8),
                             s=f"LTA{b}", transpose=True)
        opw("vector", lambda e: e.tensor_copy(out=red1[0:1, 0:1], in_=red1[0:1, 0:1]))
        amx = op("vector", "CV", lambda e: e.tensor_copy(out=red1[:], in_=amax_x[:]))
        wait("gpsimd", "CV", amx)
        dma("gpsimd", mrow_sp, red1[:, 0:1], s="LGM")
        wait("gpsimd", "LGM", cnt["LGM"])
        g1 = dma("gpsimd", mrow[:], mrow_sp, s="LGM")
        wait("vector", "LGM", g1)
        opq("vector", lambda e: e.reduce_max(out=m11[:], in_=mrow[:], axis=AX.X))
        bc1 = op("vector", "CV",
                 lambda e: e.tensor_copy(out=gmax1[:], in_=m11[:].broadcast_to([1, 8])))
        wait("gpsimd", "CV", bc1)
        cci = dma("gpsimd", cc1_in.ap(), gmax1[:], s="LGC")
        wait("gpsimd", "LGC", cci)
        cc1n = op("gpsimd", "CG", lambda e: e.collective_compute(
            "AllReduce", ALU.max, replica_groups=[list(range(NCORES))],
            ins=[cc1_in.ap().opt()], outs=[cc1_out.ap().opt()]))
        wait("gpsimd", "CG", cc1n)
        cco1 = dma("gpsimd", gmax1[:], cc1_out.ap(), s="LGC")

        w1_state = {"stg_free": 0, "i8_free": 0, "lg_done": 0, "emitted": 0,
                    "trn_b": [0, 0], "ld_of": [0, 0], "stt_of": [0, 0],
                    "sc_of": [0, 0]}
        NW1 = 2 * NIC

        def w1_src(idx):
            src_ap = wg if idx < NIC else wu
            ch = idx % NIC
            return src_ap[ch * 128:(ch + 1) * 128, :]

        def emit_w1_chunk(idx):
            dst_sp = lg_sp if idx < NIC else lu_sp
            ch = idx % NIC
            b = idx % 2
            o4 = b * 4
            if idx == 0:
                w1_state["ld_of"][0] = dma("sync", wstg_gu[0][:], w1_src(0), s="LXW0")
            if idx + 1 < NW1:
                nb = (idx + 1) % 2
                wait("sync", "CV", w1_state["stt_of"][nb])
                w1_state["ld_of"][nb] = dma("sync", wstg_gu[nb][:], w1_src(idx + 1), s="LXW0")
            wait("vector", "LXW0", w1_state["ld_of"][b])
            wait("vector", "CS", w1_state["sc_of"][b])
            opw("vector", lambda e, o4=o4, b=b: e.reduce_max(
                out=winv[:, o4:o4 + 1], in_=wstg_gu[b][:], axis=AX.X,
                apply_absolute_value=True))
            opq("vector", lambda e, o4=o4: e.tensor_scalar(
                out=winv[:, o4 + 1:o4 + 2], in0=winv[:, o4:o4 + 1],
                scalar1=1.0 / QMAX, scalar2=None, op0=ALU.mult))
            opw("vector", lambda e, o4=o4: e.reciprocal(
                out=winv[:, o4 + 2:o4 + 3], in_=winv[:, o4:o4 + 1]))
            opw("vector", lambda e, o4=o4: e.tensor_scalar(
                out=winv[:, o4 + 2:o4 + 3], in0=winv[:, o4 + 2:o4 + 3],
                scalar1=QMAX * (1.0 - 2.0 ** -22), scalar2=None, op0=ALU.mult))
            i8n = op("vector", "CV", lambda e, o4=o4, b=b: e.scalar_tensor_tensor(
                out=wi8_gu[b][:], in0=wstg_gu[b][:], scalar=winv[:, o4 + 2:o4 + 3],
                in1=wstg_gu[b][:], op0=ALU.mult, op1=ALU.bypass))
            w1_state["i8_free"] = i8n
            w1_state["stt_of"][b] = i8n
            wait("scalar", "CV", i8n)
            wait("scalar", "LTW", w1_state["trn_b"][b])
            sc = op("scalar", "CS", lambda e, b=b, o4=o4: e.activation(
                out=wnb_gu[b][:], in_=wi8_gu[b][:], func=ACT.Copy,
                scale=winv[:, o4 + 1:o4 + 2]))
            wait("sync", "CS", sc)
            wait("sync", "LGL", w1_state["lg_done"])
            trn = dma("sync", lhs_stg[:, :].rearrange("p (c q) -> p c q", c=8),
                      wnb_gu[b][:, :].rearrange("p (c q) -> p c q", c=8),
                      s="LTW", transpose=True)
            w1_state["trn_b"][b] = trn
            w1_state["sc_of"][b] = sc
            wait("sync", "LTW", trn)
            w1_state["lg_done"] = dma("sync", dst_sp[ch, :, :], lhs_stg[:, :], s="LGL")

        def w1_drive(frac):
            want = min(NW1, int(NW1 * frac + 0.999))
            while w1_state["emitted"] < want:
                emit_w1_chunk(w1_state["emitted"])
                w1_state["emitted"] += 1

        wait("tensor", "CV", pa_init)
        wait("sync", "LGX0", sp_done[(nxt - 2) - (nxt % 2 == 1)] if nxt >= 2 else 0)
        wait("sync", "LGX0", max(sp_done[i] for i in range(0, nxt, 2)))
        wait("sync", "LGX1", max(sp_done[i] for i in range(1, nxt, 2)))
        wait("sync", "LTA0", max(tr_done[i] for i in range(0, nxt, 2)))
        wait("sync", "LTA1", max(tr_done[i] for i in range(1, nxt, 2)))
        qkv_done = [0] * nslabs
        slab_av = [0] * nslabs
        oev_prev = 0
        sev_prev = 0
        ssp_prev = 0
        vtm_av = [0] * ntiles
        vt_of = [0] * ntiles
        gather_of = [0] * ntiles
        aan_of = [0] * (ntiles + 1)
        ph1_done = [False] * ntiles
        mred_prev = [0]

        qkv_ev = [0, 0]
        qkv_kk = [0]
        qkv_em = [0] * nslabs

        def emit_qkv_groups(s, ngroups):
            if ngroups <= 0 or qkv_em[s] >= 24:
                return
            wdt = sw[s]
            basec = s * SLAB
            rslot = s % 2
            if qkv_em[s] == 0:
                need_tile = min(nxt - 1, (basec + wdt - 1) // 128)
                wait("tensor", f"LTA{need_tile % 2}", tr_done[need_tile])
            wait("tensor", "CS", sev_prev)
            while ngroups > 0 and qkv_em[s] < 24:
                j = qkv_em[s]
                kk = qkv_kk[0]
                pq = pD if kk % 2 == 0 else pE
                wait("tensor", "LCG", lcg_of[j])
                wait("tensor", "CV", qkv_ev[kk % 2])
                mmn = 0
                for ccn in range(8):
                    def mk(e, j=j, ccn=ccn, basec=basec, wdt=wdt, pq=pq):
                        return e.matmul(
                            pq[:, 0:wdt],
                            lhsT=wqkv_b[:, ccn * I3 + j * 128:ccn * I3 + (j + 1) * 128],
                            rhs=xn_fm[:, ccn * FRW + basec: ccn * FRW + basec + wdt],
                            start=(ccn == 0), stop=(ccn == 7))
                    if ccn == 7:
                        mmn = op("tensor", "CT", mk)
                    else:
                        opq("tensor", mk)
                wait("vector", "CT", mmn)
                if j == 0 and s >= 2:
                    wait("vector", "CT", slab_av[s - 2])
                if j < 8:
                    def ev(e, j=j, rslot=rslot, wdt=wdt, pq=pq):
                        return e.tensor_scalar(
                            out=qk_r[rslot][:, j * SLOTW:j * SLOTW + wdt],
                            in0=pq[:, 0:wdt], scalar1=1.0 / 32.0, scalar2=None,
                            op0=ALU.mult)
                elif j < 16:
                    def ev(e, j=j, rslot=rslot, wdt=wdt, pq=pq):
                        return e.tensor_copy(
                            out=qk_r[rslot][:, j * SLOTW:j * SLOTW + wdt],
                            in_=pq[:, 0:wdt])
                else:
                    def ev(e, j=j, rslot=rslot, wdt=wdt, pq=pq):
                        return e.tensor_copy(
                            out=vf_r[rslot][:, (j - 16) * SLOTW:(j - 16) * SLOTW + wdt],
                            in_=pq[:, 0:wdt])
                qkv_ev[kk % 2] = op("vector", "CV", ev)
                qkv_kk[0] += 1
                qkv_em[s] += 1
                ngroups -= 1
            if qkv_em[s] == 24 and qkv_done[s] == 0:
                qkv_done[s] = cnt["CV"]

        def emit_phase1(j, s, rslot, qn, kbase):
            stX = pA if j % 2 == 0 else pB
            wait("tensor", "CV", qkv_done[s])
            stn = 0
            for h in range(HEADS):
                def mk(e, h=h, rslot=rslot, kbase=kbase, qn=qn, stX=stX):
                    return e.matmul(
                        stX[:, h * 128:h * 128 + qn],
                        lhsT=qk_r[rslot][:, (8 + h) * SLOTW + kbase:(8 + h) * SLOTW + kbase + 128],
                        rhs=qk_r[rslot][:, h * SLOTW + kbase + 2: h * SLOTW + kbase + 2 + qn],
                        start=True, stop=True)
                if h == HEADS - 1:
                    stn = op("tensor", "CT", mk)
                else:
                    opq("tensor", mk)
            wait("vector", "CT", stn)
            if j >= 1:
                wait("vector", "LGM", gather_of[j - 1])
            mred = op("vector", "CV", lambda e, qn=qn, stX=stX: e.reduce_max(
                out=red1[:], in_=st3(stX, qn), axis=AX.XY, negate=True))
            wait("gpsimd", "CV", mred)
            dma("gpsimd", mrow_sp, red1[:, 0:1], s="LGM")
            wait("gpsimd", "LGM", cnt["LGM"])
            gather_of[j] = dma("gpsimd", mrow[:], mrow_sp, s="LGM")

        emit_qkv_groups(0, 24)
        if nslabs > 1:
            emit_qkv_groups(1, 10)
        for s in range(nslabs):
            rslot = s % 2
            for j in tiles_of[s]:
                qn = qts[j]
                kbase = QT * j - s * SLAB
                vb = j % 2
                if j >= 2:
                    wait("sync", "CT", vtm_av[j - 2])
                wait("sync", "CV", qkv_done[s])
                vt = 0
                for ccn in range(8):
                    vt = dma("sync", v_tm[vb][:, ccn * 128:(ccn + 1) * 128],
                             vf_r[rslot][:, ccn * SLOTW + kbase: ccn * SLOTW + kbase + 128],
                             s=f"LTV{vb}", transpose=True)
                vt_of[j] = vt
                if not ph1_done[j]:
                    emit_phase1(j, s, rslot, qn, kbase)
                stX = pA if j % 2 == 0 else pB
                g_j = gather_of[j]
                wait("vector", "LGM", g_j)
                mv = op("vector", "CV", lambda e: e.tensor_reduce(
                    out=m11[:], in_=mrow[:], axis=AX.X, op=ALU.min))
                wait("tensor", "CV", mv)
                mb = op("tensor", "CT", lambda e: e.matmul(
                    pC[:, 124:125], lhsT=ones1f[:, :], rhs=m11[:], start=True, stop=True))
                wait("vector", "CT", mb)
                opq("vector", lambda e: e.tensor_copy(out=mbias[:], in_=pC[:, 124:125]))
                km1 = m1z if j == ntiles - 1 else m1a
                km2 = m2z if j == 0 else m2a
                if j == 0:
                    wait("vector", "LCG", ld_masks)
                opq("vector", lambda e, km1=km1, stX=stX: e.tensor_tensor(
                    out=a1[:], in0=stX[:, :], in1=km1[:], op=ALU.add))
                aan = op("vector", "CV", lambda e, km2=km2, stX=stX: e.tensor_tensor(
                    out=a2[:], in0=stX[:, :], in1=km2[:], op=ALU.add))
                aan_of[j] = aan
                wait("scalar", "CV", aan)
                opq("scalar", lambda e: e.activation(out=e1[:], in_=a1[:], func=ACT.Exp,
                                                     bias=mbias[:]))
                een = op("scalar", "CS", lambda e: e.activation(out=e2[:], in_=a2[:],
                                                                func=ACT.Exp, bias=mbias[:]))
                wait("tensor", "CS", een)
                edge = (j == 0) or (j == ntiles - 1)
                zn = 0
                for (eb, pz_) in ((e1, stX), (e2, pC)):
                    for half in range(2):
                        n0 = half * 512
                        if edge:
                            opq("tensor", lambda e, pz_=pz_, n0=n0: e.matmul(
                                pz_[:, n0:n0 + 512], lhsT=ones1f[:, :], rhs=epsrow[:, :],
                                start=True, stop=False))
                        def mkz(e, eb=eb, pz_=pz_, n0=n0, edge=edge):
                            return e.matmul(pz_[:, n0:n0 + 512], lhsT=onesK[:, :],
                                            rhs=eb[:, n0:n0 + 512], start=not edge, stop=True)
                        if pz_ is pC and half == 1:
                            zn = op("tensor", "CT", mkz)
                        else:
                            opq("tensor", mkz)
                jn = j + 1
                if jn < ntiles and not ph1_done[jn]:
                    s_n = min(jn // 4, nslabs - 1)
                    if s_n == s or qkv_em[s_n] >= 24:
                        emit_phase1(jn, s_n, s_n % 2, qts[jn], QT * jn - s_n * SLAB)
                        ph1_done[jn] = True
                if s + 1 < nslabs:
                    emit_qkv_groups(s + 1, min(3, min(24, 8 * (j - 4 * s + 1)) - qkv_em[s + 1]))
                wait("scalar", "CT", zn)
                opq("scalar", lambda e, stX=stX: e.activation(out=a1[:], in_=stX[:, :],
                                                              func=ACT.Ln))
                opq("scalar", lambda e: e.activation(out=a2[:], in_=pC[:, :], func=ACT.Ln))
                opq("scalar", lambda e: e.activation(out=r1[:], in_=a1[:], func=ACT.Exp,
                                                     scale=-1.0))
                rr2 = op("scalar", "CS", lambda e: e.activation(out=r2[:], in_=a2[:],
                                                                func=ACT.Exp, scale=-1.0))
                wait("vector", "CS", rr2)
                opq("vector", lambda e: e.tensor_tensor(out=ww[:], in0=e1[:], in1=r1[:], op=ALU.mult))
                opw("vector", lambda e: e.tensor_tensor(out=r2[:], in0=e2[:], in1=r2[:],
                                                        op=ALU.mult))
                wn = op("vector", "CV", lambda e: e.tensor_tensor(out=ww[:], in0=ww[:],
                                                                  in1=r2[:], op=ALU.add))
                wait("tensor", "CV", wn)
                wait("tensor", f"LTV{vb}", vt_of[j])
                wait("tensor", "CS", oev_prev)
                avn = 0
                for h in range(HEADS):
                    def mk(e, h=h, vb=vb, qn=qn):
                        return e.matmul(
                            pC[:, h * 128:h * 128 + qn],
                            lhsT=v_tm[vb][:, h * 128:(h + 1) * 128],
                            rhs=ww[:, h * 128:h * 128 + qn],
                            start=True, stop=True)
                    if h == HEADS - 1:
                        avn = op("tensor", "CT", mk)
                    else:
                        opq("tensor", mk)
                vtm_av[j] = avn
                slab_av[s] = avn
                wait("scalar", "CT", avn)
                oev = op("scalar", "CS", lambda e: e.activation(out=o_sb[:], in_=pC[:, :],
                                                                func=ACT.Copy))
                oev_prev = oev
                if j == 0:
                    wait("tensor", "LCG", ld_wout)
                wait("tensor", "CS", oev)
                wait("tensor", "CS", sev_prev)
                wait("tensor", "CV", qkv_ev[0])
                wait("tensor", "CV", qkv_ev[1])
                for half, pq_ in ((0, pD), (1, pE)):
                    n0 = half * 512
                    for ccn in range(8):
                        opq("tensor", lambda e, ccn=ccn, n0=n0, qn=qn, pq_=pq_: e.matmul(
                            pq_[0:qn, :],
                            lhsT=o_sb[:, ccn * 128:ccn * 128 + qn],
                            rhs=wout_b[:, ccn * C + n0:ccn * C + n0 + 512],
                            start=(ccn == 0), stop=False))
                    won = op("tensor", "CT", lambda e, j=j, n0=n0, qn=qn, pq_=pq_: e.matmul(
                        pq_[0:qn, :],
                        lhsT=bmul_s[:, j * QT:j * QT + qn],
                        rhs=bout_b[:, n0:n0 + 512], start=False, stop=True))
                wait("scalar", "CT", won)
                wait("scalar", "LGS", ssp_prev)
                opq("scalar", lambda e, qn=qn: e.activation(
                    out=s_t[0:qn, 0:512], in_=pD[0:qn, :], func=ACT.Copy))
                sev = op("scalar", "CS", lambda e, qn=qn: e.activation(
                    out=s_t[0:qn, 512:1024], in_=pE[0:qn, :], func=ACT.Copy))
                sev_prev = sev
                wait("gpsimd", "CS", sev)
                ssp_prev = dma("gpsimd", s_sp[j * QT:j * QT + qn, :], s_t[0:qn, :], s="LGS")
                if s + 1 < nslabs:
                    emit_qkv_groups(s + 1, min(24, 8 * (j - 4 * s + 1)) - qkv_em[s + 1])
                w1_drive((j + 1) / ntiles)

        wait("scalar", "LGC", cco1)
        sg1 = op("scalar", "CS", lambda e: e.activation(out=sc_s[:, 0:2], in_=clip_s[:, 0:2],
                                                        func=ACT.Sigmoid))
        wait("vector", "CS", sg1)
        opw("vector", lambda e: e.tensor_tensor(out=sc_s[:, 2:3], in0=sc_s[:, 0:1],
                                                in1=gmax1[:, 0:1], op=ALU.mult))
        opw("vector", lambda e: e.tensor_scalar(out=sc_s[:, 2:3], in0=sc_s[:, 2:3],
                                                scalar1=1.0 / QMAX, scalar2=None, op0=ALU.mult))
        sxv = op("vector", "CV", lambda e: e.reciprocal(out=sc_s[:, 3:4], in_=sc_s[:, 2:3]))
        wait("tensor", "CV", sxv)
        bb = op("tensor", "CT", lambda e: e.matmul(pE[:, 4:6], lhsT=ones1f[:, :],
                                                   rhs=sc_s[:, 2:4], start=True, stop=True))
        wait("vector", "CT", bb)
        opq("vector", lambda e: e.tensor_copy(out=sx_b[:], in_=pE[:, 4:5]))
        sxb = op("vector", "CV", lambda e: e.tensor_copy(out=invsx_b[:], in_=pE[:, 5:6]))

        wait("sync", "CV", w1_state["i8_free"])
        wait("sync", "LTW", cnt.get("LTW", 0))
        wait("sync", "LGL", w1_state["lg_done"])
        wait("scalar", "CV", w1_state["i8_free"])
        wait("scalar", "LTW", cnt.get("LTW", 0))
        xq_done = 0
        qv_done = [0, 0]
        stt_done = [0, 0]
        for m in range(NTT):
            tb = m % 2
            wait("scalar", "CV", stt_done[tb])
            ldx = dma("scalar", xrd_d[tb][:], xn_sp[2 + m * 128: 2 + (m + 1) * 128, :], s=f"LXD{tb}")
            wait("vector", f"LXD{tb}", ldx)
            if m == 0:
                wait("vector", "CV", sxb)
            stt_done[tb] = opw("vector", lambda e, tb=tb: e.scalar_tensor_tensor(
                out=uq[:], in0=xrd_d[tb][:], scalar=invsx_b[:],
                in1=cneg_b[:, 0:1].broadcast_to([128, C]), op0=ALU.mult, op1=ALU.max))
            opw("vector", lambda e: e.tensor_scalar(out=xq_i8[:], in0=uq[:], scalar1=127.0,
                                                    scalar2=None, op0=ALU.min))
            if m >= 2:
                wait("vector", f"LTQ{tb}", qv_done[tb])
            qv = op("vector", "CV", lambda e, tb=tb: e.tensor_copy(
                out=xq_t[tb][:], in_=xq_i8[:]))
            wait("sync", "CV", qv)
            xq_done = dma("sync",
                          xq_fm[:, :].rearrange("p (c f) -> p c f", c=8)[:, :, m * 128:(m + 1) * 128],
                          xq_t[tb][:, :].rearrange("p (c q) -> p c q", c=8),
                          s=f"LTQ{tb}", transpose=True)
            qv_done[tb] = xq_done

        w2_state = {"wd_tr": 0, "wstg_fr": 0, "emitted": 0}

        def emit_w2_chunk(hch):
            if hch == 0:
                wait("gpsimd", "CT", slab_av[nslabs - 1])
            wait("gpsimd", "CV", w2_state["wstg_fr"])
            ldw = dma("gpsimd", wstg_d[:], wd[hch * 128:(hch + 1) * 128, :], s="LXV")
            wait("vector", "LXV", ldw)
            opw("vector", lambda e: e.reduce_max(out=winv[:, 0:1], in_=wstg_d[:], axis=AX.X,
                                                 apply_absolute_value=True))
            opq("vector", lambda e: e.tensor_scalar(out=winv[:, 1:2], in0=winv[:, 0:1],
                                                    scalar1=1.0 / QMAX, scalar2=None, op0=ALU.mult))
            opw("vector", lambda e: e.reciprocal(out=winv[:, 2:3], in_=winv[:, 0:1]))
            opw("vector", lambda e: e.tensor_scalar(
                out=winv[:, 2:3], in0=winv[:, 2:3], scalar1=QMAX * (1.0 - 2.0 ** -22),
                scalar2=None, op0=ALU.mult))
            i8n = op("vector", "CV", lambda e: e.scalar_tensor_tensor(
                out=wi8_d[:], in0=wstg_d[:], scalar=winv[:, 2:3],
                in1=wstg_d[:], op0=ALU.mult, op1=ALU.bypass))
            w2_state["wstg_fr"] = i8n
            wait("scalar", "CV", i8n)
            wait("scalar", "LTD", w2_state["wd_tr"])
            sc = op("scalar", "CS", lambda e: e.activation(
                out=wnb_d[:], in_=wi8_d[:], func=ACT.Copy, scale=winv[:, 1:2]))
            wait("sync", "CS", sc)
            w2_state["wd_tr"] = dma("sync",
                        rhs_d[:, :].rearrange("p (c f) -> p c f", c=NIC)[:, :, hch * 128:(hch + 1) * 128],
                        wnb_d[:, :].rearrange("p (c q) -> p c q", c=NIC),
                        s="LTD", transpose=True)

        def w2_drive(want):
            while w2_state["emitted"] < min(want, 8):
                emit_w2_chunk(w2_state["emitted"])
                w2_state["emitted"] += 1

        wait("tensor", "LTQ0", qv_done[0])
        wait("tensor", "LTQ1", qv_done[1])
        tpn_par = [0, 0]
        tp_sp = [0, 0]
        tp_sp_of = [0] * NIC
        ld_used = [0, 0]
        for ch in range(NIC):
            b = ch % 2
            if ch == 0:
                wait("sync", "LGL", w1_state["lg_done"])
            wait("sync", "CT", ld_used[b])
            dma("sync", lgs[b][:], lg_sp[ch, :, :], s=f"LXL{b}")
            ldu = dma("sync", lus[b][:], lu_sp[ch, :, :], s=f"LXL{b}")
            wait("tensor", f"LXL{b}", ldu)
            mmg = 0
            for nn in range((T + 511) // 512):
                n0 = nn * 512
                nw = min(512, T - n0)
                pg = pD if nn % 2 == 0 else pA[:, 0:512]
                pu = pE if nn % 2 == 0 else pA[:, 512:1024]
                wait("tensor", "CV", tpn_par[nn % 2])
                for ccn in range(8):
                    opq("tensor", lambda e, b=b, ccn=ccn, n0=n0, nw=nw, pg=pg: e.matmul(
                        pg[:, 0:nw], lhsT=lgs[b][:, ccn * 128:(ccn + 1) * 128],
                        rhs=xq_fm[:, ccn * T + n0: ccn * T + n0 + nw],
                        start=(ccn == 0), stop=(ccn == 7)))
                for ccn in range(8):
                    def mku(e, b=b, ccn=ccn, n0=n0, nw=nw, pu=pu):
                        return e.matmul(
                            pu[:, 0:nw], lhsT=lus[b][:, ccn * 128:(ccn + 1) * 128],
                            rhs=xq_fm[:, ccn * T + n0: ccn * T + n0 + nw],
                            start=(ccn == 0), stop=(ccn == 7))
                    if ccn == 7:
                        mmg = op("tensor", "CT", mku)
                    else:
                        opq("tensor", mku)
                wait("scalar", "CT", mmg)
                sg = silu_g[nn % 2]
                sgn = op("scalar", "CS", lambda e, nw=nw, pg=pg, sg=sg: e.activation(
                    out=sg[:, 0:nw], in_=pg[:, 0:nw], func=ACT.Silu, scale=sx_b[:]))
                wait("vector", "CS", sgn)
                if nn == 0 and ch >= 2:
                    wait("vector", f"LGT{b}", tp_sp[b])
                tpn = op("vector", "CV", lambda e, b=b, n0=n0, nw=nw, pu=pu, sg=sg: e.scalar_tensor_tensor(
                    out=tpre[b][:, n0:n0 + nw], in0=pu[:, 0:nw], scalar=sx_b[:],
                    in1=sg[:, 0:nw], op0=ALU.mult, op1=ALU.mult))
                tpn_par[nn % 2] = tpn
            ld_used[b] = cnt["CT"]
            opw("vector", lambda e, b=b: e.reduce_max(out=red1[:], in_=tpre[b][:],
                                                      axis=AX.X, apply_absolute_value=True))
            tpv = op("vector", "CV", lambda e: e.tensor_tensor(out=amax_t[:], in0=amax_t[:],
                                                               in1=red1[:], op=ALU.max))
            wait("gpsimd", "CV", tpv)
            tp_sp[b] = dma("gpsimd", t_sp[ch, :, :], tpre[b][:], s=f"LGT{b}")
            tp_sp_of[ch] = tp_sp[b]
            w2_drive((ch + 1) * 8 // NIC + 1)

        opw("vector", lambda e: e.tensor_copy(out=red1[0:1, 0:1], in_=red1[0:1, 0:1]))
        am2 = op("vector", "CV", lambda e: e.tensor_copy(out=red1[:], in_=amax_t[:]))
        wait("gpsimd", "CV", am2)
        dma("gpsimd", mrow_sp, red1[:, 0:1], s="LGM")
        wait("gpsimd", "LGM", cnt["LGM"])
        g2 = dma("gpsimd", mrow[:], mrow_sp, s="LGM")
        wait("vector", "LGM", g2)
        opq("vector", lambda e: e.reduce_max(out=m11[:], in_=mrow[:], axis=AX.X))
        bc2 = op("vector", "CV",
                 lambda e: e.tensor_copy(out=gmax2[:], in_=m11[:].broadcast_to([1, 8])))
        wait("gpsimd", "CV", bc2)
        cci2 = dma("gpsimd", cc2_in.ap(), gmax2[:], s="LGC")
        wait("gpsimd", "LGC", cci2)
        cc2n = op("gpsimd", "CG", lambda e: e.collective_compute(
            "AllReduce", ALU.max, replica_groups=[list(range(NCORES))],
            ins=[cc2_in.ap().opt()], outs=[cc2_out.ap().opt()]))
        wait("gpsimd", "CG", cc2n)
        cco2 = dma("gpsimd", gmax2[:], cc2_out.ap(), s="LGC")
        wait("vector", "LGC", cco2)
        opw("vector", lambda e: e.tensor_tensor(out=sc_s[:, 4:5], in0=sc_s[:, 1:2],
                                                in1=gmax2[:, 0:1], op=ALU.mult))
        opw("vector", lambda e: e.tensor_scalar(out=sc_s[:, 4:5], in0=sc_s[:, 4:5],
                                                scalar1=1.0 / QMAX, scalar2=None, op0=ALU.mult))
        shv = op("vector", "CV", lambda e: e.reciprocal(out=sc_s[:, 5:6], in_=sc_s[:, 4:5]))
        wait("tensor", "CV", shv)
        bb2 = op("tensor", "CT", lambda e: e.matmul(pE[:, 6:8], lhsT=ones1f[:, :],
                                                    rhs=sc_s[:, 4:6], start=True, stop=True))
        wait("vector", "CT", bb2)
        opq("vector", lambda e: e.tensor_copy(out=sh_b[:], in_=pE[:, 6:7]))
        shb = op("vector", "CV", lambda e: e.tensor_copy(out=invsh_b[:], in_=pE[:, 7:8]))

        d2_cs_last = cnt["CS"]
        w2_drive(8)
        wait("tensor", "LTD", w2_state["wd_tr"])
        trd_free = [0, 0]
        mm_of = [0, 0]
        out_free = [0, 0]
        srd_free = [0, 0]
        ov_par = [0, 0]
        gn_prev = [d2_cs_last] * 4
        for s in range(nslabs):
            sl = slabs[s]
            ntt = len(tiles_of[s])
            mmd = 0
            for ch in range(NIC):
                tb = ch % 2
                wait("sync", "CV", trd_free[tb])
                wait("sync", f"LGT{ch % 2}", tp_sp_of[ch])
                ldt = dma("sync", trd[tb][:, 0:sl], t_sp[ch, :, s * SLAB:s * SLAB + sl], s=f"LXT{tb}")
                wait("scalar", f"LXT{tb}", ldt)
                if s == 0 and ch == 0:
                    wait("scalar", "CV", shb)
                un = op("scalar", "CS", lambda e, tb=tb, sl=sl: e.activation(
                    out=trd[tb][:, 0:sl], in_=trd[tb][:, 0:sl], func=ACT.Copy,
                    scale=invsh_b[:]))
                wait("vector", "CS", un)
                opw("vector", lambda e, tb=tb, sl=sl: e.tensor_scalar(
                    out=h_i8[:, 0:sl], in0=trd[tb][:, 0:sl], scalar1=-128.0,
                    scalar2=127.0, op0=ALU.max, op1=ALU.min))
                wait("vector", "CT", mm_of[tb])
                hq = op("vector", "CV", lambda e, tb=tb, sl=sl: e.tensor_copy(
                    out=h_cb[tb][:, 0:sl], in_=h_i8[:, 0:sl]))
                trd_free[tb] = hq
                wait("tensor", "CV", hq)
                nmm = 2 * ntt
                imm = 0
                for r in range(ntt):
                    qn = qts[s * 4 + r]
                    if ch == 0:
                        wait("tensor", "CS", gn_prev[r])
                    for half in range(2):
                        if ntt >= 3 and r == 2:
                            pdst = (pD if half == 0 else pE)[0:qn, 0:512]
                        else:
                            pt = (pC, pA, None, pB)[r]
                            pdst = pt[0:qn, half * 512:half * 512 + 512]
                        imm += 1
                        def mkd(e, ch=ch, r=r, qn=qn, tb=tb, pdst=pdst, half=half):
                            return e.matmul(
                                pdst,
                                lhsT=h_cb[tb][:, r * QT: r * QT + qn],
                                rhs=rhs_d[:, ch * C + half * 512:ch * C + half * 512 + 512],
                                start=(ch == 0), stop=(ch == NIC - 1))
                        if imm == nmm:
                            mmd = op("tensor", "CT", mkd)
                        else:
                            opq("tensor", mkd)
                mm_of[tb] = mmd if mmd else mm_of[tb]
            for r in range(ntt):
                jj = s * 4 + r
                qn = qts[jj]
                wait("scalar", "CT", mmd)
                wait("scalar", "CV", ov_par[jj % 2])
                gb = g_t[jj % 2]
                if ntt >= 3 and r == 2:
                    opq("scalar", lambda e, qn=qn, gb=gb: e.activation(
                        out=gb[0:qn, 0:512], in_=pD[0:qn, :], func=ACT.Silu,
                        scale=sh_b[0:qn, :]))
                    gn = op("scalar", "CS", lambda e, qn=qn, gb=gb: e.activation(
                        out=gb[0:qn, 512:1024], in_=pE[0:qn, :], func=ACT.Silu,
                        scale=sh_b[0:qn, :]))
                else:
                    pdn = (pC, pA, None, pB)[r]
                    gn = op("scalar", "CS", lambda e, qn=qn, pdn=pdn, gb=gb: e.activation(
                        out=gb[0:qn, :], in_=pdn[0:qn, :], func=ACT.Silu,
                        scale=sh_b[0:qn, :]))
                gn_prev[r] = gn
                rb = r % 2
                wait("scalar", "CV", srd_free[rb])
                dma("scalar", s_rd[rb][0:qn, :], s_sp[jj * QT:jj * QT + qn, :], s=f"LXF{rb}")
                ldx2 = dma("scalar", x_rd[rb][0:qn, :], xh[2 + jj * QT:2 + jj * QT + qn, :], s=f"LXF{rb}")
                wait("vector", "CS", gn)
                wait("vector", f"ST{rb}", out_free[rb])
                wait("vector", f"LXF{rb}", ldx2)
                opw("vector", lambda e, rb=rb, qn=qn, gb=gb: e.tensor_tensor(
                    out=out_t[rb][0:qn, :], in0=s_rd[rb][0:qn, :], in1=gb[0:qn, :],
                    op=ALU.mult))
                ov = op("vector", "CV", lambda e, rb=rb, qn=qn: e.tensor_tensor(
                    out=out_t[rb][0:qn, :], in0=out_t[rb][0:qn, :], in1=x_rd[rb][0:qn, :],
                    op=ALU.add))
                srd_free[rb] = ov
                ov_par[jj % 2] = ov
                wait("gpsimd", "CV", ov)
                out_free[rb] = dma("gpsimd", out[jj * QT:jj * QT + qn, :],
                                   out_t[rb][0:qn, :], s=f"ST{rb}")

        wait("sync", "ST0", cnt.get("ST0", 0))
        wait("sync", "ST1", cnt.get("ST1", 0))

        E(nc.allow_low_precision("bf16 softmax weights / quantized operands are sanctioned for this fp32 reference (bf16 matmul + fp32 accum)"))
        block = E(nc.Block())

        @block.sync
        def _(e):
            for f in prog["sync"]:
                f(e)

        @block.vector
        def _(e):
            for f in prog["vector"]:
                f(e)

        @block.scalar
        def _(e):
            for f in prog["scalar"]:
                f(e)

        @block.tensor
        def _(e):
            for f in prog["tensor"]:
                f(e)

        @block.gpsimd
        def _(e):
            for f in prog["gpsimd"]:
                f(e)

    return nc


def make_masks():
    k = np.arange(128)[:, None]
    q = np.arange(QT)[None, :]
    d = k - q - 2
    even = (q % 2) == 0
    m1 = np.where(even, (d >= 0) & (d <= 3), (d >= -1) & (d <= 2))
    m2 = np.where(even, (d >= -2) & (d <= 1), (d >= -3) & (d <= 0))

    def rep(m):
        a = np.full((128, 1024), NEG, np.float32)
        for h in range(HEADS):
            a[:, h * 128:h * 128 + QT] = np.where(m, 0.0, NEG)
        return a
    return rep(m1), rep(m2)


def kernel(L0_attn_x, L0_attn_w_qkv, L0_attn_w_out, L0_attn_b_out,
           L0_ffn_clip_x, L0_ffn_clip_down, L0_ffn_w_gate, L0_ffn_w_up,
           L0_ffn_w_down):
    x = np.ascontiguousarray(np.asarray(L0_attn_x, dtype=np.float32)[0])
    L = x.shape[0]
    T = L // NCORES
    nfull, rem = T // QT, T % QT
    ntiles = nfull + (1 if rem else 0)
    nxt = (T + 4 + 127) // 128
    XROWS = nxt * 128

    wqkv = np.ascontiguousarray(np.asarray(L0_attn_w_qkv, np.float32))
    woutw = np.ascontiguousarray(np.asarray(L0_attn_w_out, np.float32))
    boutv = np.ascontiguousarray(np.asarray(L0_attn_b_out, np.float32)).reshape(1, C)
    wgv = np.ascontiguousarray(np.asarray(L0_ffn_w_gate, np.float32))
    wuv = np.ascontiguousarray(np.asarray(L0_ffn_w_up, np.float32))
    wdv = np.ascontiguousarray(np.asarray(L0_ffn_w_down, np.float32))
    clip2 = np.zeros((1, 8), np.float32)
    clip2[0, 0] = np.asarray(L0_ffn_clip_x, np.float32).reshape(-1)[0]
    clip2[0, 1] = np.asarray(L0_ffn_clip_down, np.float32).reshape(-1)[0]

    mask1, mask2 = make_masks()
    lastq = rem if rem else QT
    mask2_first = mask2.copy()
    for h in range(HEADS):
        mask2_first[:, h * 128:h * 128 + 2] = NEG
    mask1_last = mask1.copy()
    for h in range(HEADS):
        mask1_last[:, h * 128 + lastq - 2:h * 128 + lastq] = NEG

    bmul_base = np.zeros((1, ntiles * QT), np.float32)
    for j in range(ntiles):
        qn = min(QT, T - j * QT)
        bmul_base[0, j * QT:j * QT + qn] = 2.0

    in_maps = []
    for c in range(NCORES):
        t0 = c * T
        xa = np.zeros((XROWS, C), np.float32)
        lo, hi = t0 - 2, t0 + T + 2
        src_lo, src_hi = max(0, lo), min(L, hi)
        xa[src_lo - lo: src_hi - lo] = x[src_lo:src_hi]
        bm = bmul_base.copy()
        m1zv, m2zv = mask1, mask2
        if c == 0:
            bm[0, 0:2] = 1.0
            m2zv = mask2_first
        if c == NCORES - 1:
            bm[0, (ntiles - 1) * QT + lastq - 2:(ntiles - 1) * QT + lastq] = 1.0
            m1zv = mask1_last
        in_maps.append(dict(
            xh=xa, wqkv=wqkv, wout=woutw, bout=boutv, clip2=clip2,
            wg=wgv, wu=wuv, wd=wdv,
            mask1a=mask1, mask2a=mask2, mask1z=m1zv, mask2z=m2zv, bmul=bm))

    nc = build_nc(T)
    res = run_bass_kernel_spmd(nc, in_maps, core_ids=list(range(NCORES)))
    if DEBUG_SPILLS:
        DEBUG_RES.clear()
        DEBUG_RES.append(res)
    outs = [np.asarray(res.results[c]["out"], np.float32) for c in range(NCORES)]
    return np.concatenate(outs, axis=0)[None, :, :]


# version 16
# speedup vs baseline: 1.0247x; 1.0247x over previous
import functools
import operator
import numpy as np
import ml_dtypes
from contextlib import ExitStack

import concourse.bass as bass
import concourse.mybir as mybir
from concourse.bass_utils import run_bass_kernel_spmd

F32 = mybir.dt.float32
BF16 = mybir.dt.bfloat16
I8 = mybir.dt.int8
U8 = mybir.dt.uint8
ALU = mybir.AluOpType
ACT = mybir.ActivationFunctionType
AX = mybir.AxisListType

NCORES = 8
C = 1024
I3 = 3072
II = 2816
NIC = II // 128
HEADS = 8
QMAX = 127.0
EPS_LN = 1e-5
NEG = -30000.0
ZEPS = 1e-30
QT = 124
DEBUG_SPILLS = False
DEBUG_RES = []


def build_nc(T):
    nfull, rem = T // QT, T % QT
    ntiles = nfull + (1 if rem else 0)
    qts = [QT] * nfull + ([rem] if rem else [])
    SLAB = 4 * QT
    nslab_full, srem = T // SLAB, T % SLAB
    slabs = [SLAB] * nslab_full + ([srem] if srem else [])
    nslabs = len(slabs)
    sw = [slabs[s] + 4 for s in range(nslabs)]
    last_s = nslabs - 1
    sw[last_s] = max(sw[last_s], (ntiles - 1) * QT - SLAB * last_s + 128)
    SLOTW = max(sw)
    nxt = (T + 4 + 127) // 128
    XROWS = nxt * 128
    FRW = XROWS
    NTT = T // 128
    assert T % 128 == 0
    tiles_of = [[] for _ in range(nslabs)]
    for j in range(ntiles):
        tiles_of[min(j // 4, nslabs - 1)].append(j)

    nc = bass.Bass("TRN2", target_bir_lowering=False, detect_race_conditions=False)

    ap = lambda *a, **k: nc.dram_tensor(*a, **k).ap()
    xh = ap("xh", [XROWS, C], F32, kind="ExternalInput")
    wqkv = ap("wqkv", [C, I3], F32, kind="ExternalInput")
    wout = ap("wout", [C, C], F32, kind="ExternalInput")
    bout = ap("bout", [1, C], F32, kind="ExternalInput")
    clip2 = ap("clip2", [1, 8], F32, kind="ExternalInput")
    wg = ap("wg", [II, C], F32, kind="ExternalInput")
    wu = ap("wu", [II, C], F32, kind="ExternalInput")
    wd = ap("wd", [C, II], F32, kind="ExternalInput")
    mask1a_d = ap("mask1a", [128, 1024], F32, kind="ExternalInput")
    mask2a_d = ap("mask2a", [128, 1024], F32, kind="ExternalInput")
    mask1z_d = ap("mask1z", [128, 1024], F32, kind="ExternalInput")
    mask2z_d = ap("mask2z", [128, 1024], F32, kind="ExternalInput")
    bmul_d = ap("bmul", [1, ntiles * QT], F32, kind="ExternalInput")
    out = ap("out", [T, C], F32, kind="ExternalOutput")

    spk = dict(kind="ExternalOutput") if DEBUG_SPILLS else {}
    xn_sp = ap("xn_sp", [XROWS, C], F32, **spk)
    s_sp = ap("s_sp", [ntiles * QT, C], BF16, **spk)
    t_sp = ap("t_sp", [NIC, 128, T], F32, **spk)
    lg_sp = ap("lg_sp", [NIC, 128, 8 * 128], BF16)
    lu_sp = ap("lu_sp", [NIC, 128, 8 * 128], BF16)
    mrow_sp = ap("mrow_sp", [1, 128], F32)
    cc1_in = nc.dram_tensor("cc1_in", [1, 8], F32)
    cc1_out = nc.dram_tensor("cc1_out", [1, 8], F32)
    cc2_in = nc.dram_tensor("cc2_in", [1, 8], F32)
    cc2_out = nc.dram_tensor("cc2_out", [1, 8], F32)

    prog = {e: [] for e in ("sync", "vector", "scalar", "tensor", "gpsimd")}
    cnt = {}

    def inc(s, n=1):
        cnt[s] = cnt.get(s, 0) + n
        return cnt[s]

    with ExitStack() as stk:
        E = stk.enter_context

        R1B, R2B, R3B, R4B = 83968, 48000, 47104, 22528
        arena = E(nc.sbuf_tensor("arena", [128, R1B + R2B + R3B + R4B], U8))
        base = nc.lookup_mloc(arena).addr
        OFF = {"R1": 0, "R2": R1B, "R3": R1B + R2B, "R4": R1B + R2B + R3B}
        SZ = {"R1": R1B, "R2": R2B, "R3": R3B, "R4": R4B}
        _n = [0]

        def ovl(region, off, shape, dtype):
            _n[0] += 1
            name = f"ov{_n[0]}"
            hnd = nc.virtual_sbuf_tensor(name, shape, dtype)
            nc._tensor(hnd.name, hnd.shape, hnd.dtype, type="SB")
            mloc = nc.lookup_mloc(hnd)
            mloc.allocated = True
            mloc.addr = base + OFF[region] + off
            nb = functools.reduce(operator.mul, shape[1:], 1) * mybir.dt.size(dtype)
            assert off % 32 == 0 and off + nb <= SZ[region], (region, off, nb)
            return hnd

        wqkv_b = ovl("R1", 0, [128, 8 * I3], BF16)
        xn_fm = ovl("R1", 49152, [128, 8 * FRW], BF16)
        wstg_gu = [ovl("R4", 8192 + i * 4096, [128, C], F32) for i in range(2)]
        wnb_gu = [ovl("R4", 16384 + i * 2048, [128, C], BF16) for i in range(2)]
        wi8_gu = [ovl("R4", 20480 + i * 1024, [128, C], I8) for i in range(2)]
        lhs_stg = ovl("R3", 20480, [128, 8 * 128], BF16)
        xq_fm = ovl("R1", 0, [128, 8 * T], BF16)
        lgs = [ovl("R1", 32768 + i * 2048, [128, 8 * 128], BF16) for i in range(2)]
        lus = [ovl("R1", 36864 + i * 2048, [128, 8 * 128], BF16) for i in range(2)]
        tpre = [ovl("R1", 40960 + i * (T * 4), [128, T], F32) for i in range(2)]
        qk_r = [ovl("R2", i * 16 * SLOTW * 2, [128, 16 * SLOTW], BF16) for i in range(2)]
        vf_r = [ovl("R2", 64 * SLOTW + i * 16 * SLOTW, [128, 8 * SLOTW], BF16)
                for i in range(2)]
        wstg_d = ovl("R2", 0, [128, II], F32)
        wi8_d = ovl("R2", 11264, [128, II], I8)
        wnb_d = ovl("R2", 14112, [128, II], BF16)
        h_cb = [ovl("R2", 21856 + i * 1024, [128, SLAB], BF16) for i in range(2)]
        trd = [ovl("R2", 43712 + i * 2016, [128, SLAB], F32) for i in range(2)]
        wout_b = ovl("R3", 0, [128, 8 * C], BF16)
        e1 = ovl("R3", 16384, [128, 1024], BF16)
        e2 = ovl("R3", 18432, [128, 1024], BF16)
        zb1 = ovl("R3", 20480, [128, 1024], BF16)
        zb2 = ovl("R3", 22528, [128, 1024], BF16)
        r1 = ovl("R3", 24576, [128, 1024], BF16)
        r2 = ovl("R3", 26624, [128, 1024], BF16)
        ww = ovl("R3", 28672, [128, 1024], BF16)
        o_sb = ovl("R3", 30720, [128, 1024], BF16)
        s_t = ovl("R3", 32768, [128, 1024], BF16)
        v_tm = [ovl("R3", 34816 + i * 2048, [128, 1024], BF16) for i in range(2)]
        m1a = ovl("R3", 38912, [128, 1024], BF16)
        m2a = ovl("R3", 40960, [128, 1024], BF16)
        m1z = ovl("R3", 43008, [128, 1024], BF16)
        m2z = ovl("R3", 45056, [128, 1024], BF16)
        rhs_d = ovl("R3", 0, [128, NIC * C], BF16)
        x_t = [ovl("R4", i * 4096, [128, C], F32) for i in range(2)]
        xn_t = [ovl("R4", 8192 + i * 4096, [128, C], F32) for i in range(2)]
        xnb_t = [ovl("R4", 16384 + i * 2048, [128, C], BF16) for i in range(2)]
        a1 = ovl("R4", 0, [128, 1024], F32)
        a2 = ovl("R4", 4096, [128, 1024], F32)
        xq_t = [ovl("R4", i * 2048, [128, C], BF16) for i in range(2)]
        xq_i8 = ovl("R4", 4096, [128, C], I8)
        uq = ovl("R4", 5120, [128, C], F32)
        silu_g = [ovl("R4", 5120 + i * 2048, [128, 512], F32) for i in range(2)]
        xrd_d = [ovl("R4", 9216 + i * 4096, [128, C], F32) for i in range(2)]
        out_t = [ovl("R4", i * 4096, [128, C], F32) for i in range(2)]
        x_rd = [ovl("R4", 8192 + i * 4096, [128, C], F32) for i in range(2)]
        g_t = [ovl("R4", 16384, [128, C], BF16), ovl("R3", 45056, [128, C], BF16)]
        s_rd = [ovl("R4", 18432 + i * 2048, [128, C], BF16) for i in range(2)]

        sb = lambda name, shape, dt: E(nc.sbuf_tensor(name, shape, dt))
        bmul_s = sb("bmul_s", [1, ntiles * QT], BF16)
        bout_b = sb("bout_b", [1, C], BF16)
        clip_s = sb("clip_s", [1, 8], F32)
        onesK = sb("onesK", [128, 128], BF16)
        ones1f = sb("ones1f", [1, 128], F32)
        epsrow = sb("epsrow", [1, 512], F32)
        ln_s = sb("ln_s", [128, 8], F32)
        amax_x = sb("amax_x", [128, 1], F32)
        amax_t = sb("amax_t", [128, 1], F32)
        red1 = sb("red1", [128, 1], F32)
        mrow = sb("mrow", [1, 128], F32)
        m11 = sb("m11", [1, 1], F32)
        mbias = sb("mbias", [128, 1], F32)
        gmax1 = sb("gmax1", [1, 8], F32)
        gmax2 = sb("gmax2", [1, 8], F32)
        sc_s = sb("sc_s", [1, 8], F32)
        sx_b = sb("sx_b", [128, 1], F32)
        invsx_b = sb("invsx_b", [128, 1], F32)
        sh_b = sb("sh_b", [128, 1], F32)
        invsh_b = sb("invsh_b", [128, 1], F32)
        winv = sb("winv", [128, 8], F32)
        h_i8 = sb("h_i8", [128, SLAB], I8)
        epsb = sb("epsb", [128, 1], F32)
        cneg_b = sb("cneg_b", [128, 1], F32)

        pA = E(nc.psum_tensor("pA", [128, 1024], F32))
        pB = E(nc.psum_tensor("pB", [128, 1024], F32))
        pC = E(nc.psum_tensor("pC", [128, 1024], F32))
        pD = E(nc.psum_tensor("pD", [128, 512], F32))
        pE = E(nc.psum_tensor("pE", [128, 512], F32))

        sem = {}

        def S(n):
            if n not in sem:
                sem[n] = E(nc.semaphore(n))
            return sem[n]

        for _nm in ("CV", "CS", "CT", "CG"):
            S(_nm)

        def emit(eng, f):
            prog[eng].append(f)

        def dma(eng, out_ap, in_ap, s, transpose=False):
            S(s)
            n = inc(s, 16)
            def f(e, out_ap=out_ap, in_ap=in_ap, s=s, transpose=transpose):
                if transpose:
                    e.dma_start(out=out_ap, in_=in_ap, transpose=True).then_inc(sem[s], 16)
                else:
                    e.dma_start(out=out_ap, in_=in_ap).then_inc(sem[s], 16)
            emit(eng, f)
            return n

        def wait(eng, s, val):
            S(s)
            if val > 0:
                emit(eng, lambda e, s=s, val=val: e.wait_ge(sem[s], val))

        ENG_SEM = {"vector": "CV", "scalar": "CS", "tensor": "CT", "gpsimd": "CG"}

        def op(eng, s, fmaker):
            assert s == ENG_SEM[eng]
            if eng in ("vector", "scalar"):
                prev = cnt.get(s, 0)
                if prev > 0:
                    emit(eng, lambda e, s=s, prev=prev: e.wait_ge(sem[s], prev))
            n = inc(s)
            emit(eng, lambda e, fmaker=fmaker, s=s: fmaker(e).then_inc(sem[s], 1))
            return n

        def opq(eng, fmaker):
            if eng in ("vector", "scalar"):
                op(eng, ENG_SEM[eng], fmaker)
            else:
                emit(eng, lambda e, fmaker=fmaker: fmaker(e))

        def opw(eng, fmaker):
            return op(eng, ENG_SEM[eng], fmaker)

        st3 = lambda t, qn: t[:, :].rearrange("p (h q) -> p h q", h=8)[:, :, 0:qn]

        dma("sync", clip_s[:], clip2, s="LC")
        lcg_of = [0] * 24
        wqkv_b3 = wqkv_b[:, :].rearrange("p (c n) -> p c n", c=8)
        wqkv_s3 = wqkv.rearrange("(c p) n -> p c n", p=128)

        def ld_wqkv_chunk(jj):
            lcg_of[jj] = dma("gpsimd", wqkv_b3[:, :, jj * 128:(jj + 1) * 128],
                             wqkv_s3[:, :, jj * 128:(jj + 1) * 128], s="LCG")

        for jj in range(6):
            ld_wqkv_chunk(jj)
        for (dst, srcd) in ((m1a, mask1a_d), (m2a, mask2a_d), (m1z, mask1z_d),
                            (m2z, mask2z_d)):
            dma("gpsimd", dst[:], srcd, s="LCG")
        ld_masks = cnt["LCG"]
        dma("gpsimd", bmul_s[:], bmul_d, s="LCG")
        dma("gpsimd", bout_b[:], bout, s="LCG")
        for jj in range(6, 10):
            ld_wqkv_chunk(jj)
        for ccn in range(8):
            dma("gpsimd", wout_b[:, ccn * C:(ccn + 1) * C],
                wout[ccn * 128:(ccn + 1) * 128, :], s="LCG")
        ld_wout = cnt["LCG"]
        for jj in range(10, 24):
            ld_wqkv_chunk(jj)
        ld_wq = cnt["LCG"]
        ld_const = cnt["LC"]

        wait("vector", "LC", ld_const)
        opq("vector", lambda e: e.memset(onesK[:], 1.0))
        opq("vector", lambda e: e.memset(ones1f[:], 1.0))
        opq("vector", lambda e: e.memset(epsrow[:], ZEPS))
        opq("vector", lambda e: e.memset(amax_x[:], 0.0))
        opq("vector", lambda e: e.memset(amax_t[:], 0.0))
        opq("vector", lambda e: e.memset(epsb[:], EPS_LN))
        opq("vector", lambda e: e.memset(cneg_b[:], -128.0))
        opq("vector", lambda e: e.memset(pA[:, :], 0.0))
        pa_init = op("vector", "CV", lambda e: e.memset(pB[:, :], 0.0))

        tr_done = [0] * nxt
        sp_done = [0] * nxt
        xa_done = [0] * nxt
        for i in range(nxt):
            b = i % 2
            if i >= 2:
                wait("scalar", "CV", xa_done[i - 2])
            ldx = dma("scalar", x_t[b][:], xh[i * 128:(i + 1) * 128, :], s=f"LXX{b}")
            wait("vector", f"LXX{b}", ldx)
            if i >= 2:
                wait("vector", f"LGX{b}", sp_done[i - 2])
            opw("vector", lambda e, b=b: e.reduce_sum(out=ln_s[:, 0:1], in_=x_t[b][:], axis=AX.X))
            wait("scalar", f"LXX{b}", ldx)
            if i >= 2:
                wait("scalar", f"LTA{b}", tr_done[i - 2])
            sq = op("scalar", "CS", lambda e, b=b: e.activation(
                out=xnb_t[b][:], in_=x_t[b][:], func=ACT.Square, accum_out=ln_s[:, 1:2]))
            wait("vector", "CS", sq)
            opw("vector", lambda e: e.tensor_scalar(out=ln_s[:, 2:3], in0=ln_s[:, 0:1],
                                                    scalar1=1.0 / C, scalar2=None, op0=ALU.mult))
            opq("vector", lambda e: e.tensor_scalar(out=ln_s[:, 3:4], in0=ln_s[:, 1:2],
                                                    scalar1=1.0 / C, scalar2=None, op0=ALU.mult))
            opw("vector", lambda e: e.tensor_tensor(out=ln_s[:, 4:5], in0=ln_s[:, 2:3],
                                                    in1=ln_s[:, 2:3], op=ALU.mult))
            vv = op("vector", "CV", lambda e: e.tensor_tensor(out=ln_s[:, 4:5], in0=ln_s[:, 3:4],
                                                              in1=ln_s[:, 4:5], op=ALU.subtract))
            wait("scalar", "CV", vv)
            sq2 = op("scalar", "CS", lambda e: e.activation(out=ln_s[:, 5:6], in_=ln_s[:, 4:5],
                                                            func=ACT.Sqrt, bias=epsb[:]))
            wait("vector", "CS", sq2)
            opw("vector", lambda e: e.reciprocal(out=ln_s[:, 6:7], in_=ln_s[:, 5:6]))
            opw("vector", lambda e, b=b: e.scalar_tensor_tensor(
                out=xn_t[b][:], in0=x_t[b][:], scalar=ln_s[:, 2:3],
                in1=ln_s[:, 6:7].broadcast_to([128, C]),
                op0=ALU.subtract, op1=ALU.mult))
            opw("vector", lambda e, b=b: e.reduce_max(out=red1[:], in_=xn_t[b][:], axis=AX.X,
                                                      apply_absolute_value=True))
            opq("vector", lambda e: e.tensor_tensor(out=amax_x[:], in0=amax_x[:],
                                                    in1=red1[:], op=ALU.max))
            xa_done[i] = op("vector", "CV",
                            lambda e, b=b: e.tensor_copy(out=xnb_t[b][:], in_=xn_t[b][:]))
            wait("gpsimd", "CV", xa_done[i])
            sp_done[i] = dma("gpsimd", xn_sp[i * 128:(i + 1) * 128, :], xn_t[b][:], s=f"LGX{b}")
            wait("sync", "CV", xa_done[i])
            tr_done[i] = dma("sync",
                             xn_fm[:, :].rearrange("p (c f) -> p c f", c=8)[:, :, i * 128:(i + 1) * 128],
                             xnb_t[b][:, :].rearrange("p (c q) -> p c q", c=8),
                             s=f"LTA{b}", transpose=True)
        opw("vector", lambda e: e.tensor_copy(out=red1[0:1, 0:1], in_=red1[0:1, 0:1]))
        amx = op("vector", "CV", lambda e: e.tensor_copy(out=red1[:], in_=amax_x[:]))
        wait("gpsimd", "CV", amx)
        dma("gpsimd", mrow_sp, red1[:, 0:1], s="LGM")
        wait("gpsimd", "LGM", cnt["LGM"])
        g1 = dma("gpsimd", mrow[:], mrow_sp, s="LGM")
        wait("vector", "LGM", g1)
        opq("vector", lambda e: e.reduce_max(out=m11[:], in_=mrow[:], axis=AX.X))
        bc1 = op("vector", "CV",
                 lambda e: e.tensor_copy(out=gmax1[:], in_=m11[:].broadcast_to([1, 8])))
        wait("gpsimd", "CV", bc1)
        cci = dma("gpsimd", cc1_in.ap(), gmax1[:], s="LGC")
        wait("gpsimd", "LGC", cci)
        cc1n = op("gpsimd", "CG", lambda e: e.collective_compute(
            "AllReduce", ALU.max, replica_groups=[list(range(NCORES))],
            ins=[cc1_in.ap().opt()], outs=[cc1_out.ap().opt()]))
        wait("gpsimd", "CG", cc1n)
        cco1 = dma("gpsimd", gmax1[:], cc1_out.ap(), s="LGC")

        w1_state = {"stg_free": 0, "i8_free": 0, "lg_done": 0, "emitted": 0,
                    "trn_b": [0, 0], "ld_of": [0, 0], "stt_of": [0, 0],
                    "sc_of": [0, 0]}
        NW1 = 2 * NIC

        def w1_src(idx):
            src_ap = wg if idx < NIC else wu
            ch = idx % NIC
            return src_ap[ch * 128:(ch + 1) * 128, :]

        def emit_w1_chunk(idx):
            dst_sp = lg_sp if idx < NIC else lu_sp
            ch = idx % NIC
            b = idx % 2
            o4 = b * 4
            if idx == 0:
                w1_state["ld_of"][0] = dma("sync", wstg_gu[0][:], w1_src(0), s="LXW0")
            if idx + 1 < NW1:
                nb = (idx + 1) % 2
                wait("sync", "CV", w1_state["stt_of"][nb])
                w1_state["ld_of"][nb] = dma("sync", wstg_gu[nb][:], w1_src(idx + 1), s="LXW0")
            wait("vector", "LXW0", w1_state["ld_of"][b])
            wait("vector", "CS", w1_state["sc_of"][b])
            opw("vector", lambda e, o4=o4, b=b: e.reduce_max(
                out=winv[:, o4:o4 + 1], in_=wstg_gu[b][:], axis=AX.X,
                apply_absolute_value=True))
            opq("vector", lambda e, o4=o4: e.tensor_scalar(
                out=winv[:, o4 + 1:o4 + 2], in0=winv[:, o4:o4 + 1],
                scalar1=1.0 / QMAX, scalar2=None, op0=ALU.mult))
            opw("vector", lambda e, o4=o4: e.reciprocal(
                out=winv[:, o4 + 2:o4 + 3], in_=winv[:, o4:o4 + 1]))
            opw("vector", lambda e, o4=o4: e.tensor_scalar(
                out=winv[:, o4 + 2:o4 + 3], in0=winv[:, o4 + 2:o4 + 3],
                scalar1=QMAX * (1.0 - 2.0 ** -22), scalar2=None, op0=ALU.mult))
            i8n = op("vector", "CV", lambda e, o4=o4, b=b: e.scalar_tensor_tensor(
                out=wi8_gu[b][:], in0=wstg_gu[b][:], scalar=winv[:, o4 + 2:o4 + 3],
                in1=wstg_gu[b][:], op0=ALU.mult, op1=ALU.bypass))
            w1_state["i8_free"] = i8n
            w1_state["stt_of"][b] = i8n
            wait("scalar", "CV", i8n)
            wait("scalar", "LTW", w1_state["trn_b"][b])
            sc = op("scalar", "CS", lambda e, b=b, o4=o4: e.activation(
                out=wnb_gu[b][:], in_=wi8_gu[b][:], func=ACT.Copy,
                scale=winv[:, o4 + 1:o4 + 2]))
            wait("sync", "CS", sc)
            wait("sync", "LGL", w1_state["lg_done"])
            trn = dma("sync", lhs_stg[:, :].rearrange("p (c q) -> p c q", c=8),
                      wnb_gu[b][:, :].rearrange("p (c q) -> p c q", c=8),
                      s="LTW", transpose=True)
            w1_state["trn_b"][b] = trn
            w1_state["sc_of"][b] = sc
            wait("sync", "LTW", trn)
            w1_state["lg_done"] = dma("sync", dst_sp[ch, :, :], lhs_stg[:, :], s="LGL")

        def w1_drive(frac):
            want = min(NW1, int(NW1 * frac + 0.999))
            while w1_state["emitted"] < want:
                emit_w1_chunk(w1_state["emitted"])
                w1_state["emitted"] += 1

        wait("tensor", "CV", pa_init)
        wait("sync", "LGX0", sp_done[(nxt - 2) - (nxt % 2 == 1)] if nxt >= 2 else 0)
        wait("sync", "LGX0", max(sp_done[i] for i in range(0, nxt, 2)))
        wait("sync", "LGX1", max(sp_done[i] for i in range(1, nxt, 2)))
        wait("sync", "LTA0", max(tr_done[i] for i in range(0, nxt, 2)))
        wait("sync", "LTA1", max(tr_done[i] for i in range(1, nxt, 2)))
        qkv_done = [0] * nslabs
        slab_av = [0] * nslabs
        oev_prev = 0
        sev_prev = 0
        ssp_prev = 0
        vtm_av = [0] * ntiles
        vt_of = [0] * ntiles
        gather_of = [0] * ntiles
        aan_of = [0] * (ntiles + 1)
        ph1_done = [False] * ntiles
        mred_prev = [0]

        qkv_ev = [0, 0]
        qkv_kk = [0]
        qkv_em = [0] * nslabs

        def emit_qkv_groups(s, ngroups):
            if ngroups <= 0 or qkv_em[s] >= 24:
                return
            wdt = sw[s]
            basec = s * SLAB
            rslot = s % 2
            if qkv_em[s] == 0:
                need_tile = min(nxt - 1, (basec + wdt - 1) // 128)
                wait("tensor", f"LTA{need_tile % 2}", tr_done[need_tile])
            wait("tensor", "CS", sev_prev)
            while ngroups > 0 and qkv_em[s] < 24:
                j = qkv_em[s]
                kk = qkv_kk[0]
                pq = pD if kk % 2 == 0 else pE
                wait("tensor", "LCG", lcg_of[j])
                wait("tensor", "CV", qkv_ev[kk % 2])
                mmn = 0
                for ccn in range(8):
                    def mk(e, j=j, ccn=ccn, basec=basec, wdt=wdt, pq=pq):
                        return e.matmul(
                            pq[:, 0:wdt],
                            lhsT=wqkv_b[:, ccn * I3 + j * 128:ccn * I3 + (j + 1) * 128],
                            rhs=xn_fm[:, ccn * FRW + basec: ccn * FRW + basec + wdt],
                            start=(ccn == 0), stop=(ccn == 7))
                    if ccn == 7:
                        mmn = op("tensor", "CT", mk)
                    else:
                        opq("tensor", mk)
                wait("vector", "CT", mmn)
                if j == 0 and s >= 2:
                    wait("vector", "CT", slab_av[s - 2])
                if j < 8:
                    def ev(e, j=j, rslot=rslot, wdt=wdt, pq=pq):
                        return e.tensor_scalar(
                            out=qk_r[rslot][:, j * SLOTW:j * SLOTW + wdt],
                            in0=pq[:, 0:wdt], scalar1=1.0 / 32.0, scalar2=None,
                            op0=ALU.mult)
                elif j < 16:
                    def ev(e, j=j, rslot=rslot, wdt=wdt, pq=pq):
                        return e.tensor_copy(
                            out=qk_r[rslot][:, j * SLOTW:j * SLOTW + wdt],
                            in_=pq[:, 0:wdt])
                else:
                    def ev(e, j=j, rslot=rslot, wdt=wdt, pq=pq):
                        return e.tensor_copy(
                            out=vf_r[rslot][:, (j - 16) * SLOTW:(j - 16) * SLOTW + wdt],
                            in_=pq[:, 0:wdt])
                qkv_ev[kk % 2] = op("vector", "CV", ev)
                qkv_kk[0] += 1
                qkv_em[s] += 1
                ngroups -= 1
            if qkv_em[s] == 24 and qkv_done[s] == 0:
                qkv_done[s] = cnt["CV"]

        def emit_phase1(j, s, rslot, qn, kbase):
            stX = pA if j % 2 == 0 else pB
            wait("tensor", "CV", qkv_done[s])
            stn = 0
            for h in range(HEADS):
                def mk(e, h=h, rslot=rslot, kbase=kbase, qn=qn, stX=stX):
                    return e.matmul(
                        stX[:, h * 128:h * 128 + qn],
                        lhsT=qk_r[rslot][:, (8 + h) * SLOTW + kbase:(8 + h) * SLOTW + kbase + 128],
                        rhs=qk_r[rslot][:, h * SLOTW + kbase + 2: h * SLOTW + kbase + 2 + qn],
                        start=True, stop=True)
                if h == HEADS - 1:
                    stn = op("tensor", "CT", mk)
                else:
                    opq("tensor", mk)
            wait("vector", "CT", stn)
            if j >= 1:
                wait("vector", "LGM", gather_of[j - 1])
            mred = op("vector", "CV", lambda e, qn=qn, stX=stX: e.reduce_max(
                out=red1[:], in_=st3(stX, qn), axis=AX.XY, negate=True))
            wait("gpsimd", "CV", mred)
            dma("gpsimd", mrow_sp, red1[:, 0:1], s="LGM")
            wait("gpsimd", "LGM", cnt["LGM"])
            gather_of[j] = dma("gpsimd", mrow[:], mrow_sp, s="LGM")

        emit_qkv_groups(0, 24)
        if nslabs > 1:
            emit_qkv_groups(1, 10)
        for s in range(nslabs):
            rslot = s % 2
            for j in tiles_of[s]:
                qn = qts[j]
                kbase = QT * j - s * SLAB
                vb = j % 2
                if j >= 2:
                    wait("sync", "CT", vtm_av[j - 2])
                wait("sync", "CV", qkv_done[s])
                vt = 0
                for ccn in range(8):
                    vt = dma("sync", v_tm[vb][:, ccn * 128:(ccn + 1) * 128],
                             vf_r[rslot][:, ccn * SLOTW + kbase: ccn * SLOTW + kbase + 128],
                             s=f"LTV{vb}", transpose=True)
                vt_of[j] = vt
                if not ph1_done[j]:
                    emit_phase1(j, s, rslot, qn, kbase)
                stX = pA if j % 2 == 0 else pB
                g_j = gather_of[j]
                wait("vector", "LGM", g_j)
                mv = op("vector", "CV", lambda e: e.tensor_reduce(
                    out=m11[:], in_=mrow[:], axis=AX.X, op=ALU.min))
                wait("tensor", "CV", mv)
                mb = op("tensor", "CT", lambda e: e.matmul(
                    pC[:, 124:125], lhsT=ones1f[:, :], rhs=m11[:], start=True, stop=True))
                wait("vector", "CT", mb)
                opq("vector", lambda e: e.tensor_copy(out=mbias[:], in_=pC[:, 124:125]))
                km1 = m1z if j == ntiles - 1 else m1a
                km2 = m2z if j == 0 else m2a
                if j == 0:
                    wait("vector", "LCG", ld_masks)
                opq("vector", lambda e, km1=km1, stX=stX: e.tensor_tensor(
                    out=a1[:], in0=stX[:, :], in1=km1[:], op=ALU.add))
                aan = op("vector", "CV", lambda e, km2=km2, stX=stX: e.tensor_tensor(
                    out=a2[:], in0=stX[:, :], in1=km2[:], op=ALU.add))
                aan_of[j] = aan
                wait("scalar", "CV", aan)
                opq("scalar", lambda e: e.activation(out=e1[:], in_=a1[:], func=ACT.Exp,
                                                     bias=mbias[:]))
                een = op("scalar", "CS", lambda e: e.activation(out=e2[:], in_=a2[:],
                                                                func=ACT.Exp, bias=mbias[:]))
                wait("tensor", "CS", een)
                edge = (j == 0) or (j == ntiles - 1)
                zn = 0
                for (eb, pz_) in ((e1, stX), (e2, pC)):
                    for half in range(2):
                        n0 = half * 512
                        if edge:
                            opq("tensor", lambda e, pz_=pz_, n0=n0: e.matmul(
                                pz_[:, n0:n0 + 512], lhsT=ones1f[:, :], rhs=epsrow[:, :],
                                start=True, stop=False))
                        def mkz(e, eb=eb, pz_=pz_, n0=n0, edge=edge):
                            return e.matmul(pz_[:, n0:n0 + 512], lhsT=onesK[:, :],
                                            rhs=eb[:, n0:n0 + 512], start=not edge, stop=True)
                        if pz_ is pC and half == 1:
                            zn = op("tensor", "CT", mkz)
                        else:
                            opq("tensor", mkz)
                jn = j + 1
                if jn < ntiles and not ph1_done[jn]:
                    s_n = min(jn // 4, nslabs - 1)
                    if s_n == s or qkv_em[s_n] >= 24:
                        emit_phase1(jn, s_n, s_n % 2, qts[jn], QT * jn - s_n * SLAB)
                        ph1_done[jn] = True
                if s + 1 < nslabs:
                    emit_qkv_groups(s + 1, min(3, min(24, 8 * (j - 4 * s + 1)) - qkv_em[s + 1]))
                wait("scalar", "CT", zn)
                opq("scalar", lambda e, stX=stX: e.activation(out=a1[:], in_=stX[:, :],
                                                              func=ACT.Ln))
                opq("scalar", lambda e: e.activation(out=a2[:], in_=pC[:, :], func=ACT.Ln))
                opq("scalar", lambda e: e.activation(out=r1[:], in_=a1[:], func=ACT.Exp,
                                                     scale=-1.0))
                rr2 = op("scalar", "CS", lambda e: e.activation(out=r2[:], in_=a2[:],
                                                                func=ACT.Exp, scale=-1.0))
                wait("vector", "CS", rr2)
                opq("vector", lambda e: e.tensor_tensor(out=ww[:], in0=e1[:], in1=r1[:], op=ALU.mult))
                opw("vector", lambda e: e.tensor_tensor(out=r2[:], in0=e2[:], in1=r2[:],
                                                        op=ALU.mult))
                wn = op("vector", "CV", lambda e: e.tensor_tensor(out=ww[:], in0=ww[:],
                                                                  in1=r2[:], op=ALU.add))
                wait("tensor", "CV", wn)
                wait("tensor", f"LTV{vb}", vt_of[j])
                wait("tensor", "CS", oev_prev)
                avn = 0
                for h in range(HEADS):
                    def mk(e, h=h, vb=vb, qn=qn):
                        return e.matmul(
                            pC[:, h * 128:h * 128 + qn],
                            lhsT=v_tm[vb][:, h * 128:(h + 1) * 128],
                            rhs=ww[:, h * 128:h * 128 + qn],
                            start=True, stop=True)
                    if h == HEADS - 1:
                        avn = op("tensor", "CT", mk)
                    else:
                        opq("tensor", mk)
                vtm_av[j] = avn
                slab_av[s] = avn
                wait("scalar", "CT", avn)
                oev = op("scalar", "CS", lambda e: e.activation(out=o_sb[:], in_=pC[:, :],
                                                                func=ACT.Copy))
                oev_prev = oev
                if j == 0:
                    wait("tensor", "LCG", ld_wout)
                wait("tensor", "CS", oev)
                wait("tensor", "CS", sev_prev)
                wait("tensor", "CV", qkv_ev[0])
                wait("tensor", "CV", qkv_ev[1])
                for half, pq_ in ((0, pD), (1, pE)):
                    n0 = half * 512
                    for ccn in range(8):
                        opq("tensor", lambda e, ccn=ccn, n0=n0, qn=qn, pq_=pq_: e.matmul(
                            pq_[0:qn, :],
                            lhsT=o_sb[:, ccn * 128:ccn * 128 + qn],
                            rhs=wout_b[:, ccn * C + n0:ccn * C + n0 + 512],
                            start=(ccn == 0), stop=False))
                    won = op("tensor", "CT", lambda e, j=j, n0=n0, qn=qn, pq_=pq_: e.matmul(
                        pq_[0:qn, :],
                        lhsT=bmul_s[:, j * QT:j * QT + qn],
                        rhs=bout_b[:, n0:n0 + 512], start=False, stop=True))
                wait("scalar", "CT", won)
                wait("scalar", "LGS", ssp_prev)
                opq("scalar", lambda e, qn=qn: e.activation(
                    out=s_t[0:qn, 0:512], in_=pD[0:qn, :], func=ACT.Copy))
                sev = op("scalar", "CS", lambda e, qn=qn: e.activation(
                    out=s_t[0:qn, 512:1024], in_=pE[0:qn, :], func=ACT.Copy))
                sev_prev = sev
                wait("gpsimd", "CS", sev)
                ssp_prev = dma("gpsimd", s_sp[j * QT:j * QT + qn, :], s_t[0:qn, :], s="LGS")
                if s + 1 < nslabs:
                    emit_qkv_groups(s + 1, min(24, 8 * (j - 4 * s + 1)) - qkv_em[s + 1])
                w1_drive((j + 1) / ntiles)

        wait("scalar", "LGC", cco1)
        sg1 = op("scalar", "CS", lambda e: e.activation(out=sc_s[:, 0:2], in_=clip_s[:, 0:2],
                                                        func=ACT.Sigmoid))
        wait("vector", "CS", sg1)
        opw("vector", lambda e: e.tensor_tensor(out=sc_s[:, 2:3], in0=sc_s[:, 0:1],
                                                in1=gmax1[:, 0:1], op=ALU.mult))
        opw("vector", lambda e: e.tensor_scalar(out=sc_s[:, 2:3], in0=sc_s[:, 2:3],
                                                scalar1=1.0 / QMAX, scalar2=None, op0=ALU.mult))
        sxv = op("vector", "CV", lambda e: e.reciprocal(out=sc_s[:, 3:4], in_=sc_s[:, 2:3]))
        wait("tensor", "CV", sxv)
        bb = op("tensor", "CT", lambda e: e.matmul(pE[:, 4:6], lhsT=ones1f[:, :],
                                                   rhs=sc_s[:, 2:4], start=True, stop=True))
        wait("vector", "CT", bb)
        opq("vector", lambda e: e.tensor_copy(out=sx_b[:], in_=pE[:, 4:5]))
        sxb = op("vector", "CV", lambda e: e.tensor_copy(out=invsx_b[:], in_=pE[:, 5:6]))

        wait("sync", "CV", w1_state["i8_free"])
        wait("sync", "LTW", cnt.get("LTW", 0))
        wait("sync", "LGL", w1_state["lg_done"])
        wait("scalar", "CV", w1_state["i8_free"])
        wait("scalar", "LTW", cnt.get("LTW", 0))
        xq_done = 0
        qv_done = [0, 0]
        stt_done = [0, 0]
        for m in range(NTT):
            tb = m % 2
            wait("scalar", "CV", stt_done[tb])
            ldx = dma("scalar", xrd_d[tb][:], xn_sp[2 + m * 128: 2 + (m + 1) * 128, :], s=f"LXD{tb}")
            wait("vector", f"LXD{tb}", ldx)
            if m == 0:
                wait("vector", "CV", sxb)
            stt_done[tb] = opw("vector", lambda e, tb=tb: e.scalar_tensor_tensor(
                out=uq[:], in0=xrd_d[tb][:], scalar=invsx_b[:],
                in1=cneg_b[:, 0:1].broadcast_to([128, C]), op0=ALU.mult, op1=ALU.max))
            opw("vector", lambda e: e.tensor_scalar(out=xq_i8[:], in0=uq[:], scalar1=127.0,
                                                    scalar2=None, op0=ALU.min))
            if m >= 2:
                wait("vector", f"LTQ{tb}", qv_done[tb])
            qv = op("vector", "CV", lambda e, tb=tb: e.tensor_copy(
                out=xq_t[tb][:], in_=xq_i8[:]))
            wait("sync", "CV", qv)
            xq_done = dma("sync",
                          xq_fm[:, :].rearrange("p (c f) -> p c f", c=8)[:, :, m * 128:(m + 1) * 128],
                          xq_t[tb][:, :].rearrange("p (c q) -> p c q", c=8),
                          s=f"LTQ{tb}", transpose=True)
            qv_done[tb] = xq_done

        w2_state = {"wd_tr": 0, "wstg_fr": 0, "emitted": 0}

        def emit_w2_chunk(hch):
            if hch == 0:
                wait("gpsimd", "CT", slab_av[nslabs - 1])
            wait("gpsimd", "CV", w2_state["wstg_fr"])
            ldw = dma("gpsimd", wstg_d[:], wd[hch * 128:(hch + 1) * 128, :], s="LXV")
            wait("vector", "LXV", ldw)
            opw("vector", lambda e: e.reduce_max(out=winv[:, 0:1], in_=wstg_d[:], axis=AX.X,
                                                 apply_absolute_value=True))
            opq("vector", lambda e: e.tensor_scalar(out=winv[:, 1:2], in0=winv[:, 0:1],
                                                    scalar1=1.0 / QMAX, scalar2=None, op0=ALU.mult))
            opw("vector", lambda e: e.reciprocal(out=winv[:, 2:3], in_=winv[:, 0:1]))
            opw("vector", lambda e: e.tensor_scalar(
                out=winv[:, 2:3], in0=winv[:, 2:3], scalar1=QMAX * (1.0 - 2.0 ** -22),
                scalar2=None, op0=ALU.mult))
            i8n = op("vector", "CV", lambda e: e.scalar_tensor_tensor(
                out=wi8_d[:], in0=wstg_d[:], scalar=winv[:, 2:3],
                in1=wstg_d[:], op0=ALU.mult, op1=ALU.bypass))
            w2_state["wstg_fr"] = i8n
            wait("scalar", "CV", i8n)
            wait("scalar", "LTD", w2_state["wd_tr"])
            sc = op("scalar", "CS", lambda e: e.activation(
                out=wnb_d[:], in_=wi8_d[:], func=ACT.Copy, scale=winv[:, 1:2]))
            wait("sync", "CS", sc)
            w2_state["wd_tr"] = dma("sync",
                        rhs_d[:, :].rearrange("p (c f) -> p c f", c=NIC)[:, :, hch * 128:(hch + 1) * 128],
                        wnb_d[:, :].rearrange("p (c q) -> p c q", c=NIC),
                        s="LTD", transpose=True)

        def w2_drive(want):
            while w2_state["emitted"] < min(want, 8):
                emit_w2_chunk(w2_state["emitted"])
                w2_state["emitted"] += 1

        wait("tensor", "LTQ0", qv_done[0])
        wait("tensor", "LTQ1", qv_done[1])
        tpn_par = [0, 0]
        tp_sp = [0, 0]
        tp_sp_of = [0] * NIC
        ld_used = [0, 0]
        for ch in range(NIC):
            b = ch % 2
            if ch == 0:
                wait("sync", "LGL", w1_state["lg_done"])
            wait("sync", "CT", ld_used[b])
            dma("sync", lgs[b][:], lg_sp[ch, :, :], s=f"LXL{b}")
            ldu = dma("sync", lus[b][:], lu_sp[ch, :, :], s=f"LXL{b}")
            wait("tensor", f"LXL{b}", ldu)
            mmg = 0
            for nn in range((T + 511) // 512):
                n0 = nn * 512
                nw = min(512, T - n0)
                pg = pD if nn % 2 == 0 else pA[:, 0:512]
                pu = pE if nn % 2 == 0 else pA[:, 512:1024]
                wait("tensor", "CV", tpn_par[nn % 2])
                for ccn in range(8):
                    opq("tensor", lambda e, b=b, ccn=ccn, n0=n0, nw=nw, pg=pg: e.matmul(
                        pg[:, 0:nw], lhsT=lgs[b][:, ccn * 128:(ccn + 1) * 128],
                        rhs=xq_fm[:, ccn * T + n0: ccn * T + n0 + nw],
                        start=(ccn == 0), stop=(ccn == 7)))
                for ccn in range(8):
                    def mku(e, b=b, ccn=ccn, n0=n0, nw=nw, pu=pu):
                        return e.matmul(
                            pu[:, 0:nw], lhsT=lus[b][:, ccn * 128:(ccn + 1) * 128],
                            rhs=xq_fm[:, ccn * T + n0: ccn * T + n0 + nw],
                            start=(ccn == 0), stop=(ccn == 7))
                    if ccn == 7:
                        mmg = op("tensor", "CT", mku)
                    else:
                        opq("tensor", mku)
                wait("scalar", "CT", mmg)
                sg = silu_g[nn % 2]
                sgn = op("scalar", "CS", lambda e, nw=nw, pg=pg, sg=sg: e.activation(
                    out=sg[:, 0:nw], in_=pg[:, 0:nw], func=ACT.Silu, scale=sx_b[:]))
                wait("vector", "CS", sgn)
                if nn == 0 and ch >= 2:
                    wait("vector", f"LGT{b}", tp_sp[b])
                tpn = op("vector", "CV", lambda e, b=b, n0=n0, nw=nw, pu=pu, sg=sg: e.scalar_tensor_tensor(
                    out=tpre[b][:, n0:n0 + nw], in0=pu[:, 0:nw], scalar=sx_b[:],
                    in1=sg[:, 0:nw], op0=ALU.mult, op1=ALU.mult))
                tpn_par[nn % 2] = tpn
            ld_used[b] = cnt["CT"]
            opw("vector", lambda e, b=b: e.reduce_max(out=red1[:], in_=tpre[b][:],
                                                      axis=AX.X, apply_absolute_value=True))
            tpv = op("vector", "CV", lambda e: e.tensor_tensor(out=amax_t[:], in0=amax_t[:],
                                                               in1=red1[:], op=ALU.max))
            wait("gpsimd", "CV", tpv)
            tp_sp[b] = dma("gpsimd", t_sp[ch, :, :], tpre[b][:], s=f"LGT{b}")
            tp_sp_of[ch] = tp_sp[b]
            w2_drive((ch + 1) * 8 // NIC + 1)

        opw("vector", lambda e: e.tensor_copy(out=red1[0:1, 0:1], in_=red1[0:1, 0:1]))
        am2 = op("vector", "CV", lambda e: e.tensor_copy(out=red1[:], in_=amax_t[:]))
        wait("gpsimd", "CV", am2)
        dma("gpsimd", mrow_sp, red1[:, 0:1], s="LGM")
        wait("gpsimd", "LGM", cnt["LGM"])
        g2 = dma("gpsimd", mrow[:], mrow_sp, s="LGM")
        wait("vector", "LGM", g2)
        opq("vector", lambda e: e.reduce_max(out=m11[:], in_=mrow[:], axis=AX.X))
        bc2 = op("vector", "CV",
                 lambda e: e.tensor_copy(out=gmax2[:], in_=m11[:].broadcast_to([1, 8])))
        wait("gpsimd", "CV", bc2)
        cci2 = dma("gpsimd", cc2_in.ap(), gmax2[:], s="LGC")
        wait("gpsimd", "LGC", cci2)
        cc2n = op("gpsimd", "CG", lambda e: e.collective_compute(
            "AllReduce", ALU.max, replica_groups=[list(range(NCORES))],
            ins=[cc2_in.ap().opt()], outs=[cc2_out.ap().opt()]))
        wait("gpsimd", "CG", cc2n)
        cco2 = dma("gpsimd", gmax2[:], cc2_out.ap(), s="LGC")
        wait("vector", "LGC", cco2)
        opw("vector", lambda e: e.tensor_tensor(out=sc_s[:, 4:5], in0=sc_s[:, 1:2],
                                                in1=gmax2[:, 0:1], op=ALU.mult))
        opw("vector", lambda e: e.tensor_scalar(out=sc_s[:, 4:5], in0=sc_s[:, 4:5],
                                                scalar1=1.0 / QMAX, scalar2=None, op0=ALU.mult))
        shv = op("vector", "CV", lambda e: e.reciprocal(out=sc_s[:, 5:6], in_=sc_s[:, 4:5]))
        wait("tensor", "CV", shv)
        bb2 = op("tensor", "CT", lambda e: e.matmul(pE[:, 6:8], lhsT=ones1f[:, :],
                                                    rhs=sc_s[:, 4:6], start=True, stop=True))
        wait("vector", "CT", bb2)
        opq("vector", lambda e: e.tensor_copy(out=sh_b[:], in_=pE[:, 6:7]))
        shb = op("vector", "CV", lambda e: e.tensor_copy(out=invsh_b[:], in_=pE[:, 7:8]))

        d2_cs_last = cnt["CS"]
        w2_drive(8)
        wait("tensor", "LTD", w2_state["wd_tr"])
        trd_free = [0, 0]
        mm_of = [0, 0]
        out_free = [0, 0]
        srd_free = [0, 0]
        ov_par = [0, 0]
        gn_prev = [d2_cs_last] * 4
        for s in range(nslabs):
            sl = slabs[s]
            ntt = len(tiles_of[s])
            mmd = 0
            for ch in range(NIC):
                tb = ch % 2
                wait("sync", "CV", trd_free[tb])
                wait("sync", f"LGT{ch % 2}", tp_sp_of[ch])
                ldt = dma("sync", trd[tb][:, 0:sl], t_sp[ch, :, s * SLAB:s * SLAB + sl], s=f"LXT{tb}")
                wait("scalar", f"LXT{tb}", ldt)
                if s == 0 and ch == 0:
                    wait("scalar", "CV", shb)
                un = op("scalar", "CS", lambda e, tb=tb, sl=sl: e.activation(
                    out=trd[tb][:, 0:sl], in_=trd[tb][:, 0:sl], func=ACT.Copy,
                    scale=invsh_b[:]))
                wait("vector", "CS", un)
                opw("vector", lambda e, tb=tb, sl=sl: e.tensor_scalar(
                    out=h_i8[:, 0:sl], in0=trd[tb][:, 0:sl], scalar1=-128.0,
                    scalar2=127.0, op0=ALU.max, op1=ALU.min))
                wait("vector", "CT", mm_of[tb])
                hq = op("vector", "CV", lambda e, tb=tb, sl=sl: e.tensor_copy(
                    out=h_cb[tb][:, 0:sl], in_=h_i8[:, 0:sl]))
                trd_free[tb] = hq
                wait("tensor", "CV", hq)
                nmm = 2 * ntt
                imm = 0
                for r in range(ntt):
                    qn = qts[s * 4 + r]
                    if ch == 0:
                        wait("tensor", "CS", gn_prev[r])
                    for half in range(2):
                        if ntt >= 3 and r == 2:
                            pdst = (pD if half == 0 else pE)[0:qn, 0:512]
                        else:
                            pt = (pC, pA, None, pB)[r]
                            pdst = pt[0:qn, half * 512:half * 512 + 512]
                        imm += 1
                        def mkd(e, ch=ch, r=r, qn=qn, tb=tb, pdst=pdst, half=half):
                            return e.matmul(
                                pdst,
                                lhsT=h_cb[tb][:, r * QT: r * QT + qn],
                                rhs=rhs_d[:, ch * C + half * 512:ch * C + half * 512 + 512],
                                start=(ch == 0), stop=(ch == NIC - 1))
                        if imm == nmm:
                            mmd = op("tensor", "CT", mkd)
                        else:
                            opq("tensor", mkd)
                mm_of[tb] = mmd if mmd else mm_of[tb]
            for r in range(ntt):
                jj = s * 4 + r
                qn = qts[jj]
                wait("scalar", "CT", mmd)
                wait("scalar", "CV", ov_par[jj % 2])
                gb = g_t[jj % 2]
                if ntt >= 3 and r == 2:
                    opq("scalar", lambda e, qn=qn, gb=gb: e.activation(
                        out=gb[0:qn, 0:512], in_=pD[0:qn, :], func=ACT.Silu,
                        scale=sh_b[0:qn, :]))
                    gn = op("scalar", "CS", lambda e, qn=qn, gb=gb: e.activation(
                        out=gb[0:qn, 512:1024], in_=pE[0:qn, :], func=ACT.Silu,
                        scale=sh_b[0:qn, :]))
                else:
                    pdn = (pC, pA, None, pB)[r]
                    gn = op("scalar", "CS", lambda e, qn=qn, pdn=pdn, gb=gb: e.activation(
                        out=gb[0:qn, :], in_=pdn[0:qn, :], func=ACT.Silu,
                        scale=sh_b[0:qn, :]))
                gn_prev[r] = gn
                rb = r % 2
                wait("scalar", "CV", srd_free[rb])
                dma("scalar", s_rd[rb][0:qn, :], s_sp[jj * QT:jj * QT + qn, :], s=f"LXF{rb}")
                ldx2 = dma("scalar", x_rd[rb][0:qn, :], xh[2 + jj * QT:2 + jj * QT + qn, :], s=f"LXF{rb}")
                wait("vector", "CS", gn)
                wait("vector", f"ST{rb}", out_free[rb])
                wait("vector", f"LXF{rb}", ldx2)
                opw("vector", lambda e, rb=rb, qn=qn, gb=gb: e.tensor_tensor(
                    out=out_t[rb][0:qn, :], in0=s_rd[rb][0:qn, :], in1=gb[0:qn, :],
                    op=ALU.mult))
                ov = op("vector", "CV", lambda e, rb=rb, qn=qn: e.tensor_tensor(
                    out=out_t[rb][0:qn, :], in0=out_t[rb][0:qn, :], in1=x_rd[rb][0:qn, :],
                    op=ALU.add))
                srd_free[rb] = ov
                ov_par[jj % 2] = ov
                wait("gpsimd", "CV", ov)
                out_free[rb] = dma("gpsimd", out[jj * QT:jj * QT + qn, :],
                                   out_t[rb][0:qn, :], s=f"ST{rb}")

        wait("sync", "ST0", cnt.get("ST0", 0))
        wait("sync", "ST1", cnt.get("ST1", 0))

        E(nc.allow_low_precision("bf16 softmax weights / quantized operands are sanctioned for this fp32 reference (bf16 matmul + fp32 accum)"))
        block = E(nc.Block())

        @block.sync
        def _(e):
            for f in prog["sync"]:
                f(e)

        @block.vector
        def _(e):
            for f in prog["vector"]:
                f(e)

        @block.scalar
        def _(e):
            for f in prog["scalar"]:
                f(e)

        @block.tensor
        def _(e):
            for f in prog["tensor"]:
                f(e)

        @block.gpsimd
        def _(e):
            for f in prog["gpsimd"]:
                f(e)

    return nc


def make_masks():
    k = np.arange(128)[:, None]
    q = np.arange(QT)[None, :]
    d = k - q - 2
    even = (q % 2) == 0
    m1 = np.where(even, (d >= 0) & (d <= 3), (d >= -1) & (d <= 2))
    m2 = np.where(even, (d >= -2) & (d <= 1), (d >= -3) & (d <= 0))

    def rep(m):
        a = np.full((128, 1024), NEG, np.float32)
        for h in range(HEADS):
            a[:, h * 128:h * 128 + QT] = np.where(m, 0.0, NEG)
        return a
    return rep(m1), rep(m2)


def kernel(L0_attn_x, L0_attn_w_qkv, L0_attn_w_out, L0_attn_b_out,
           L0_ffn_clip_x, L0_ffn_clip_down, L0_ffn_w_gate, L0_ffn_w_up,
           L0_ffn_w_down):
    x = np.ascontiguousarray(np.asarray(L0_attn_x, dtype=np.float32)[0])
    L = x.shape[0]
    T = L // NCORES
    nfull, rem = T // QT, T % QT
    ntiles = nfull + (1 if rem else 0)
    nxt = (T + 4 + 127) // 128
    XROWS = nxt * 128

    wqkv = np.ascontiguousarray(np.asarray(L0_attn_w_qkv, np.float32))
    woutw = np.ascontiguousarray(np.asarray(L0_attn_w_out, np.float32))
    boutv = np.ascontiguousarray(np.asarray(L0_attn_b_out, np.float32)).reshape(1, C)
    wgv = np.ascontiguousarray(np.asarray(L0_ffn_w_gate, np.float32))
    wuv = np.ascontiguousarray(np.asarray(L0_ffn_w_up, np.float32))
    wdv = np.ascontiguousarray(np.asarray(L0_ffn_w_down, np.float32))
    clip2 = np.zeros((1, 8), np.float32)
    clip2[0, 0] = np.asarray(L0_ffn_clip_x, np.float32).reshape(-1)[0]
    clip2[0, 1] = np.asarray(L0_ffn_clip_down, np.float32).reshape(-1)[0]

    mask1, mask2 = make_masks()
    lastq = rem if rem else QT
    mask2_first = mask2.copy()
    for h in range(HEADS):
        mask2_first[:, h * 128:h * 128 + 2] = NEG
    mask1_last = mask1.copy()
    for h in range(HEADS):
        mask1_last[:, h * 128 + lastq - 2:h * 128 + lastq] = NEG

    bmul_base = np.zeros((1, ntiles * QT), np.float32)
    for j in range(ntiles):
        qn = min(QT, T - j * QT)
        bmul_base[0, j * QT:j * QT + qn] = 2.0

    in_maps = []
    for c in range(NCORES):
        t0 = c * T
        xa = np.zeros((XROWS, C), np.float32)
        lo, hi = t0 - 2, t0 + T + 2
        src_lo, src_hi = max(0, lo), min(L, hi)
        xa[src_lo - lo: src_hi - lo] = x[src_lo:src_hi]
        bm = bmul_base.copy()
        m1zv, m2zv = mask1, mask2
        if c == 0:
            bm[0, 0:2] = 1.0
            m2zv = mask2_first
        if c == NCORES - 1:
            bm[0, (ntiles - 1) * QT + lastq - 2:(ntiles - 1) * QT + lastq] = 1.0
            m1zv = mask1_last
        in_maps.append(dict(
            xh=xa, wqkv=wqkv, wout=woutw, bout=boutv, clip2=clip2,
            wg=wgv, wu=wuv, wd=wdv,
            mask1a=mask1, mask2a=mask2, mask1z=m1zv, mask2z=m2zv, bmul=bm))

    nc = build_nc(T)
    res = run_bass_kernel_spmd(nc, in_maps, core_ids=list(range(NCORES)))
    if DEBUG_SPILLS:
        DEBUG_RES.clear()
        DEBUG_RES.append(res)
    outs = [np.asarray(res.results[c]["out"], np.float32) for c in range(NCORES)]
    return np.concatenate(outs, axis=0)[None, :, :]


# version 17
# speedup vs baseline: 1.0642x; 1.0385x over previous
import functools
import operator
import numpy as np
import ml_dtypes
from contextlib import ExitStack

import concourse.bass as bass
import concourse.mybir as mybir
from concourse.bass_utils import run_bass_kernel_spmd

F32 = mybir.dt.float32
BF16 = mybir.dt.bfloat16
I8 = mybir.dt.int8
U8 = mybir.dt.uint8
ALU = mybir.AluOpType
ACT = mybir.ActivationFunctionType
AX = mybir.AxisListType

NCORES = 8
C = 1024
I3 = 3072
II = 2816
NIC = II // 128
HEADS = 8
QMAX = 127.0
EPS_LN = 1e-5
NEG = -30000.0
ZEPS = 1e-30
QT = 124
DEBUG_SPILLS = False
DEBUG_RES = []


def build_nc(T):
    nfull, rem = T // QT, T % QT
    ntiles = nfull + (1 if rem else 0)
    qts = [QT] * nfull + ([rem] if rem else [])
    SLAB = 4 * QT
    nslab_full, srem = T // SLAB, T % SLAB
    slabs = [SLAB] * nslab_full + ([srem] if srem else [])
    nslabs = len(slabs)
    sw = [slabs[s] + 4 for s in range(nslabs)]
    last_s = nslabs - 1
    sw[last_s] = max(sw[last_s], (ntiles - 1) * QT - SLAB * last_s + 128)
    SLOTW = max(sw)
    nxt = (T + 4 + 127) // 128
    XROWS = nxt * 128
    FRW = XROWS
    NTT = T // 128
    assert T % 128 == 0
    tiles_of = [[] for _ in range(nslabs)]
    for j in range(ntiles):
        tiles_of[min(j // 4, nslabs - 1)].append(j)

    nc = bass.Bass("TRN2", target_bir_lowering=False, detect_race_conditions=False)

    ap = lambda *a, **k: nc.dram_tensor(*a, **k).ap()
    xh = ap("xh", [XROWS, C], F32, kind="ExternalInput")
    wqkv = ap("wqkv", [C, I3], F32, kind="ExternalInput")
    wout = ap("wout", [C, C], F32, kind="ExternalInput")
    bout = ap("bout", [1, C], F32, kind="ExternalInput")
    clip2 = ap("clip2", [1, 8], F32, kind="ExternalInput")
    wg = ap("wg", [II, C], F32, kind="ExternalInput")
    wu = ap("wu", [II, C], F32, kind="ExternalInput")
    wd = ap("wd", [C, II], F32, kind="ExternalInput")
    mask1a_d = ap("mask1a", [128, 1024], F32, kind="ExternalInput")
    mask2a_d = ap("mask2a", [128, 1024], F32, kind="ExternalInput")
    mask1z_d = ap("mask1z", [128, 1024], F32, kind="ExternalInput")
    mask2z_d = ap("mask2z", [128, 1024], F32, kind="ExternalInput")
    bmul_d = ap("bmul", [1, ntiles * QT], F32, kind="ExternalInput")
    out = ap("out", [T, C], F32, kind="ExternalOutput")

    spk = dict(kind="ExternalOutput") if DEBUG_SPILLS else {}
    xn_sp = ap("xn_sp", [XROWS, C], F32, **spk)
    s_sp = ap("s_sp", [ntiles * QT, C], BF16, **spk)
    t_sp = ap("t_sp", [NIC, 128, T], F32, **spk)
    lg_sp = ap("lg_sp", [NIC, 128, 8 * 128], BF16)
    lu_sp = ap("lu_sp", [NIC, 128, 8 * 128], BF16)
    mrow_sp = ap("mrow_sp", [1, 128], F32)
    cc1_in = nc.dram_tensor("cc1_in", [1, 8], F32)
    cc1_out = nc.dram_tensor("cc1_out", [1, 8], F32)
    cc2_in = nc.dram_tensor("cc2_in", [1, 8], F32)
    cc2_out = nc.dram_tensor("cc2_out", [1, 8], F32)

    prog = {e: [] for e in ("sync", "vector", "scalar", "tensor", "gpsimd")}
    cnt = {}

    def inc(s, n=1):
        cnt[s] = cnt.get(s, 0) + n
        return cnt[s]

    with ExitStack() as stk:
        E = stk.enter_context

        R1B, R2B, R3B, R4B = 83968, 48000, 47104, 22528
        arena = E(nc.sbuf_tensor("arena", [128, R1B + R2B + R3B + R4B], U8))
        base = nc.lookup_mloc(arena).addr
        OFF = {"R1": 0, "R2": R1B, "R3": R1B + R2B, "R4": R1B + R2B + R3B}
        SZ = {"R1": R1B, "R2": R2B, "R3": R3B, "R4": R4B}
        _n = [0]

        def ovl(region, off, shape, dtype):
            _n[0] += 1
            name = f"ov{_n[0]}"
            hnd = nc.virtual_sbuf_tensor(name, shape, dtype)
            nc._tensor(hnd.name, hnd.shape, hnd.dtype, type="SB")
            mloc = nc.lookup_mloc(hnd)
            mloc.allocated = True
            mloc.addr = base + OFF[region] + off
            nb = functools.reduce(operator.mul, shape[1:], 1) * mybir.dt.size(dtype)
            assert off % 32 == 0 and off + nb <= SZ[region], (region, off, nb)
            return hnd

        wqkv_b = ovl("R1", 0, [128, 8 * I3], BF16)
        xn_fm = ovl("R1", 49152, [128, 8 * FRW], BF16)
        wstg_gu = [ovl("R4", 8192 + i * 4096, [128, C], F32) for i in range(2)]
        wnb_gu = [ovl("R4", 16384 + i * 2048, [128, C], BF16) for i in range(2)]
        wi8_gu = [ovl("R4", 20480 + i * 1024, [128, C], I8) for i in range(2)]
        lhs_stg = ovl("R3", 20480, [128, 8 * 128], BF16)
        xq_fm = ovl("R1", 0, [128, 8 * T], BF16)
        lgs = [ovl("R1", 32768 + i * 2048, [128, 8 * 128], BF16) for i in range(2)]
        lus = [ovl("R1", 36864 + i * 2048, [128, 8 * 128], BF16) for i in range(2)]
        tpre = [ovl("R1", 40960 + i * (T * 4), [128, T], F32) for i in range(2)]
        qk_r = [ovl("R2", i * 16 * SLOTW * 2, [128, 16 * SLOTW], BF16) for i in range(2)]
        vf_r = [ovl("R2", 64 * SLOTW + i * 16 * SLOTW, [128, 8 * SLOTW], BF16)
                for i in range(2)]
        wstg_d = ovl("R2", 0, [128, II], F32)
        wi8_d = ovl("R2", 11264, [128, II], I8)
        wnb_d = [ovl("R2", 25376 + i * 5632, [128, II], BF16) for i in range(2)]
        h_cb = [ovl("R2", 21856 + i * 1024, [128, SLAB], BF16) for i in range(2)]
        trd = [ovl("R2", 43712 + i * 2016, [128, SLAB], F32) for i in range(2)]
        wout_b = ovl("R3", 0, [128, 8 * C], BF16)
        e1 = ovl("R3", 16384, [128, 1024], BF16)
        e2 = ovl("R3", 18432, [128, 1024], BF16)
        zb1 = ovl("R3", 20480, [128, 1024], BF16)
        zb2 = ovl("R3", 22528, [128, 1024], BF16)
        r1 = ovl("R3", 24576, [128, 1024], BF16)
        r2 = ovl("R3", 26624, [128, 1024], BF16)
        ww = ovl("R3", 28672, [128, 1024], BF16)
        o_sb = ovl("R3", 30720, [128, 1024], BF16)
        s_t = ovl("R3", 32768, [128, 1024], BF16)
        v_tm = [ovl("R3", 34816 + i * 2048, [128, 1024], BF16) for i in range(2)]
        m1a = ovl("R3", 38912, [128, 1024], BF16)
        m2a = ovl("R3", 40960, [128, 1024], BF16)
        m1z = ovl("R3", 43008, [128, 1024], BF16)
        m2z = ovl("R3", 45056, [128, 1024], BF16)
        rhs_d = ovl("R3", 0, [128, NIC * C], BF16)
        x_t = [ovl("R4", i * 4096, [128, C], F32) for i in range(2)]
        xn_t = [ovl("R4", 8192 + i * 4096, [128, C], F32) for i in range(2)]
        xnb_t = [ovl("R4", 16384 + i * 2048, [128, C], BF16) for i in range(2)]
        a1 = ovl("R4", 0, [128, 1024], F32)
        a2 = ovl("R4", 4096, [128, 1024], F32)
        xq_t = [ovl("R4", i * 2048, [128, C], BF16) for i in range(2)]
        xq_i8 = ovl("R4", 4096, [128, C], I8)
        uq = ovl("R4", 5120, [128, C], F32)
        silu_g = [ovl("R4", 5120 + i * 2048, [128, 512], F32) for i in range(2)]
        xrd_d = [ovl("R4", 9216 + i * 4096, [128, C], F32) for i in range(2)]
        out_t = [ovl("R4", i * 4096, [128, C], F32) for i in range(2)]
        x_rd = [ovl("R4", 8192 + i * 4096, [128, C], F32) for i in range(2)]
        g_t = [ovl("R4", 16384, [128, C], BF16), ovl("R3", 45056, [128, C], BF16)]
        s_rd = [ovl("R4", 18432 + i * 2048, [128, C], BF16) for i in range(2)]

        sb = lambda name, shape, dt: E(nc.sbuf_tensor(name, shape, dt))
        bmul_s = sb("bmul_s", [1, ntiles * QT], BF16)
        bout_b = sb("bout_b", [1, C], BF16)
        clip_s = sb("clip_s", [1, 8], F32)
        onesK = sb("onesK", [128, 128], BF16)
        ones1f = sb("ones1f", [1, 128], F32)
        epsrow = sb("epsrow", [1, 512], F32)
        ln_s = sb("ln_s", [128, 8], F32)
        amax_x = sb("amax_x", [128, 1], F32)
        amax_t = sb("amax_t", [128, 1], F32)
        red1 = sb("red1", [128, 1], F32)
        mrow = sb("mrow", [1, 128], F32)
        m11 = sb("m11", [1, 1], F32)
        mbias = sb("mbias", [128, 1], F32)
        gmax1 = sb("gmax1", [1, 8], F32)
        gmax2 = sb("gmax2", [1, 8], F32)
        sc_s = sb("sc_s", [1, 8], F32)
        sx_b = sb("sx_b", [128, 1], F32)
        invsx_b = sb("invsx_b", [128, 1], F32)
        sh_b = sb("sh_b", [128, 1], F32)
        invsh_b = sb("invsh_b", [128, 1], F32)
        winv = sb("winv", [128, 8], F32)
        h_i8 = sb("h_i8", [128, SLAB], I8)
        epsb = sb("epsb", [128, 1], F32)
        cneg_b = sb("cneg_b", [128, 1], F32)

        pA = E(nc.psum_tensor("pA", [128, 1024], F32))
        pB = E(nc.psum_tensor("pB", [128, 1024], F32))
        pC = E(nc.psum_tensor("pC", [128, 1024], F32))
        pD = E(nc.psum_tensor("pD", [128, 512], F32))
        pE = E(nc.psum_tensor("pE", [128, 512], F32))

        sem = {}

        def S(n):
            if n not in sem:
                sem[n] = E(nc.semaphore(n))
            return sem[n]

        for _nm in ("CV", "CS", "CT", "CG"):
            S(_nm)

        def emit(eng, f):
            prog[eng].append(f)

        def dma(eng, out_ap, in_ap, s, transpose=False):
            S(s)
            n = inc(s, 16)
            def f(e, out_ap=out_ap, in_ap=in_ap, s=s, transpose=transpose):
                if transpose:
                    e.dma_start(out=out_ap, in_=in_ap, transpose=True).then_inc(sem[s], 16)
                else:
                    e.dma_start(out=out_ap, in_=in_ap).then_inc(sem[s], 16)
            emit(eng, f)
            return n

        def wait(eng, s, val):
            S(s)
            if val > 0:
                emit(eng, lambda e, s=s, val=val: e.wait_ge(sem[s], val))

        ENG_SEM = {"vector": "CV", "scalar": "CS", "tensor": "CT", "gpsimd": "CG"}

        def op(eng, s, fmaker):
            assert s == ENG_SEM[eng]
            if eng in ("vector", "scalar"):
                prev = cnt.get(s, 0)
                if prev > 0:
                    emit(eng, lambda e, s=s, prev=prev: e.wait_ge(sem[s], prev))
            n = inc(s)
            emit(eng, lambda e, fmaker=fmaker, s=s: fmaker(e).then_inc(sem[s], 1))
            return n

        def opq(eng, fmaker):
            if eng in ("vector", "scalar"):
                op(eng, ENG_SEM[eng], fmaker)
            else:
                emit(eng, lambda e, fmaker=fmaker: fmaker(e))

        def opw(eng, fmaker):
            return op(eng, ENG_SEM[eng], fmaker)

        st3 = lambda t, qn: t[:, :].rearrange("p (h q) -> p h q", h=8)[:, :, 0:qn]

        dma("sync", clip_s[:], clip2, s="LC")
        lcg_of = [0] * 24
        wqkv_b3 = wqkv_b[:, :].rearrange("p (c n) -> p c n", c=8)
        wqkv_s3 = wqkv.rearrange("(c p) n -> p c n", p=128)

        def ld_wqkv_chunk(jj):
            lcg_of[jj] = dma("gpsimd", wqkv_b3[:, :, jj * 128:(jj + 1) * 128],
                             wqkv_s3[:, :, jj * 128:(jj + 1) * 128], s="LCG")

        for jj in range(6):
            ld_wqkv_chunk(jj)
        for (dst, srcd) in ((m1a, mask1a_d), (m2a, mask2a_d), (m1z, mask1z_d),
                            (m2z, mask2z_d)):
            dma("gpsimd", dst[:], srcd, s="LCG")
        ld_masks = cnt["LCG"]
        dma("gpsimd", bmul_s[:], bmul_d, s="LCG")
        dma("gpsimd", bout_b[:], bout, s="LCG")
        for jj in range(6, 10):
            ld_wqkv_chunk(jj)
        for ccn in range(8):
            dma("gpsimd", wout_b[:, ccn * C:(ccn + 1) * C],
                wout[ccn * 128:(ccn + 1) * 128, :], s="LCG")
        ld_wout = cnt["LCG"]
        for jj in range(10, 24):
            ld_wqkv_chunk(jj)
        ld_wq = cnt["LCG"]
        ld_const = cnt["LC"]

        wait("vector", "LC", ld_const)
        opq("vector", lambda e: e.memset(onesK[:], 1.0))
        opq("vector", lambda e: e.memset(ones1f[:], 1.0))
        opq("vector", lambda e: e.memset(epsrow[:], ZEPS))
        opq("vector", lambda e: e.memset(amax_x[:], 0.0))
        opq("vector", lambda e: e.memset(amax_t[:], 0.0))
        opq("vector", lambda e: e.memset(epsb[:], EPS_LN))
        opq("vector", lambda e: e.memset(cneg_b[:], -128.0))
        opq("vector", lambda e: e.memset(pA[:, :], 0.0))
        pa_init = op("vector", "CV", lambda e: e.memset(pB[:, :], 0.0))

        tr_done = [0] * nxt
        sp_done = [0] * nxt
        xa_done = [0] * nxt
        for i in range(nxt):
            b = i % 2
            if i >= 2:
                wait("scalar", "CV", xa_done[i - 2])
            ldx = dma("scalar", x_t[b][:], xh[i * 128:(i + 1) * 128, :], s=f"LXX{b}")
            wait("vector", f"LXX{b}", ldx)
            if i >= 2:
                wait("vector", f"LGX{b}", sp_done[i - 2])
            opw("vector", lambda e, b=b: e.reduce_sum(out=ln_s[:, 0:1], in_=x_t[b][:], axis=AX.X))
            wait("scalar", f"LXX{b}", ldx)
            if i >= 2:
                wait("scalar", f"LTA{b}", tr_done[i - 2])
            sq = op("scalar", "CS", lambda e, b=b: e.activation(
                out=xnb_t[b][:], in_=x_t[b][:], func=ACT.Square, accum_out=ln_s[:, 1:2]))
            wait("vector", "CS", sq)
            opw("vector", lambda e: e.tensor_scalar(out=ln_s[:, 2:3], in0=ln_s[:, 0:1],
                                                    scalar1=1.0 / C, scalar2=None, op0=ALU.mult))
            opq("vector", lambda e: e.tensor_scalar(out=ln_s[:, 3:4], in0=ln_s[:, 1:2],
                                                    scalar1=1.0 / C, scalar2=None, op0=ALU.mult))
            opw("vector", lambda e: e.tensor_tensor(out=ln_s[:, 4:5], in0=ln_s[:, 2:3],
                                                    in1=ln_s[:, 2:3], op=ALU.mult))
            vv = op("vector", "CV", lambda e: e.tensor_tensor(out=ln_s[:, 4:5], in0=ln_s[:, 3:4],
                                                              in1=ln_s[:, 4:5], op=ALU.subtract))
            wait("scalar", "CV", vv)
            sq2 = op("scalar", "CS", lambda e: e.activation(out=ln_s[:, 5:6], in_=ln_s[:, 4:5],
                                                            func=ACT.Sqrt, bias=epsb[:]))
            wait("vector", "CS", sq2)
            opw("vector", lambda e: e.reciprocal(out=ln_s[:, 6:7], in_=ln_s[:, 5:6]))
            opw("vector", lambda e, b=b: e.scalar_tensor_tensor(
                out=xn_t[b][:], in0=x_t[b][:], scalar=ln_s[:, 2:3],
                in1=ln_s[:, 6:7].broadcast_to([128, C]),
                op0=ALU.subtract, op1=ALU.mult))
            opw("vector", lambda e, b=b: e.reduce_max(out=red1[:], in_=xn_t[b][:], axis=AX.X,
                                                      apply_absolute_value=True))
            opq("vector", lambda e: e.tensor_tensor(out=amax_x[:], in0=amax_x[:],
                                                    in1=red1[:], op=ALU.max))
            xa_done[i] = op("vector", "CV",
                            lambda e, b=b: e.tensor_copy(out=xnb_t[b][:], in_=xn_t[b][:]))
            wait("gpsimd", "CV", xa_done[i])
            sp_done[i] = dma("gpsimd", xn_sp[i * 128:(i + 1) * 128, :], xn_t[b][:], s=f"LGX{b}")
            wait("sync", "CV", xa_done[i])
            tr_done[i] = dma("sync",
                             xn_fm[:, :].rearrange("p (c f) -> p c f", c=8)[:, :, i * 128:(i + 1) * 128],
                             xnb_t[b][:, :].rearrange("p (c q) -> p c q", c=8),
                             s=f"LTA{b}", transpose=True)
        opw("vector", lambda e: e.tensor_copy(out=red1[0:1, 0:1], in_=red1[0:1, 0:1]))
        amx = op("vector", "CV", lambda e: e.tensor_copy(out=red1[:], in_=amax_x[:]))
        wait("gpsimd", "CV", amx)
        dma("gpsimd", mrow_sp, red1[:, 0:1], s="LGM")
        wait("gpsimd", "LGM", cnt["LGM"])
        g1 = dma("gpsimd", mrow[:], mrow_sp, s="LGM")
        wait("vector", "LGM", g1)
        opq("vector", lambda e: e.reduce_max(out=m11[:], in_=mrow[:], axis=AX.X))
        bc1 = op("vector", "CV",
                 lambda e: e.tensor_copy(out=gmax1[:], in_=m11[:].broadcast_to([1, 8])))
        wait("gpsimd", "CV", bc1)
        cci = dma("gpsimd", cc1_in.ap(), gmax1[:], s="LGC")
        wait("gpsimd", "LGC", cci)
        cc1n = op("gpsimd", "CG", lambda e: e.collective_compute(
            "AllReduce", ALU.max, replica_groups=[list(range(NCORES))],
            ins=[cc1_in.ap().opt()], outs=[cc1_out.ap().opt()]))
        wait("gpsimd", "CG", cc1n)
        cco1 = dma("gpsimd", gmax1[:], cc1_out.ap(), s="LGC")

        w1_state = {"stg_free": 0, "i8_free": 0, "lg_done": 0, "emitted": 0,
                    "trn_b": [0, 0], "ld_of": [0, 0], "stt_of": [0, 0],
                    "sc_of": [0, 0]}
        NW1 = 2 * NIC

        def w1_src(idx):
            src_ap = wg if idx < NIC else wu
            ch = idx % NIC
            return src_ap[ch * 128:(ch + 1) * 128, :]

        def emit_w1_chunk(idx):
            dst_sp = lg_sp if idx < NIC else lu_sp
            ch = idx % NIC
            b = idx % 2
            o4 = b * 4
            if idx == 0:
                w1_state["ld_of"][0] = dma("sync", wstg_gu[0][:], w1_src(0), s="LXW0")
            if idx + 1 < NW1:
                nb = (idx + 1) % 2
                wait("sync", "CV", w1_state["stt_of"][nb])
                w1_state["ld_of"][nb] = dma("sync", wstg_gu[nb][:], w1_src(idx + 1), s="LXW0")
            wait("vector", "LXW0", w1_state["ld_of"][b])
            wait("vector", "CS", w1_state["sc_of"][b])
            opw("vector", lambda e, o4=o4, b=b: e.reduce_max(
                out=winv[:, o4:o4 + 1], in_=wstg_gu[b][:], axis=AX.X,
                apply_absolute_value=True))
            opq("vector", lambda e, o4=o4: e.tensor_scalar(
                out=winv[:, o4 + 1:o4 + 2], in0=winv[:, o4:o4 + 1],
                scalar1=1.0 / QMAX, scalar2=None, op0=ALU.mult))
            opw("vector", lambda e, o4=o4: e.reciprocal(
                out=winv[:, o4 + 2:o4 + 3], in_=winv[:, o4:o4 + 1]))
            opw("vector", lambda e, o4=o4: e.tensor_scalar(
                out=winv[:, o4 + 2:o4 + 3], in0=winv[:, o4 + 2:o4 + 3],
                scalar1=QMAX * (1.0 - 2.0 ** -22), scalar2=None, op0=ALU.mult))
            i8n = op("vector", "CV", lambda e, o4=o4, b=b: e.scalar_tensor_tensor(
                out=wi8_gu[b][:], in0=wstg_gu[b][:], scalar=winv[:, o4 + 2:o4 + 3],
                in1=wstg_gu[b][:], op0=ALU.mult, op1=ALU.bypass))
            w1_state["i8_free"] = i8n
            w1_state["stt_of"][b] = i8n
            wait("scalar", "CV", i8n)
            wait("scalar", "LTW", w1_state["trn_b"][b])
            sc = op("scalar", "CS", lambda e, b=b, o4=o4: e.activation(
                out=wnb_gu[b][:], in_=wi8_gu[b][:], func=ACT.Copy,
                scale=winv[:, o4 + 1:o4 + 2]))
            wait("sync", "CS", sc)
            wait("sync", "LGL", w1_state["lg_done"])
            trn = dma("sync", lhs_stg[:, :].rearrange("p (c q) -> p c q", c=8),
                      wnb_gu[b][:, :].rearrange("p (c q) -> p c q", c=8),
                      s="LTW", transpose=True)
            w1_state["trn_b"][b] = trn
            w1_state["sc_of"][b] = sc
            wait("sync", "LTW", trn)
            w1_state["lg_done"] = dma("sync", dst_sp[ch, :, :], lhs_stg[:, :], s="LGL")

        def w1_drive(frac):
            want = min(NW1, int(NW1 * frac + 0.999))
            while w1_state["emitted"] < want:
                emit_w1_chunk(w1_state["emitted"])
                w1_state["emitted"] += 1

        wait("tensor", "CV", pa_init)
        wait("sync", "LGX0", sp_done[(nxt - 2) - (nxt % 2 == 1)] if nxt >= 2 else 0)
        wait("sync", "LGX0", max(sp_done[i] for i in range(0, nxt, 2)))
        wait("sync", "LGX1", max(sp_done[i] for i in range(1, nxt, 2)))
        wait("sync", "LTA0", max(tr_done[i] for i in range(0, nxt, 2)))
        wait("sync", "LTA1", max(tr_done[i] for i in range(1, nxt, 2)))
        qkv_done = [0] * nslabs
        slab_av = [0] * nslabs
        oev_prev = 0
        sev_prev = 0
        ssp_prev = 0
        vtm_av = [0] * ntiles
        vt_of = [0] * ntiles
        gather_of = [0] * ntiles
        aan_of = [0] * (ntiles + 1)
        ph1_done = [False] * ntiles
        mred_prev = [0]

        qkv_ev = [0, 0]
        qkv_kk = [0]
        qkv_em = [0] * nslabs

        def emit_qkv_groups(s, ngroups):
            if ngroups <= 0 or qkv_em[s] >= 24:
                return
            wdt = sw[s]
            basec = s * SLAB
            rslot = s % 2
            if qkv_em[s] == 0:
                need_tile = min(nxt - 1, (basec + wdt - 1) // 128)
                wait("tensor", f"LTA{need_tile % 2}", tr_done[need_tile])
            wait("tensor", "CS", sev_prev)
            while ngroups > 0 and qkv_em[s] < 24:
                j = qkv_em[s]
                kk = qkv_kk[0]
                pq = pD if kk % 2 == 0 else pE
                wait("tensor", "LCG", lcg_of[j])
                wait("tensor", "CV", qkv_ev[kk % 2])
                mmn = 0
                for ccn in range(8):
                    def mk(e, j=j, ccn=ccn, basec=basec, wdt=wdt, pq=pq):
                        return e.matmul(
                            pq[:, 0:wdt],
                            lhsT=wqkv_b[:, ccn * I3 + j * 128:ccn * I3 + (j + 1) * 128],
                            rhs=xn_fm[:, ccn * FRW + basec: ccn * FRW + basec + wdt],
                            start=(ccn == 0), stop=(ccn == 7))
                    if ccn == 7:
                        mmn = op("tensor", "CT", mk)
                    else:
                        opq("tensor", mk)
                wait("vector", "CT", mmn)
                if j == 0 and s >= 2:
                    wait("vector", "CT", slab_av[s - 2])
                if j < 8:
                    def ev(e, j=j, rslot=rslot, wdt=wdt, pq=pq):
                        return e.tensor_scalar(
                            out=qk_r[rslot][:, j * SLOTW:j * SLOTW + wdt],
                            in0=pq[:, 0:wdt], scalar1=1.0 / 32.0, scalar2=None,
                            op0=ALU.mult)
                elif j < 16:
                    def ev(e, j=j, rslot=rslot, wdt=wdt, pq=pq):
                        return e.tensor_copy(
                            out=qk_r[rslot][:, j * SLOTW:j * SLOTW + wdt],
                            in_=pq[:, 0:wdt])
                else:
                    def ev(e, j=j, rslot=rslot, wdt=wdt, pq=pq):
                        return e.tensor_copy(
                            out=vf_r[rslot][:, (j - 16) * SLOTW:(j - 16) * SLOTW + wdt],
                            in_=pq[:, 0:wdt])
                qkv_ev[kk % 2] = op("vector", "CV", ev)
                qkv_kk[0] += 1
                qkv_em[s] += 1
                ngroups -= 1
            if qkv_em[s] == 24 and qkv_done[s] == 0:
                qkv_done[s] = cnt["CV"]

        def emit_phase1(j, s, rslot, qn, kbase):
            stX = pA if j % 2 == 0 else pB
            wait("tensor", "CV", qkv_done[s])
            stn = 0
            for h in range(HEADS):
                def mk(e, h=h, rslot=rslot, kbase=kbase, qn=qn, stX=stX):
                    return e.matmul(
                        stX[:, h * 128:h * 128 + qn],
                        lhsT=qk_r[rslot][:, (8 + h) * SLOTW + kbase:(8 + h) * SLOTW + kbase + 128],
                        rhs=qk_r[rslot][:, h * SLOTW + kbase + 2: h * SLOTW + kbase + 2 + qn],
                        start=True, stop=True)
                if h == HEADS - 1:
                    stn = op("tensor", "CT", mk)
                else:
                    opq("tensor", mk)
            wait("vector", "CT", stn)
            if j >= 1:
                wait("vector", "LGM", gather_of[j - 1])
            mred = op("vector", "CV", lambda e, qn=qn, stX=stX: e.reduce_max(
                out=red1[:], in_=st3(stX, qn), axis=AX.XY, negate=True))
            wait("gpsimd", "CV", mred)
            dma("gpsimd", mrow_sp, red1[:, 0:1], s="LGM")
            wait("gpsimd", "LGM", cnt["LGM"])
            gather_of[j] = dma("gpsimd", mrow[:], mrow_sp, s="LGM")

        emit_qkv_groups(0, 24)
        if nslabs > 1:
            emit_qkv_groups(1, 10)
        for s in range(nslabs):
            rslot = s % 2
            for j in tiles_of[s]:
                qn = qts[j]
                kbase = QT * j - s * SLAB
                vb = j % 2
                if j >= 2:
                    wait("sync", "CT", vtm_av[j - 2])
                wait("sync", "CV", qkv_done[s])
                vt = 0
                for ccn in range(8):
                    vt = dma("sync", v_tm[vb][:, ccn * 128:(ccn + 1) * 128],
                             vf_r[rslot][:, ccn * SLOTW + kbase: ccn * SLOTW + kbase + 128],
                             s=f"LTV{vb}", transpose=True)
                vt_of[j] = vt
                if not ph1_done[j]:
                    emit_phase1(j, s, rslot, qn, kbase)
                stX = pA if j % 2 == 0 else pB
                g_j = gather_of[j]
                wait("vector", "LGM", g_j)
                mv = op("vector", "CV", lambda e: e.tensor_reduce(
                    out=m11[:], in_=mrow[:], axis=AX.X, op=ALU.min))
                wait("tensor", "CV", mv)
                mb = op("tensor", "CT", lambda e: e.matmul(
                    pC[:, 124:125], lhsT=ones1f[:, :], rhs=m11[:], start=True, stop=True))
                wait("vector", "CT", mb)
                opq("vector", lambda e: e.tensor_copy(out=mbias[:], in_=pC[:, 124:125]))
                km1 = m1z if j == ntiles - 1 else m1a
                km2 = m2z if j == 0 else m2a
                if j == 0:
                    wait("vector", "LCG", ld_masks)
                opq("vector", lambda e, km1=km1, stX=stX: e.tensor_tensor(
                    out=a1[:], in0=stX[:, :], in1=km1[:], op=ALU.add))
                aan = op("vector", "CV", lambda e, km2=km2, stX=stX: e.tensor_tensor(
                    out=a2[:], in0=stX[:, :], in1=km2[:], op=ALU.add))
                aan_of[j] = aan
                wait("scalar", "CV", aan)
                opq("scalar", lambda e: e.activation(out=e1[:], in_=a1[:], func=ACT.Exp,
                                                     bias=mbias[:]))
                een = op("scalar", "CS", lambda e: e.activation(out=e2[:], in_=a2[:],
                                                                func=ACT.Exp, bias=mbias[:]))
                wait("tensor", "CS", een)
                edge = (j == 0) or (j == ntiles - 1)
                zn = 0
                for (eb, pz_) in ((e1, stX), (e2, pC)):
                    for half in range(2):
                        n0 = half * 512
                        if edge:
                            opq("tensor", lambda e, pz_=pz_, n0=n0: e.matmul(
                                pz_[:, n0:n0 + 512], lhsT=ones1f[:, :], rhs=epsrow[:, :],
                                start=True, stop=False))
                        def mkz(e, eb=eb, pz_=pz_, n0=n0, edge=edge):
                            return e.matmul(pz_[:, n0:n0 + 512], lhsT=onesK[:, :],
                                            rhs=eb[:, n0:n0 + 512], start=not edge, stop=True)
                        if pz_ is pC and half == 1:
                            zn = op("tensor", "CT", mkz)
                        else:
                            opq("tensor", mkz)
                jn = j + 1
                if jn < ntiles and not ph1_done[jn]:
                    s_n = min(jn // 4, nslabs - 1)
                    if s_n == s or qkv_em[s_n] >= 24:
                        emit_phase1(jn, s_n, s_n % 2, qts[jn], QT * jn - s_n * SLAB)
                        ph1_done[jn] = True
                if s + 1 < nslabs:
                    emit_qkv_groups(s + 1, min(3, min(24, 8 * (j - 4 * s + 1)) - qkv_em[s + 1]))
                wait("scalar", "CT", zn)
                opq("scalar", lambda e, stX=stX: e.activation(out=a1[:], in_=stX[:, :],
                                                              func=ACT.Ln))
                opq("scalar", lambda e: e.activation(out=a2[:], in_=pC[:, :], func=ACT.Ln))
                opq("scalar", lambda e: e.activation(out=r1[:], in_=a1[:], func=ACT.Exp,
                                                     scale=-1.0))
                rr2 = op("scalar", "CS", lambda e: e.activation(out=r2[:], in_=a2[:],
                                                                func=ACT.Exp, scale=-1.0))
                wait("vector", "CS", rr2)
                opq("vector", lambda e: e.tensor_tensor(out=ww[:], in0=e1[:], in1=r1[:], op=ALU.mult))
                opw("vector", lambda e: e.tensor_tensor(out=r2[:], in0=e2[:], in1=r2[:],
                                                        op=ALU.mult))
                wn = op("vector", "CV", lambda e: e.tensor_tensor(out=ww[:], in0=ww[:],
                                                                  in1=r2[:], op=ALU.add))
                wait("tensor", "CV", wn)
                wait("tensor", f"LTV{vb}", vt_of[j])
                wait("tensor", "CS", oev_prev)
                avn = 0
                for h in range(HEADS):
                    def mk(e, h=h, vb=vb, qn=qn):
                        return e.matmul(
                            pC[:, h * 128:h * 128 + qn],
                            lhsT=v_tm[vb][:, h * 128:(h + 1) * 128],
                            rhs=ww[:, h * 128:h * 128 + qn],
                            start=True, stop=True)
                    if h == HEADS - 1:
                        avn = op("tensor", "CT", mk)
                    else:
                        opq("tensor", mk)
                vtm_av[j] = avn
                slab_av[s] = avn
                wait("scalar", "CT", avn)
                oev = op("scalar", "CS", lambda e: e.activation(out=o_sb[:], in_=pC[:, :],
                                                                func=ACT.Copy))
                oev_prev = oev
                if j == 0:
                    wait("tensor", "LCG", ld_wout)
                wait("tensor", "CS", oev)
                wait("tensor", "CS", sev_prev)
                wait("tensor", "CV", qkv_ev[0])
                wait("tensor", "CV", qkv_ev[1])
                for half, pq_ in ((0, pD), (1, pE)):
                    n0 = half * 512
                    for ccn in range(8):
                        opq("tensor", lambda e, ccn=ccn, n0=n0, qn=qn, pq_=pq_: e.matmul(
                            pq_[0:qn, :],
                            lhsT=o_sb[:, ccn * 128:ccn * 128 + qn],
                            rhs=wout_b[:, ccn * C + n0:ccn * C + n0 + 512],
                            start=(ccn == 0), stop=False))
                    won = op("tensor", "CT", lambda e, j=j, n0=n0, qn=qn, pq_=pq_: e.matmul(
                        pq_[0:qn, :],
                        lhsT=bmul_s[:, j * QT:j * QT + qn],
                        rhs=bout_b[:, n0:n0 + 512], start=False, stop=True))
                wait("scalar", "CT", won)
                wait("scalar", "LGS", ssp_prev)
                opq("scalar", lambda e, qn=qn: e.activation(
                    out=s_t[0:qn, 0:512], in_=pD[0:qn, :], func=ACT.Copy))
                sev = op("scalar", "CS", lambda e, qn=qn: e.activation(
                    out=s_t[0:qn, 512:1024], in_=pE[0:qn, :], func=ACT.Copy))
                sev_prev = sev
                wait("gpsimd", "CS", sev)
                ssp_prev = dma("gpsimd", s_sp[j * QT:j * QT + qn, :], s_t[0:qn, :], s="LGS")
                if s + 1 < nslabs:
                    emit_qkv_groups(s + 1, min(24, 8 * (j - 4 * s + 1)) - qkv_em[s + 1])
                w1_drive((j + 1) / ntiles)

        wait("scalar", "LGC", cco1)
        sg1 = op("scalar", "CS", lambda e: e.activation(out=sc_s[:, 0:2], in_=clip_s[:, 0:2],
                                                        func=ACT.Sigmoid))
        wait("vector", "CS", sg1)
        opw("vector", lambda e: e.tensor_tensor(out=sc_s[:, 2:3], in0=sc_s[:, 0:1],
                                                in1=gmax1[:, 0:1], op=ALU.mult))
        opw("vector", lambda e: e.tensor_scalar(out=sc_s[:, 2:3], in0=sc_s[:, 2:3],
                                                scalar1=1.0 / QMAX, scalar2=None, op0=ALU.mult))
        sxv = op("vector", "CV", lambda e: e.reciprocal(out=sc_s[:, 3:4], in_=sc_s[:, 2:3]))
        wait("tensor", "CV", sxv)
        bb = op("tensor", "CT", lambda e: e.matmul(pE[:, 4:6], lhsT=ones1f[:, :],
                                                   rhs=sc_s[:, 2:4], start=True, stop=True))
        wait("vector", "CT", bb)
        opq("vector", lambda e: e.tensor_copy(out=sx_b[:], in_=pE[:, 4:5]))
        sxb = op("vector", "CV", lambda e: e.tensor_copy(out=invsx_b[:], in_=pE[:, 5:6]))

        wait("sync", "CV", w1_state["i8_free"])
        wait("sync", "LTW", cnt.get("LTW", 0))
        wait("sync", "LGL", w1_state["lg_done"])
        wait("scalar", "CV", w1_state["i8_free"])
        wait("scalar", "LTW", cnt.get("LTW", 0))
        xq_done = 0
        qv_done = [0, 0]
        stt_done = [0, 0]
        for m in range(NTT):
            tb = m % 2
            wait("scalar", "CV", stt_done[tb])
            ldx = dma("scalar", xrd_d[tb][:], xn_sp[2 + m * 128: 2 + (m + 1) * 128, :], s=f"LXD{tb}")
            wait("vector", f"LXD{tb}", ldx)
            if m == 0:
                wait("vector", "CV", sxb)
            stt_done[tb] = opw("vector", lambda e, tb=tb: e.scalar_tensor_tensor(
                out=uq[:], in0=xrd_d[tb][:], scalar=invsx_b[:],
                in1=cneg_b[:, 0:1].broadcast_to([128, C]), op0=ALU.mult, op1=ALU.max))
            opw("vector", lambda e: e.tensor_scalar(out=xq_i8[:], in0=uq[:], scalar1=127.0,
                                                    scalar2=None, op0=ALU.min))
            if m >= 2:
                wait("vector", f"LTQ{tb}", qv_done[tb])
            qv = op("vector", "CV", lambda e, tb=tb: e.tensor_copy(
                out=xq_t[tb][:], in_=xq_i8[:]))
            wait("sync", "CV", qv)
            xq_done = dma("sync",
                          xq_fm[:, :].rearrange("p (c f) -> p c f", c=8)[:, :, m * 128:(m + 1) * 128],
                          xq_t[tb][:, :].rearrange("p (c q) -> p c q", c=8),
                          s=f"LTQ{tb}", transpose=True)
            qv_done[tb] = xq_done

        w2_state = {"wd_tr": 0, "wstg_fr": 0, "emitted": 0, "pend": None,
                    "trn_of": [0, 0]}

        def w2_flush():
            if w2_state["pend"] is None:
                return
            sc_p, b_p, hch_p = w2_state["pend"]
            w2_state["pend"] = None
            wait("sync", "CS", sc_p)
            trn = dma("sync",
                      rhs_d[:, :].rearrange("p (c f) -> p c f", c=NIC)[:, :, hch_p * 128:(hch_p + 1) * 128],
                      wnb_d[b_p][:, :].rearrange("p (c q) -> p c q", c=NIC),
                      s="LTD", transpose=True)
            w2_state["wd_tr"] = trn
            w2_state["trn_of"][b_p] = trn

        def emit_w2_chunk(hch):
            w2_flush()
            if hch == 0:
                wait("gpsimd", "CT", slab_av[nslabs - 1])
            wait("gpsimd", "CV", w2_state["wstg_fr"])
            ldw = dma("gpsimd", wstg_d[:], wd[hch * 128:(hch + 1) * 128, :], s="LXV")
            wait("vector", "LXV", ldw)
            opw("vector", lambda e: e.reduce_max(out=winv[:, 0:1], in_=wstg_d[:], axis=AX.X,
                                                 apply_absolute_value=True))
            opq("vector", lambda e: e.tensor_scalar(out=winv[:, 1:2], in0=winv[:, 0:1],
                                                    scalar1=1.0 / QMAX, scalar2=None, op0=ALU.mult))
            opw("vector", lambda e: e.reciprocal(out=winv[:, 2:3], in_=winv[:, 0:1]))
            opw("vector", lambda e: e.tensor_scalar(
                out=winv[:, 2:3], in0=winv[:, 2:3], scalar1=QMAX * (1.0 - 2.0 ** -22),
                scalar2=None, op0=ALU.mult))
            i8n = op("vector", "CV", lambda e: e.scalar_tensor_tensor(
                out=wi8_d[:], in0=wstg_d[:], scalar=winv[:, 2:3],
                in1=wstg_d[:], op0=ALU.mult, op1=ALU.bypass))
            w2_state["wstg_fr"] = i8n
            b = hch % 2
            wait("scalar", "CV", i8n)
            wait("scalar", "LTD", w2_state["trn_of"][b])
            sc = op("scalar", "CS", lambda e, b=b: e.activation(
                out=wnb_d[b][:], in_=wi8_d[:], func=ACT.Copy, scale=winv[:, 1:2]))
            w2_state["pend"] = (sc, b, hch)

        def w2_drive(want):
            while w2_state["emitted"] < min(want, 8):
                emit_w2_chunk(w2_state["emitted"])
                w2_state["emitted"] += 1

        wait("tensor", "LTQ0", qv_done[0])
        wait("tensor", "LTQ1", qv_done[1])
        tpn_par = [0, 0]
        tp_sp = [0, 0]
        tp_sp_of = [0] * NIC
        ld_used = [0, 0]
        for ch in range(NIC):
            b = ch % 2
            if ch == 0:
                wait("sync", "LGL", w1_state["lg_done"])
            wait("sync", "CT", ld_used[b])
            dma("sync", lgs[b][:], lg_sp[ch, :, :], s=f"LXL{b}")
            ldu = dma("sync", lus[b][:], lu_sp[ch, :, :], s=f"LXL{b}")
            wait("tensor", f"LXL{b}", ldu)
            mmg = 0
            for nn in range((T + 511) // 512):
                n0 = nn * 512
                nw = min(512, T - n0)
                pg = pD if nn % 2 == 0 else pA[:, 0:512]
                pu = pE if nn % 2 == 0 else pA[:, 512:1024]
                wait("tensor", "CV", tpn_par[nn % 2])
                for ccn in range(8):
                    opq("tensor", lambda e, b=b, ccn=ccn, n0=n0, nw=nw, pg=pg: e.matmul(
                        pg[:, 0:nw], lhsT=lgs[b][:, ccn * 128:(ccn + 1) * 128],
                        rhs=xq_fm[:, ccn * T + n0: ccn * T + n0 + nw],
                        start=(ccn == 0), stop=(ccn == 7)))
                for ccn in range(8):
                    def mku(e, b=b, ccn=ccn, n0=n0, nw=nw, pu=pu):
                        return e.matmul(
                            pu[:, 0:nw], lhsT=lus[b][:, ccn * 128:(ccn + 1) * 128],
                            rhs=xq_fm[:, ccn * T + n0: ccn * T + n0 + nw],
                            start=(ccn == 0), stop=(ccn == 7))
                    if ccn == 7:
                        mmg = op("tensor", "CT", mku)
                    else:
                        opq("tensor", mku)
                wait("scalar", "CT", mmg)
                sg = silu_g[nn % 2]
                sgn = op("scalar", "CS", lambda e, nw=nw, pg=pg, sg=sg: e.activation(
                    out=sg[:, 0:nw], in_=pg[:, 0:nw], func=ACT.Silu, scale=sx_b[:]))
                wait("vector", "CS", sgn)
                if nn == 0 and ch >= 2:
                    wait("vector", f"LGT{b}", tp_sp[b])
                tpn = op("vector", "CV", lambda e, b=b, n0=n0, nw=nw, pu=pu, sg=sg: e.scalar_tensor_tensor(
                    out=tpre[b][:, n0:n0 + nw], in0=pu[:, 0:nw], scalar=sx_b[:],
                    in1=sg[:, 0:nw], op0=ALU.mult, op1=ALU.mult))
                tpn_par[nn % 2] = tpn
            ld_used[b] = cnt["CT"]
            opw("vector", lambda e, b=b: e.reduce_max(out=red1[:], in_=tpre[b][:],
                                                      axis=AX.X, apply_absolute_value=True))
            tpv = op("vector", "CV", lambda e: e.tensor_tensor(out=amax_t[:], in0=amax_t[:],
                                                               in1=red1[:], op=ALU.max))
            wait("gpsimd", "CV", tpv)
            tp_sp[b] = dma("gpsimd", t_sp[ch, :, :], tpre[b][:], s=f"LGT{b}")
            tp_sp_of[ch] = tp_sp[b]
            w2_drive((ch + 1) * 8 // NIC + 1)

        opw("vector", lambda e: e.tensor_copy(out=red1[0:1, 0:1], in_=red1[0:1, 0:1]))
        am2 = op("vector", "CV", lambda e: e.tensor_copy(out=red1[:], in_=amax_t[:]))
        wait("gpsimd", "CV", am2)
        dma("gpsimd", mrow_sp, red1[:, 0:1], s="LGM")
        wait("gpsimd", "LGM", cnt["LGM"])
        g2 = dma("gpsimd", mrow[:], mrow_sp, s="LGM")
        wait("vector", "LGM", g2)
        opq("vector", lambda e: e.reduce_max(out=m11[:], in_=mrow[:], axis=AX.X))
        bc2 = op("vector", "CV",
                 lambda e: e.tensor_copy(out=gmax2[:], in_=m11[:].broadcast_to([1, 8])))
        wait("gpsimd", "CV", bc2)
        cci2 = dma("gpsimd", cc2_in.ap(), gmax2[:], s="LGC")
        wait("gpsimd", "LGC", cci2)
        cc2n = op("gpsimd", "CG", lambda e: e.collective_compute(
            "AllReduce", ALU.max, replica_groups=[list(range(NCORES))],
            ins=[cc2_in.ap().opt()], outs=[cc2_out.ap().opt()]))
        wait("gpsimd", "CG", cc2n)
        cco2 = dma("gpsimd", gmax2[:], cc2_out.ap(), s="LGC")
        wait("vector", "LGC", cco2)
        opw("vector", lambda e: e.tensor_tensor(out=sc_s[:, 4:5], in0=sc_s[:, 1:2],
                                                in1=gmax2[:, 0:1], op=ALU.mult))
        opw("vector", lambda e: e.tensor_scalar(out=sc_s[:, 4:5], in0=sc_s[:, 4:5],
                                                scalar1=1.0 / QMAX, scalar2=None, op0=ALU.mult))
        shv = op("vector", "CV", lambda e: e.reciprocal(out=sc_s[:, 5:6], in_=sc_s[:, 4:5]))
        wait("tensor", "CV", shv)
        bb2 = op("tensor", "CT", lambda e: e.matmul(pE[:, 6:8], lhsT=ones1f[:, :],
                                                    rhs=sc_s[:, 4:6], start=True, stop=True))
        wait("vector", "CT", bb2)
        opq("vector", lambda e: e.tensor_copy(out=sh_b[:], in_=pE[:, 6:7]))
        shb = op("vector", "CV", lambda e: e.tensor_copy(out=invsh_b[:], in_=pE[:, 7:8]))

        d2_cs_last = cnt["CS"]
        w2_drive(8)
        w2_flush()
        wait("tensor", "LTD", w2_state["wd_tr"])
        trd_free = [0, 0]
        mm_of = [0, 0]
        out_free = [0, 0]
        srd_free = [0, 0]
        ov_par = [0, 0]
        gn_prev = [d2_cs_last] * 4
        for s in range(nslabs):
            sl = slabs[s]
            ntt = len(tiles_of[s])
            mmd = 0
            for ch in range(NIC):
                tb = ch % 2
                wait("sync", "CV", trd_free[tb])
                wait("sync", f"LGT{ch % 2}", tp_sp_of[ch])
                ldt = dma("sync", trd[tb][:, 0:sl], t_sp[ch, :, s * SLAB:s * SLAB + sl], s=f"LXT{tb}")
                wait("scalar", f"LXT{tb}", ldt)
                if s == 0 and ch == 0:
                    wait("scalar", "CV", shb)
                un = op("scalar", "CS", lambda e, tb=tb, sl=sl: e.activation(
                    out=trd[tb][:, 0:sl], in_=trd[tb][:, 0:sl], func=ACT.Copy,
                    scale=invsh_b[:]))
                wait("vector", "CS", un)
                opw("vector", lambda e, tb=tb, sl=sl: e.tensor_scalar(
                    out=h_i8[:, 0:sl], in0=trd[tb][:, 0:sl], scalar1=-128.0,
                    scalar2=127.0, op0=ALU.max, op1=ALU.min))
                wait("vector", "CT", mm_of[tb])
                hq = op("vector", "CV", lambda e, tb=tb, sl=sl: e.tensor_copy(
                    out=h_cb[tb][:, 0:sl], in_=h_i8[:, 0:sl]))
                trd_free[tb] = hq
                wait("tensor", "CV", hq)
                nmm = 2 * ntt
                imm = 0
                for r in range(ntt):
                    qn = qts[s * 4 + r]
                    if ch == 0:
                        wait("tensor", "CS", gn_prev[r])
                    for half in range(2):
                        if ntt >= 3 and r == 2:
                            pdst = (pD if half == 0 else pE)[0:qn, 0:512]
                        else:
                            pt = (pC, pA, None, pB)[r]
                            pdst = pt[0:qn, half * 512:half * 512 + 512]
                        imm += 1
                        def mkd(e, ch=ch, r=r, qn=qn, tb=tb, pdst=pdst, half=half):
                            return e.matmul(
                                pdst,
                                lhsT=h_cb[tb][:, r * QT: r * QT + qn],
                                rhs=rhs_d[:, ch * C + half * 512:ch * C + half * 512 + 512],
                                start=(ch == 0), stop=(ch == NIC - 1))
                        if imm == nmm:
                            mmd = op("tensor", "CT", mkd)
                        else:
                            opq("tensor", mkd)
                mm_of[tb] = mmd if mmd else mm_of[tb]
            for r in range(ntt):
                jj = s * 4 + r
                qn = qts[jj]
                wait("scalar", "CT", mmd)
                wait("scalar", "CV", ov_par[jj % 2])
                gb = g_t[jj % 2]
                if ntt >= 3 and r == 2:
                    opq("scalar", lambda e, qn=qn, gb=gb: e.activation(
                        out=gb[0:qn, 0:512], in_=pD[0:qn, :], func=ACT.Silu,
                        scale=sh_b[0:qn, :]))
                    gn = op("scalar", "CS", lambda e, qn=qn, gb=gb: e.activation(
                        out=gb[0:qn, 512:1024], in_=pE[0:qn, :], func=ACT.Silu,
                        scale=sh_b[0:qn, :]))
                else:
                    pdn = (pC, pA, None, pB)[r]
                    gn = op("scalar", "CS", lambda e, qn=qn, pdn=pdn, gb=gb: e.activation(
                        out=gb[0:qn, :], in_=pdn[0:qn, :], func=ACT.Silu,
                        scale=sh_b[0:qn, :]))
                gn_prev[r] = gn
                rb = r % 2
                wait("scalar", "CV", srd_free[rb])
                dma("scalar", s_rd[rb][0:qn, :], s_sp[jj * QT:jj * QT + qn, :], s=f"LXF{rb}")
                ldx2 = dma("scalar", x_rd[rb][0:qn, :], xh[2 + jj * QT:2 + jj * QT + qn, :], s=f"LXF{rb}")
                wait("vector", "CS", gn)
                wait("vector", f"ST{rb}", out_free[rb])
                wait("vector", f"LXF{rb}", ldx2)
                opw("vector", lambda e, rb=rb, qn=qn, gb=gb: e.tensor_tensor(
                    out=out_t[rb][0:qn, :], in0=s_rd[rb][0:qn, :], in1=gb[0:qn, :],
                    op=ALU.mult))
                ov = op("vector", "CV", lambda e, rb=rb, qn=qn: e.tensor_tensor(
                    out=out_t[rb][0:qn, :], in0=out_t[rb][0:qn, :], in1=x_rd[rb][0:qn, :],
                    op=ALU.add))
                srd_free[rb] = ov
                ov_par[jj % 2] = ov
                wait("gpsimd", "CV", ov)
                out_free[rb] = dma("gpsimd", out[jj * QT:jj * QT + qn, :],
                                   out_t[rb][0:qn, :], s=f"ST{rb}")

        wait("sync", "ST0", cnt.get("ST0", 0))
        wait("sync", "ST1", cnt.get("ST1", 0))

        E(nc.allow_low_precision("bf16 softmax weights / quantized operands are sanctioned for this fp32 reference (bf16 matmul + fp32 accum)"))
        block = E(nc.Block())

        @block.sync
        def _(e):
            for f in prog["sync"]:
                f(e)

        @block.vector
        def _(e):
            for f in prog["vector"]:
                f(e)

        @block.scalar
        def _(e):
            for f in prog["scalar"]:
                f(e)

        @block.tensor
        def _(e):
            for f in prog["tensor"]:
                f(e)

        @block.gpsimd
        def _(e):
            for f in prog["gpsimd"]:
                f(e)

    return nc


def make_masks():
    k = np.arange(128)[:, None]
    q = np.arange(QT)[None, :]
    d = k - q - 2
    even = (q % 2) == 0
    m1 = np.where(even, (d >= 0) & (d <= 3), (d >= -1) & (d <= 2))
    m2 = np.where(even, (d >= -2) & (d <= 1), (d >= -3) & (d <= 0))

    def rep(m):
        a = np.full((128, 1024), NEG, np.float32)
        for h in range(HEADS):
            a[:, h * 128:h * 128 + QT] = np.where(m, 0.0, NEG)
        return a
    return rep(m1), rep(m2)


def kernel(L0_attn_x, L0_attn_w_qkv, L0_attn_w_out, L0_attn_b_out,
           L0_ffn_clip_x, L0_ffn_clip_down, L0_ffn_w_gate, L0_ffn_w_up,
           L0_ffn_w_down):
    x = np.ascontiguousarray(np.asarray(L0_attn_x, dtype=np.float32)[0])
    L = x.shape[0]
    T = L // NCORES
    nfull, rem = T // QT, T % QT
    ntiles = nfull + (1 if rem else 0)
    nxt = (T + 4 + 127) // 128
    XROWS = nxt * 128

    wqkv = np.ascontiguousarray(np.asarray(L0_attn_w_qkv, np.float32))
    woutw = np.ascontiguousarray(np.asarray(L0_attn_w_out, np.float32))
    boutv = np.ascontiguousarray(np.asarray(L0_attn_b_out, np.float32)).reshape(1, C)
    wgv = np.ascontiguousarray(np.asarray(L0_ffn_w_gate, np.float32))
    wuv = np.ascontiguousarray(np.asarray(L0_ffn_w_up, np.float32))
    wdv = np.ascontiguousarray(np.asarray(L0_ffn_w_down, np.float32))
    clip2 = np.zeros((1, 8), np.float32)
    clip2[0, 0] = np.asarray(L0_ffn_clip_x, np.float32).reshape(-1)[0]
    clip2[0, 1] = np.asarray(L0_ffn_clip_down, np.float32).reshape(-1)[0]

    mask1, mask2 = make_masks()
    lastq = rem if rem else QT
    mask2_first = mask2.copy()
    for h in range(HEADS):
        mask2_first[:, h * 128:h * 128 + 2] = NEG
    mask1_last = mask1.copy()
    for h in range(HEADS):
        mask1_last[:, h * 128 + lastq - 2:h * 128 + lastq] = NEG

    bmul_base = np.zeros((1, ntiles * QT), np.float32)
    for j in range(ntiles):
        qn = min(QT, T - j * QT)
        bmul_base[0, j * QT:j * QT + qn] = 2.0

    in_maps = []
    for c in range(NCORES):
        t0 = c * T
        xa = np.zeros((XROWS, C), np.float32)
        lo, hi = t0 - 2, t0 + T + 2
        src_lo, src_hi = max(0, lo), min(L, hi)
        xa[src_lo - lo: src_hi - lo] = x[src_lo:src_hi]
        bm = bmul_base.copy()
        m1zv, m2zv = mask1, mask2
        if c == 0:
            bm[0, 0:2] = 1.0
            m2zv = mask2_first
        if c == NCORES - 1:
            bm[0, (ntiles - 1) * QT + lastq - 2:(ntiles - 1) * QT + lastq] = 1.0
            m1zv = mask1_last
        in_maps.append(dict(
            xh=xa, wqkv=wqkv, wout=woutw, bout=boutv, clip2=clip2,
            wg=wgv, wu=wuv, wd=wdv,
            mask1a=mask1, mask2a=mask2, mask1z=m1zv, mask2z=m2zv, bmul=bm))

    nc = build_nc(T)
    res = run_bass_kernel_spmd(nc, in_maps, core_ids=list(range(NCORES)))
    if DEBUG_SPILLS:
        DEBUG_RES.clear()
        DEBUG_RES.append(res)
    outs = [np.asarray(res.results[c]["out"], np.float32) for c in range(NCORES)]
    return np.concatenate(outs, axis=0)[None, :, :]


# version 18 (final)
# speedup vs baseline: 1.0848x; 1.0194x over previous
import functools
import operator
import numpy as np
import ml_dtypes
from contextlib import ExitStack

import concourse.bass as bass
import concourse.mybir as mybir
from concourse.bass_utils import run_bass_kernel_spmd

F32 = mybir.dt.float32
BF16 = mybir.dt.bfloat16
I8 = mybir.dt.int8
U8 = mybir.dt.uint8
ALU = mybir.AluOpType
ACT = mybir.ActivationFunctionType
AX = mybir.AxisListType

NCORES = 8
C = 1024
I3 = 3072
II = 2816
NIC = II // 128
HEADS = 8
QMAX = 127.0
EPS_LN = 1e-5
NEG = -30000.0
ZEPS = 1e-30
QT = 124
DEBUG_SPILLS = False
DEBUG_RES = []


def build_nc(T):
    nfull, rem = T // QT, T % QT
    ntiles = nfull + (1 if rem else 0)
    qts = [QT] * nfull + ([rem] if rem else [])
    SLAB = 4 * QT
    nslab_full, srem = T // SLAB, T % SLAB
    slabs = [SLAB] * nslab_full + ([srem] if srem else [])
    nslabs = len(slabs)
    sw = [slabs[s] + 4 for s in range(nslabs)]
    last_s = nslabs - 1
    sw[last_s] = max(sw[last_s], (ntiles - 1) * QT - SLAB * last_s + 128)
    SLOTW = max(sw)
    nxt = (T + 4 + 127) // 128
    XROWS = nxt * 128
    FRW = XROWS
    NTT = T // 128
    assert T % 128 == 0
    tiles_of = [[] for _ in range(nslabs)]
    for j in range(ntiles):
        tiles_of[min(j // 4, nslabs - 1)].append(j)

    nc = bass.Bass("TRN2", target_bir_lowering=False, detect_race_conditions=False)

    ap = lambda *a, **k: nc.dram_tensor(*a, **k).ap()
    xh = ap("xh", [XROWS, C], F32, kind="ExternalInput")
    wqkv = ap("wqkv", [C, I3], F32, kind="ExternalInput")
    wout = ap("wout", [C, C], F32, kind="ExternalInput")
    bout = ap("bout", [1, C], F32, kind="ExternalInput")
    clip2 = ap("clip2", [1, 8], F32, kind="ExternalInput")
    wg = ap("wg", [II, C], F32, kind="ExternalInput")
    wu = ap("wu", [II, C], F32, kind="ExternalInput")
    wd = ap("wd", [C, II], F32, kind="ExternalInput")
    mask1a_d = ap("mask1a", [128, 1024], F32, kind="ExternalInput")
    mask2a_d = ap("mask2a", [128, 1024], F32, kind="ExternalInput")
    mask1z_d = ap("mask1z", [128, 1024], F32, kind="ExternalInput")
    mask2z_d = ap("mask2z", [128, 1024], F32, kind="ExternalInput")
    bmul_d = ap("bmul", [1, ntiles * QT], F32, kind="ExternalInput")
    out = ap("out", [T, C], F32, kind="ExternalOutput")

    spk = dict(kind="ExternalOutput") if DEBUG_SPILLS else {}
    xn_sp = ap("xn_sp", [XROWS, C], F32, **spk)
    s_sp = ap("s_sp", [ntiles * QT, C], BF16, **spk)
    t_sp = ap("t_sp", [NIC, 128, T], F32, **spk)
    lg_sp = ap("lg_sp", [NIC, 128, 8 * 128], BF16)
    lu_sp = ap("lu_sp", [NIC, 128, 8 * 128], BF16)
    mrow_sp = ap("mrow_sp", [1, 128], F32)
    cc1_in = nc.dram_tensor("cc1_in", [1, 8], F32)
    cc1_out = nc.dram_tensor("cc1_out", [1, 8], F32)
    cc2_in = nc.dram_tensor("cc2_in", [1, 8], F32)
    cc2_out = nc.dram_tensor("cc2_out", [1, 8], F32)

    prog = {e: [] for e in ("sync", "vector", "scalar", "tensor", "gpsimd")}
    cnt = {}

    def inc(s, n=1):
        cnt[s] = cnt.get(s, 0) + n
        return cnt[s]

    with ExitStack() as stk:
        E = stk.enter_context

        R1B, R2B, R3B, R4B = 83968, 48000, 47104, 22528
        arena = E(nc.sbuf_tensor("arena", [128, R1B + R2B + R3B + R4B], U8))
        base = nc.lookup_mloc(arena).addr
        OFF = {"R1": 0, "R2": R1B, "R3": R1B + R2B, "R4": R1B + R2B + R3B}
        SZ = {"R1": R1B, "R2": R2B, "R3": R3B, "R4": R4B}
        _n = [0]

        def ovl(region, off, shape, dtype):
            _n[0] += 1
            name = f"ov{_n[0]}"
            hnd = nc.virtual_sbuf_tensor(name, shape, dtype)
            nc._tensor(hnd.name, hnd.shape, hnd.dtype, type="SB")
            mloc = nc.lookup_mloc(hnd)
            mloc.allocated = True
            mloc.addr = base + OFF[region] + off
            nb = functools.reduce(operator.mul, shape[1:], 1) * mybir.dt.size(dtype)
            assert off % 32 == 0 and off + nb <= SZ[region], (region, off, nb)
            return hnd

        wqkv_b = ovl("R1", 0, [128, 8 * I3], BF16)
        xn_fm = ovl("R1", 49152, [128, 8 * FRW], BF16)
        wstg_gu = [ovl("R4", 8192 + i * 4096, [128, C], F32) for i in range(2)]
        wnb_gu = [ovl("R4", 16384 + i * 2048, [128, C], BF16) for i in range(2)]
        wi8_gu = [ovl("R4", 20480 + i * 1024, [128, C], I8) for i in range(2)]
        lhs_stg = ovl("R3", 20480, [128, 8 * 128], BF16)
        xq_fm = ovl("R1", 0, [128, 8 * T], BF16)
        lgs = [ovl("R1", 32768 + i * 2048, [128, 8 * 128], BF16) for i in range(2)]
        lus = [ovl("R1", 36864 + i * 2048, [128, 8 * 128], BF16) for i in range(2)]
        tpre = [ovl("R1", 40960 + i * (T * 4), [128, T], F32) for i in range(2)]
        qk_r = [ovl("R2", i * 16 * SLOTW * 2, [128, 16 * SLOTW], BF16) for i in range(2)]
        vf_r = [ovl("R2", 64 * SLOTW + i * 16 * SLOTW, [128, 8 * SLOTW], BF16)
                for i in range(2)]
        wstg_d = ovl("R2", 0, [128, II], F32)
        wi8_d = ovl("R2", 11264, [128, II], I8)
        wnb_d = [ovl("R2", 25376 + i * 5632, [128, II], BF16) for i in range(2)]
        h_cb = [ovl("R2", 21856 + i * 1024, [128, SLAB], BF16) for i in range(2)]
        trd = [ovl("R2", 43712 + i * 2016, [128, SLAB], F32) for i in range(2)]
        wout_b = ovl("R3", 0, [128, 8 * C], BF16)
        e1 = ovl("R3", 16384, [128, 1024], BF16)
        e2 = ovl("R3", 18432, [128, 1024], BF16)
        zb1 = ovl("R3", 20480, [128, 1024], BF16)
        zb2 = ovl("R3", 22528, [128, 1024], BF16)
        r1 = ovl("R3", 24576, [128, 1024], BF16)
        r2 = ovl("R3", 26624, [128, 1024], BF16)
        ww = ovl("R3", 28672, [128, 1024], BF16)
        o_sb = ovl("R3", 30720, [128, 1024], BF16)
        s_t = ovl("R3", 32768, [128, 1024], BF16)
        v_tm = [ovl("R3", 34816 + i * 2048, [128, 1024], BF16) for i in range(2)]
        m1a = ovl("R3", 38912, [128, 1024], BF16)
        m2a = ovl("R3", 40960, [128, 1024], BF16)
        m1z = ovl("R3", 43008, [128, 1024], BF16)
        m2z = ovl("R3", 45056, [128, 1024], BF16)
        rhs_d = ovl("R3", 0, [128, NIC * C], BF16)
        x_t = [ovl("R4", i * 4096, [128, C], F32) for i in range(2)]
        xn_t = [ovl("R4", 8192 + i * 4096, [128, C], F32) for i in range(2)]
        xnb_t = [ovl("R4", 16384 + i * 2048, [128, C], BF16) for i in range(2)]
        a1 = ovl("R4", 0, [128, 1024], F32)
        a2 = ovl("R4", 4096, [128, 1024], F32)
        xq_t = [ovl("R4", i * 2048, [128, C], BF16) for i in range(2)]
        xq_i8 = ovl("R4", 4096, [128, C], I8)
        uq = ovl("R4", 5120, [128, C], F32)
        silu_g = [ovl("R4", 5120 + i * 2048, [128, 512], F32) for i in range(2)]
        xrd_d = [ovl("R4", 9216 + i * 4096, [128, C], F32) for i in range(2)]
        out_t = [ovl("R4", i * 4096, [128, C], F32) for i in range(2)]
        x_rd = [ovl("R4", 8192 + i * 4096, [128, C], F32) for i in range(2)]
        g_t = [ovl("R4", 16384, [128, C], BF16), ovl("R3", 45056, [128, C], BF16)]
        s_rd = [ovl("R4", 18432 + i * 2048, [128, C], BF16) for i in range(2)]

        sb = lambda name, shape, dt: E(nc.sbuf_tensor(name, shape, dt))
        bmul_s = sb("bmul_s", [1, ntiles * QT], BF16)
        bout_b = sb("bout_b", [1, C], BF16)
        clip_s = sb("clip_s", [1, 8], F32)
        onesK = sb("onesK", [128, 128], BF16)
        ones1f = sb("ones1f", [1, 128], F32)
        epsrow = sb("epsrow", [1, 512], F32)
        ln_s = sb("ln_s", [128, 8], F32)
        amax_x = sb("amax_x", [128, 1], F32)
        amax_t = sb("amax_t", [128, 1], F32)
        red1 = sb("red1", [128, 1], F32)
        mrow = sb("mrow", [1, 128], F32)
        m11 = sb("m11", [1, 1], F32)
        mbias = sb("mbias", [128, 1], F32)
        gmax1 = sb("gmax1", [1, 8], F32)
        gmax2 = sb("gmax2", [1, 8], F32)
        sc_s = sb("sc_s", [1, 8], F32)
        sx_b = sb("sx_b", [128, 1], F32)
        invsx_b = sb("invsx_b", [128, 1], F32)
        sh_b = sb("sh_b", [128, 1], F32)
        invsh_b = sb("invsh_b", [128, 1], F32)
        winv = sb("winv", [128, 8], F32)
        h_i8 = sb("h_i8", [128, SLAB], I8)
        epsb = sb("epsb", [128, 1], F32)
        cneg_b = sb("cneg_b", [128, 1], F32)

        pA = E(nc.psum_tensor("pA", [128, 1024], F32))
        pB = E(nc.psum_tensor("pB", [128, 1024], F32))
        pC = E(nc.psum_tensor("pC", [128, 1024], F32))
        pD = E(nc.psum_tensor("pD", [128, 512], F32))
        pE = E(nc.psum_tensor("pE", [128, 512], F32))

        sem = {}

        def S(n):
            if n not in sem:
                sem[n] = E(nc.semaphore(n))
            return sem[n]

        for _nm in ("CV", "CS", "CT", "CG"):
            S(_nm)

        def emit(eng, f):
            prog[eng].append(f)

        def dma(eng, out_ap, in_ap, s, transpose=False):
            S(s)
            n = inc(s, 16)
            def f(e, out_ap=out_ap, in_ap=in_ap, s=s, transpose=transpose):
                if transpose:
                    e.dma_start(out=out_ap, in_=in_ap, transpose=True).then_inc(sem[s], 16)
                else:
                    e.dma_start(out=out_ap, in_=in_ap).then_inc(sem[s], 16)
            emit(eng, f)
            return n

        def wait(eng, s, val):
            S(s)
            if val > 0:
                emit(eng, lambda e, s=s, val=val: e.wait_ge(sem[s], val))

        ENG_SEM = {"vector": "CV", "scalar": "CS", "tensor": "CT", "gpsimd": "CG"}

        def op(eng, s, fmaker):
            assert s == ENG_SEM[eng]
            if eng in ("vector", "scalar"):
                prev = cnt.get(s, 0)
                if prev > 0:
                    emit(eng, lambda e, s=s, prev=prev: e.wait_ge(sem[s], prev))
            n = inc(s)
            emit(eng, lambda e, fmaker=fmaker, s=s: fmaker(e).then_inc(sem[s], 1))
            return n

        def opq(eng, fmaker):
            if eng in ("vector", "scalar"):
                op(eng, ENG_SEM[eng], fmaker)
            else:
                emit(eng, lambda e, fmaker=fmaker: fmaker(e))

        def opw(eng, fmaker):
            return op(eng, ENG_SEM[eng], fmaker)

        st3 = lambda t, qn: t[:, :].rearrange("p (h q) -> p h q", h=8)[:, :, 0:qn]

        dma("sync", clip_s[:], clip2, s="LC")
        lcg_of = [0] * 24
        wqkv_b3 = wqkv_b[:, :].rearrange("p (c n) -> p c n", c=8)
        wqkv_s3 = wqkv.rearrange("(c p) n -> p c n", p=128)

        def ld_wqkv_chunk(jj):
            lcg_of[jj] = dma("gpsimd", wqkv_b3[:, :, jj * 128:(jj + 1) * 128],
                             wqkv_s3[:, :, jj * 128:(jj + 1) * 128], s="LCG")

        for jj in range(6):
            ld_wqkv_chunk(jj)
        for (dst, srcd) in ((m1a, mask1a_d), (m2a, mask2a_d), (m1z, mask1z_d),
                            (m2z, mask2z_d)):
            dma("gpsimd", dst[:], srcd, s="LCG")
        ld_masks = cnt["LCG"]
        dma("gpsimd", bmul_s[:], bmul_d, s="LCG")
        dma("gpsimd", bout_b[:], bout, s="LCG")
        for jj in range(6, 10):
            ld_wqkv_chunk(jj)
        for ccn in range(8):
            dma("gpsimd", wout_b[:, ccn * C:(ccn + 1) * C],
                wout[ccn * 128:(ccn + 1) * 128, :], s="LCG")
        ld_wout = cnt["LCG"]
        for jj in range(10, 24):
            ld_wqkv_chunk(jj)
        ld_wq = cnt["LCG"]
        ld_const = cnt["LC"]

        wait("vector", "LC", ld_const)
        opq("vector", lambda e: e.memset(onesK[:], 1.0))
        opq("vector", lambda e: e.memset(ones1f[:], 1.0))
        opq("vector", lambda e: e.memset(epsrow[:], ZEPS))
        opq("vector", lambda e: e.memset(amax_x[:], 0.0))
        opq("vector", lambda e: e.memset(amax_t[:], 0.0))
        opq("vector", lambda e: e.memset(epsb[:], EPS_LN))
        opq("vector", lambda e: e.memset(cneg_b[:], -128.0))
        opq("vector", lambda e: e.memset(pA[:, :], 0.0))
        pa_init = op("vector", "CV", lambda e: e.memset(pB[:, :], 0.0))

        tr_done = [0] * nxt
        sp_done = [0] * nxt
        xa_done = [0] * nxt
        for i in range(nxt):
            b = i % 2
            if i >= 2:
                wait("scalar", "CV", xa_done[i - 2])
            ldx = dma("scalar", x_t[b][:], xh[i * 128:(i + 1) * 128, :], s=f"LXX{b}")
            wait("vector", f"LXX{b}", ldx)
            if i >= 2:
                wait("vector", f"LGX{b}", sp_done[i - 2])
            opw("vector", lambda e, b=b: e.reduce_sum(out=ln_s[:, 0:1], in_=x_t[b][:], axis=AX.X))
            wait("scalar", f"LXX{b}", ldx)
            if i >= 2:
                wait("scalar", f"LTA{b}", tr_done[i - 2])
            sq = op("scalar", "CS", lambda e, b=b: e.activation(
                out=xnb_t[b][:], in_=x_t[b][:], func=ACT.Square, accum_out=ln_s[:, 1:2]))
            wait("vector", "CS", sq)
            opw("vector", lambda e: e.tensor_scalar(out=ln_s[:, 2:3], in0=ln_s[:, 0:1],
                                                    scalar1=1.0 / C, scalar2=None, op0=ALU.mult))
            opq("vector", lambda e: e.tensor_scalar(out=ln_s[:, 3:4], in0=ln_s[:, 1:2],
                                                    scalar1=1.0 / C, scalar2=None, op0=ALU.mult))
            opw("vector", lambda e: e.tensor_tensor(out=ln_s[:, 4:5], in0=ln_s[:, 2:3],
                                                    in1=ln_s[:, 2:3], op=ALU.mult))
            vv = op("vector", "CV", lambda e: e.tensor_tensor(out=ln_s[:, 4:5], in0=ln_s[:, 3:4],
                                                              in1=ln_s[:, 4:5], op=ALU.subtract))
            wait("scalar", "CV", vv)
            sq2 = op("scalar", "CS", lambda e: e.activation(out=ln_s[:, 5:6], in_=ln_s[:, 4:5],
                                                            func=ACT.Sqrt, bias=epsb[:]))
            wait("vector", "CS", sq2)
            opw("vector", lambda e: e.reciprocal(out=ln_s[:, 6:7], in_=ln_s[:, 5:6]))
            opw("vector", lambda e, b=b: e.scalar_tensor_tensor(
                out=xn_t[b][:], in0=x_t[b][:], scalar=ln_s[:, 2:3],
                in1=ln_s[:, 6:7].broadcast_to([128, C]),
                op0=ALU.subtract, op1=ALU.mult))
            opw("vector", lambda e, b=b: e.reduce_max(out=red1[:], in_=xn_t[b][:], axis=AX.X,
                                                      apply_absolute_value=True))
            opq("vector", lambda e: e.tensor_tensor(out=amax_x[:], in0=amax_x[:],
                                                    in1=red1[:], op=ALU.max))
            xa_done[i] = op("vector", "CV",
                            lambda e, b=b: e.tensor_copy(out=xnb_t[b][:], in_=xn_t[b][:]))
            wait("gpsimd", "CV", xa_done[i])
            sp_done[i] = dma("gpsimd", xn_sp[i * 128:(i + 1) * 128, :], xn_t[b][:], s=f"LGX{b}")
            wait("sync", "CV", xa_done[i])
            tr_done[i] = dma("sync",
                             xn_fm[:, :].rearrange("p (c f) -> p c f", c=8)[:, :, i * 128:(i + 1) * 128],
                             xnb_t[b][:, :].rearrange("p (c q) -> p c q", c=8),
                             s=f"LTA{b}", transpose=True)
        opw("vector", lambda e: e.tensor_copy(out=red1[0:1, 0:1], in_=red1[0:1, 0:1]))
        amx = op("vector", "CV", lambda e: e.tensor_copy(out=red1[:], in_=amax_x[:]))
        wait("gpsimd", "CV", amx)
        dma("gpsimd", mrow_sp, red1[:, 0:1], s="LGM")
        wait("gpsimd", "LGM", cnt["LGM"])
        g1 = dma("gpsimd", mrow[:], mrow_sp, s="LGM")
        wait("vector", "LGM", g1)
        opq("vector", lambda e: e.reduce_max(out=m11[:], in_=mrow[:], axis=AX.X))
        bc1 = op("vector", "CV",
                 lambda e: e.tensor_copy(out=gmax1[:], in_=m11[:].broadcast_to([1, 8])))
        wait("gpsimd", "CV", bc1)
        cci = dma("gpsimd", cc1_in.ap(), gmax1[:], s="LGC")
        wait("gpsimd", "LGC", cci)
        cc1n = op("gpsimd", "CG", lambda e: e.collective_compute(
            "AllReduce", ALU.max, replica_groups=[list(range(NCORES))],
            ins=[cc1_in.ap().opt()], outs=[cc1_out.ap().opt()]))
        wait("gpsimd", "CG", cc1n)
        cco1 = dma("gpsimd", gmax1[:], cc1_out.ap(), s="LGC")

        w1_state = {"stg_free": 0, "i8_free": 0, "lg_done": 0, "emitted": 0,
                    "trn_b": [0, 0], "ld_of": [0, 0], "stt_of": [0, 0],
                    "sc_of": [0, 0], "pend": None}

        def w1_flush():
            if w1_state["pend"] is None:
                return
            sc_p, b_p, dst_sp_p, ch_p = w1_state["pend"]
            w1_state["pend"] = None
            wait("sync", "CS", sc_p)
            wait("sync", "LGL", w1_state["lg_done"])
            trn = dma("sync", lhs_stg[:, :].rearrange("p (c q) -> p c q", c=8),
                      wnb_gu[b_p][:, :].rearrange("p (c q) -> p c q", c=8),
                      s="LTW", transpose=True)
            w1_state["trn_b"][b_p] = trn
            wait("sync", "LTW", trn)
            w1_state["lg_done"] = dma("sync", dst_sp_p[ch_p, :, :], lhs_stg[:, :], s="LGL")
        NW1 = 2 * NIC

        def w1_src(idx):
            src_ap = wg if idx < NIC else wu
            ch = idx % NIC
            return src_ap[ch * 128:(ch + 1) * 128, :]

        def emit_w1_chunk(idx):
            w1_flush()
            dst_sp = lg_sp if idx < NIC else lu_sp
            ch = idx % NIC
            b = idx % 2
            o4 = b * 4
            if idx == 0:
                w1_state["ld_of"][0] = dma("sync", wstg_gu[0][:], w1_src(0), s="LXW0")
            if idx + 1 < NW1:
                nb = (idx + 1) % 2
                wait("sync", "CV", w1_state["stt_of"][nb])
                w1_state["ld_of"][nb] = dma("sync", wstg_gu[nb][:], w1_src(idx + 1), s="LXW0")
            wait("vector", "LXW0", w1_state["ld_of"][b])
            wait("vector", "CS", w1_state["sc_of"][b])
            opw("vector", lambda e, o4=o4, b=b: e.reduce_max(
                out=winv[:, o4:o4 + 1], in_=wstg_gu[b][:], axis=AX.X,
                apply_absolute_value=True))
            opq("vector", lambda e, o4=o4: e.tensor_scalar(
                out=winv[:, o4 + 1:o4 + 2], in0=winv[:, o4:o4 + 1],
                scalar1=1.0 / QMAX, scalar2=None, op0=ALU.mult))
            opw("vector", lambda e, o4=o4: e.reciprocal(
                out=winv[:, o4 + 2:o4 + 3], in_=winv[:, o4:o4 + 1]))
            opw("vector", lambda e, o4=o4: e.tensor_scalar(
                out=winv[:, o4 + 2:o4 + 3], in0=winv[:, o4 + 2:o4 + 3],
                scalar1=QMAX * (1.0 - 2.0 ** -22), scalar2=None, op0=ALU.mult))
            i8n = op("vector", "CV", lambda e, o4=o4, b=b: e.scalar_tensor_tensor(
                out=wi8_gu[b][:], in0=wstg_gu[b][:], scalar=winv[:, o4 + 2:o4 + 3],
                in1=wstg_gu[b][:], op0=ALU.mult, op1=ALU.bypass))
            w1_state["i8_free"] = i8n
            w1_state["stt_of"][b] = i8n
            wait("scalar", "CV", i8n)
            wait("scalar", "LTW", w1_state["trn_b"][b])
            sc = op("scalar", "CS", lambda e, b=b, o4=o4: e.activation(
                out=wnb_gu[b][:], in_=wi8_gu[b][:], func=ACT.Copy,
                scale=winv[:, o4 + 1:o4 + 2]))
            w1_state["sc_of"][b] = sc
            w1_state["pend"] = (sc, b, dst_sp, ch)

        def w1_drive(frac):
            want = min(NW1, int(NW1 * frac + 0.999))
            while w1_state["emitted"] < want:
                emit_w1_chunk(w1_state["emitted"])
                w1_state["emitted"] += 1

        wait("tensor", "CV", pa_init)
        wait("sync", "LGX0", sp_done[(nxt - 2) - (nxt % 2 == 1)] if nxt >= 2 else 0)
        wait("sync", "LGX0", max(sp_done[i] for i in range(0, nxt, 2)))
        wait("sync", "LGX1", max(sp_done[i] for i in range(1, nxt, 2)))
        wait("sync", "LTA0", max(tr_done[i] for i in range(0, nxt, 2)))
        wait("sync", "LTA1", max(tr_done[i] for i in range(1, nxt, 2)))
        qkv_done = [0] * nslabs
        slab_av = [0] * nslabs
        oev_prev = 0
        sev_prev = 0
        ssp_prev = 0
        vtm_av = [0] * ntiles
        vt_of = [0] * ntiles
        gather_of = [0] * ntiles
        aan_of = [0] * (ntiles + 1)
        ph1_done = [False] * ntiles
        mred_prev = [0]

        qkv_ev = [0, 0]
        qkv_kk = [0]
        qkv_em = [0] * nslabs

        def emit_qkv_groups(s, ngroups):
            if ngroups <= 0 or qkv_em[s] >= 24:
                return
            wdt = sw[s]
            basec = s * SLAB
            rslot = s % 2
            if qkv_em[s] == 0:
                need_tile = min(nxt - 1, (basec + wdt - 1) // 128)
                wait("tensor", f"LTA{need_tile % 2}", tr_done[need_tile])
            wait("tensor", "CS", sev_prev)
            while ngroups > 0 and qkv_em[s] < 24:
                j = qkv_em[s]
                kk = qkv_kk[0]
                pq = pD if kk % 2 == 0 else pE
                wait("tensor", "LCG", lcg_of[j])
                wait("tensor", "CV", qkv_ev[kk % 2])
                mmn = 0
                for ccn in range(8):
                    def mk(e, j=j, ccn=ccn, basec=basec, wdt=wdt, pq=pq):
                        return e.matmul(
                            pq[:, 0:wdt],
                            lhsT=wqkv_b[:, ccn * I3 + j * 128:ccn * I3 + (j + 1) * 128],
                            rhs=xn_fm[:, ccn * FRW + basec: ccn * FRW + basec + wdt],
                            start=(ccn == 0), stop=(ccn == 7))
                    if ccn == 7:
                        mmn = op("tensor", "CT", mk)
                    else:
                        opq("tensor", mk)
                wait("vector", "CT", mmn)
                if j == 0 and s >= 2:
                    wait("vector", "CT", slab_av[s - 2])
                if j < 8:
                    def ev(e, j=j, rslot=rslot, wdt=wdt, pq=pq):
                        return e.tensor_scalar(
                            out=qk_r[rslot][:, j * SLOTW:j * SLOTW + wdt],
                            in0=pq[:, 0:wdt], scalar1=1.0 / 32.0, scalar2=None,
                            op0=ALU.mult)
                elif j < 16:
                    def ev(e, j=j, rslot=rslot, wdt=wdt, pq=pq):
                        return e.tensor_copy(
                            out=qk_r[rslot][:, j * SLOTW:j * SLOTW + wdt],
                            in_=pq[:, 0:wdt])
                else:
                    def ev(e, j=j, rslot=rslot, wdt=wdt, pq=pq):
                        return e.tensor_copy(
                            out=vf_r[rslot][:, (j - 16) * SLOTW:(j - 16) * SLOTW + wdt],
                            in_=pq[:, 0:wdt])
                qkv_ev[kk % 2] = op("vector", "CV", ev)
                qkv_kk[0] += 1
                qkv_em[s] += 1
                ngroups -= 1
            if qkv_em[s] == 24 and qkv_done[s] == 0:
                qkv_done[s] = cnt["CV"]

        def emit_phase1(j, s, rslot, qn, kbase):
            stX = pA if j % 2 == 0 else pB
            wait("tensor", "CV", qkv_done[s])
            stn = 0
            for h in range(HEADS):
                def mk(e, h=h, rslot=rslot, kbase=kbase, qn=qn, stX=stX):
                    return e.matmul(
                        stX[:, h * 128:h * 128 + qn],
                        lhsT=qk_r[rslot][:, (8 + h) * SLOTW + kbase:(8 + h) * SLOTW + kbase + 128],
                        rhs=qk_r[rslot][:, h * SLOTW + kbase + 2: h * SLOTW + kbase + 2 + qn],
                        start=True, stop=True)
                if h == HEADS - 1:
                    stn = op("tensor", "CT", mk)
                else:
                    opq("tensor", mk)
            wait("vector", "CT", stn)
            if j >= 1:
                wait("vector", "LGM", gather_of[j - 1])
            mred = op("vector", "CV", lambda e, qn=qn, stX=stX: e.reduce_max(
                out=red1[:], in_=st3(stX, qn), axis=AX.XY, negate=True))
            wait("gpsimd", "CV", mred)
            dma("gpsimd", mrow_sp, red1[:, 0:1], s="LGM")
            wait("gpsimd", "LGM", cnt["LGM"])
            gather_of[j] = dma("gpsimd", mrow[:], mrow_sp, s="LGM")

        emit_qkv_groups(0, 24)
        if nslabs > 1:
            emit_qkv_groups(1, 10)
        for s in range(nslabs):
            rslot = s % 2
            for j in tiles_of[s]:
                qn = qts[j]
                kbase = QT * j - s * SLAB
                vb = j % 2
                if j >= 2:
                    wait("sync", "CT", vtm_av[j - 2])
                wait("sync", "CV", qkv_done[s])
                vt = 0
                for ccn in range(8):
                    vt = dma("sync", v_tm[vb][:, ccn * 128:(ccn + 1) * 128],
                             vf_r[rslot][:, ccn * SLOTW + kbase: ccn * SLOTW + kbase + 128],
                             s=f"LTV{vb}", transpose=True)
                vt_of[j] = vt
                if not ph1_done[j]:
                    emit_phase1(j, s, rslot, qn, kbase)
                stX = pA if j % 2 == 0 else pB
                g_j = gather_of[j]
                wait("vector", "LGM", g_j)
                mv = op("vector", "CV", lambda e: e.tensor_reduce(
                    out=m11[:], in_=mrow[:], axis=AX.X, op=ALU.min))
                wait("tensor", "CV", mv)
                mb = op("tensor", "CT", lambda e: e.matmul(
                    pC[:, 124:125], lhsT=ones1f[:, :], rhs=m11[:], start=True, stop=True))
                wait("vector", "CT", mb)
                opq("vector", lambda e: e.tensor_copy(out=mbias[:], in_=pC[:, 124:125]))
                km1 = m1z if j == ntiles - 1 else m1a
                km2 = m2z if j == 0 else m2a
                if j == 0:
                    wait("vector", "LCG", ld_masks)
                opq("vector", lambda e, km1=km1, stX=stX: e.tensor_tensor(
                    out=a1[:], in0=stX[:, :], in1=km1[:], op=ALU.add))
                aan = op("vector", "CV", lambda e, km2=km2, stX=stX: e.tensor_tensor(
                    out=a2[:], in0=stX[:, :], in1=km2[:], op=ALU.add))
                aan_of[j] = aan
                wait("scalar", "CV", aan)
                opq("scalar", lambda e: e.activation(out=e1[:], in_=a1[:], func=ACT.Exp,
                                                     bias=mbias[:]))
                een = op("scalar", "CS", lambda e: e.activation(out=e2[:], in_=a2[:],
                                                                func=ACT.Exp, bias=mbias[:]))
                wait("tensor", "CS", een)
                edge = (j == 0) or (j == ntiles - 1)
                zn = 0
                for (eb, pz_) in ((e1, stX), (e2, pC)):
                    for half in range(2):
                        n0 = half * 512
                        if edge:
                            opq("tensor", lambda e, pz_=pz_, n0=n0: e.matmul(
                                pz_[:, n0:n0 + 512], lhsT=ones1f[:, :], rhs=epsrow[:, :],
                                start=True, stop=False))
                        def mkz(e, eb=eb, pz_=pz_, n0=n0, edge=edge):
                            return e.matmul(pz_[:, n0:n0 + 512], lhsT=onesK[:, :],
                                            rhs=eb[:, n0:n0 + 512], start=not edge, stop=True)
                        if pz_ is pC and half == 1:
                            zn = op("tensor", "CT", mkz)
                        else:
                            opq("tensor", mkz)
                jn = j + 1
                if jn < ntiles and not ph1_done[jn]:
                    s_n = min(jn // 4, nslabs - 1)
                    if s_n == s or qkv_em[s_n] >= 24:
                        emit_phase1(jn, s_n, s_n % 2, qts[jn], QT * jn - s_n * SLAB)
                        ph1_done[jn] = True
                if s + 1 < nslabs:
                    emit_qkv_groups(s + 1, min(3, min(24, 8 * (j - 4 * s + 1)) - qkv_em[s + 1]))
                wait("scalar", "CT", zn)
                opq("scalar", lambda e, stX=stX: e.activation(out=a1[:], in_=stX[:, :],
                                                              func=ACT.Ln))
                opq("scalar", lambda e: e.activation(out=a2[:], in_=pC[:, :], func=ACT.Ln))
                opq("scalar", lambda e: e.activation(out=r1[:], in_=a1[:], func=ACT.Exp,
                                                     scale=-1.0))
                rr2 = op("scalar", "CS", lambda e: e.activation(out=r2[:], in_=a2[:],
                                                                func=ACT.Exp, scale=-1.0))
                wait("vector", "CS", rr2)
                opq("vector", lambda e: e.tensor_tensor(out=ww[:], in0=e1[:], in1=r1[:], op=ALU.mult))
                opw("vector", lambda e: e.tensor_tensor(out=r2[:], in0=e2[:], in1=r2[:],
                                                        op=ALU.mult))
                wn = op("vector", "CV", lambda e: e.tensor_tensor(out=ww[:], in0=ww[:],
                                                                  in1=r2[:], op=ALU.add))
                wait("tensor", "CV", wn)
                wait("tensor", f"LTV{vb}", vt_of[j])
                wait("tensor", "CS", oev_prev)
                avn = 0
                for h in range(HEADS):
                    def mk(e, h=h, vb=vb, qn=qn):
                        return e.matmul(
                            pC[:, h * 128:h * 128 + qn],
                            lhsT=v_tm[vb][:, h * 128:(h + 1) * 128],
                            rhs=ww[:, h * 128:h * 128 + qn],
                            start=True, stop=True)
                    if h == HEADS - 1:
                        avn = op("tensor", "CT", mk)
                    else:
                        opq("tensor", mk)
                vtm_av[j] = avn
                slab_av[s] = avn
                wait("scalar", "CT", avn)
                oev = op("scalar", "CS", lambda e: e.activation(out=o_sb[:], in_=pC[:, :],
                                                                func=ACT.Copy))
                oev_prev = oev
                if j == 0:
                    wait("tensor", "LCG", ld_wout)
                wait("tensor", "CS", oev)
                wait("tensor", "CS", sev_prev)
                wait("tensor", "CV", qkv_ev[0])
                wait("tensor", "CV", qkv_ev[1])
                for half, pq_ in ((0, pD), (1, pE)):
                    n0 = half * 512
                    for ccn in range(8):
                        opq("tensor", lambda e, ccn=ccn, n0=n0, qn=qn, pq_=pq_: e.matmul(
                            pq_[0:qn, :],
                            lhsT=o_sb[:, ccn * 128:ccn * 128 + qn],
                            rhs=wout_b[:, ccn * C + n0:ccn * C + n0 + 512],
                            start=(ccn == 0), stop=False))
                    won = op("tensor", "CT", lambda e, j=j, n0=n0, qn=qn, pq_=pq_: e.matmul(
                        pq_[0:qn, :],
                        lhsT=bmul_s[:, j * QT:j * QT + qn],
                        rhs=bout_b[:, n0:n0 + 512], start=False, stop=True))
                wait("scalar", "CT", won)
                wait("scalar", "LGS", ssp_prev)
                opq("scalar", lambda e, qn=qn: e.activation(
                    out=s_t[0:qn, 0:512], in_=pD[0:qn, :], func=ACT.Copy))
                sev = op("scalar", "CS", lambda e, qn=qn: e.activation(
                    out=s_t[0:qn, 512:1024], in_=pE[0:qn, :], func=ACT.Copy))
                sev_prev = sev
                wait("gpsimd", "CS", sev)
                ssp_prev = dma("gpsimd", s_sp[j * QT:j * QT + qn, :], s_t[0:qn, :], s="LGS")
                if s + 1 < nslabs:
                    emit_qkv_groups(s + 1, min(24, 8 * (j - 4 * s + 1)) - qkv_em[s + 1])
                w1_drive((j + 1) / ntiles)

        wait("scalar", "LGC", cco1)
        sg1 = op("scalar", "CS", lambda e: e.activation(out=sc_s[:, 0:2], in_=clip_s[:, 0:2],
                                                        func=ACT.Sigmoid))
        wait("vector", "CS", sg1)
        opw("vector", lambda e: e.tensor_tensor(out=sc_s[:, 2:3], in0=sc_s[:, 0:1],
                                                in1=gmax1[:, 0:1], op=ALU.mult))
        opw("vector", lambda e: e.tensor_scalar(out=sc_s[:, 2:3], in0=sc_s[:, 2:3],
                                                scalar1=1.0 / QMAX, scalar2=None, op0=ALU.mult))
        sxv = op("vector", "CV", lambda e: e.reciprocal(out=sc_s[:, 3:4], in_=sc_s[:, 2:3]))
        wait("tensor", "CV", sxv)
        bb = op("tensor", "CT", lambda e: e.matmul(pE[:, 4:6], lhsT=ones1f[:, :],
                                                   rhs=sc_s[:, 2:4], start=True, stop=True))
        wait("vector", "CT", bb)
        opq("vector", lambda e: e.tensor_copy(out=sx_b[:], in_=pE[:, 4:5]))
        sxb = op("vector", "CV", lambda e: e.tensor_copy(out=invsx_b[:], in_=pE[:, 5:6]))

        w1_flush()
        wait("sync", "CV", w1_state["i8_free"])
        wait("sync", "LTW", cnt.get("LTW", 0))
        wait("sync", "LGL", w1_state["lg_done"])
        wait("scalar", "CV", w1_state["i8_free"])
        wait("scalar", "LTW", cnt.get("LTW", 0))
        xq_done = 0
        qv_done = [0, 0]
        stt_done = [0, 0]
        for m in range(NTT):
            tb = m % 2
            wait("scalar", "CV", stt_done[tb])
            ldx = dma("scalar", xrd_d[tb][:], xn_sp[2 + m * 128: 2 + (m + 1) * 128, :], s=f"LXD{tb}")
            wait("vector", f"LXD{tb}", ldx)
            if m == 0:
                wait("vector", "CV", sxb)
            stt_done[tb] = opw("vector", lambda e, tb=tb: e.scalar_tensor_tensor(
                out=uq[:], in0=xrd_d[tb][:], scalar=invsx_b[:],
                in1=cneg_b[:, 0:1].broadcast_to([128, C]), op0=ALU.mult, op1=ALU.max))
            opw("vector", lambda e: e.tensor_scalar(out=xq_i8[:], in0=uq[:], scalar1=127.0,
                                                    scalar2=None, op0=ALU.min))
            if m >= 2:
                wait("vector", f"LTQ{tb}", qv_done[tb])
            qv = op("vector", "CV", lambda e, tb=tb: e.tensor_copy(
                out=xq_t[tb][:], in_=xq_i8[:]))
            wait("sync", "CV", qv)
            xq_done = dma("sync",
                          xq_fm[:, :].rearrange("p (c f) -> p c f", c=8)[:, :, m * 128:(m + 1) * 128],
                          xq_t[tb][:, :].rearrange("p (c q) -> p c q", c=8),
                          s=f"LTQ{tb}", transpose=True)
            qv_done[tb] = xq_done

        w2_state = {"wd_tr": 0, "wstg_fr": 0, "emitted": 0, "pend": None,
                    "trn_of": [0, 0]}

        def w2_flush():
            if w2_state["pend"] is None:
                return
            sc_p, b_p, hch_p = w2_state["pend"]
            w2_state["pend"] = None
            wait("sync", "CS", sc_p)
            trn = dma("sync",
                      rhs_d[:, :].rearrange("p (c f) -> p c f", c=NIC)[:, :, hch_p * 128:(hch_p + 1) * 128],
                      wnb_d[b_p][:, :].rearrange("p (c q) -> p c q", c=NIC),
                      s="LTD", transpose=True)
            w2_state["wd_tr"] = trn
            w2_state["trn_of"][b_p] = trn

        def emit_w2_chunk(hch):
            w2_flush()
            if hch == 0:
                wait("gpsimd", "CT", slab_av[nslabs - 1])
            wait("gpsimd", "CV", w2_state["wstg_fr"])
            ldw = dma("gpsimd", wstg_d[:], wd[hch * 128:(hch + 1) * 128, :], s="LXV")
            wait("vector", "LXV", ldw)
            opw("vector", lambda e: e.reduce_max(out=winv[:, 0:1], in_=wstg_d[:], axis=AX.X,
                                                 apply_absolute_value=True))
            opq("vector", lambda e: e.tensor_scalar(out=winv[:, 1:2], in0=winv[:, 0:1],
                                                    scalar1=1.0 / QMAX, scalar2=None, op0=ALU.mult))
            opw("vector", lambda e: e.reciprocal(out=winv[:, 2:3], in_=winv[:, 0:1]))
            opw("vector", lambda e: e.tensor_scalar(
                out=winv[:, 2:3], in0=winv[:, 2:3], scalar1=QMAX * (1.0 - 2.0 ** -22),
                scalar2=None, op0=ALU.mult))
            i8n = op("vector", "CV", lambda e: e.scalar_tensor_tensor(
                out=wi8_d[:], in0=wstg_d[:], scalar=winv[:, 2:3],
                in1=wstg_d[:], op0=ALU.mult, op1=ALU.bypass))
            w2_state["wstg_fr"] = i8n
            b = hch % 2
            wait("scalar", "CV", i8n)
            wait("scalar", "LTD", w2_state["trn_of"][b])
            sc = op("scalar", "CS", lambda e, b=b: e.activation(
                out=wnb_d[b][:], in_=wi8_d[:], func=ACT.Copy, scale=winv[:, 1:2]))
            w2_state["pend"] = (sc, b, hch)

        def w2_drive(want):
            while w2_state["emitted"] < min(want, 8):
                emit_w2_chunk(w2_state["emitted"])
                w2_state["emitted"] += 1

        wait("tensor", "LTQ0", qv_done[0])
        wait("tensor", "LTQ1", qv_done[1])
        tpn_par = [0, 0]
        tp_sp = [0, 0]
        tp_sp_of = [0] * NIC
        ld_used = [0, 0]
        for ch in range(NIC):
            b = ch % 2
            if ch == 0:
                wait("sync", "LGL", w1_state["lg_done"])
            wait("sync", "CT", ld_used[b])
            dma("sync", lgs[b][:], lg_sp[ch, :, :], s=f"LXL{b}")
            ldu = dma("sync", lus[b][:], lu_sp[ch, :, :], s=f"LXL{b}")
            wait("tensor", f"LXL{b}", ldu)
            mmg = 0
            for nn in range((T + 511) // 512):
                n0 = nn * 512
                nw = min(512, T - n0)
                pg = pD if nn % 2 == 0 else pA[:, 0:512]
                pu = pE if nn % 2 == 0 else pA[:, 512:1024]
                wait("tensor", "CV", tpn_par[nn % 2])
                for ccn in range(8):
                    opq("tensor", lambda e, b=b, ccn=ccn, n0=n0, nw=nw, pg=pg: e.matmul(
                        pg[:, 0:nw], lhsT=lgs[b][:, ccn * 128:(ccn + 1) * 128],
                        rhs=xq_fm[:, ccn * T + n0: ccn * T + n0 + nw],
                        start=(ccn == 0), stop=(ccn == 7)))
                for ccn in range(8):
                    def mku(e, b=b, ccn=ccn, n0=n0, nw=nw, pu=pu):
                        return e.matmul(
                            pu[:, 0:nw], lhsT=lus[b][:, ccn * 128:(ccn + 1) * 128],
                            rhs=xq_fm[:, ccn * T + n0: ccn * T + n0 + nw],
                            start=(ccn == 0), stop=(ccn == 7))
                    if ccn == 7:
                        mmg = op("tensor", "CT", mku)
                    else:
                        opq("tensor", mku)
                wait("scalar", "CT", mmg)
                sg = silu_g[nn % 2]
                sgn = op("scalar", "CS", lambda e, nw=nw, pg=pg, sg=sg: e.activation(
                    out=sg[:, 0:nw], in_=pg[:, 0:nw], func=ACT.Silu, scale=sx_b[:]))
                wait("vector", "CS", sgn)
                if nn == 0 and ch >= 2:
                    wait("vector", f"LGT{b}", tp_sp[b])
                tpn = op("vector", "CV", lambda e, b=b, n0=n0, nw=nw, pu=pu, sg=sg: e.scalar_tensor_tensor(
                    out=tpre[b][:, n0:n0 + nw], in0=pu[:, 0:nw], scalar=sx_b[:],
                    in1=sg[:, 0:nw], op0=ALU.mult, op1=ALU.mult))
                tpn_par[nn % 2] = tpn
            ld_used[b] = cnt["CT"]
            opw("vector", lambda e, b=b: e.reduce_max(out=red1[:], in_=tpre[b][:],
                                                      axis=AX.X, apply_absolute_value=True))
            tpv = op("vector", "CV", lambda e: e.tensor_tensor(out=amax_t[:], in0=amax_t[:],
                                                               in1=red1[:], op=ALU.max))
            wait("gpsimd", "CV", tpv)
            tp_sp[b] = dma("gpsimd", t_sp[ch, :, :], tpre[b][:], s=f"LGT{b}")
            tp_sp_of[ch] = tp_sp[b]
            w2_drive((ch + 1) * 8 // NIC + 1)

        opw("vector", lambda e: e.tensor_copy(out=red1[0:1, 0:1], in_=red1[0:1, 0:1]))
        am2 = op("vector", "CV", lambda e: e.tensor_copy(out=red1[:], in_=amax_t[:]))
        wait("gpsimd", "CV", am2)
        dma("gpsimd", mrow_sp, red1[:, 0:1], s="LGM")
        wait("gpsimd", "LGM", cnt["LGM"])
        g2 = dma("gpsimd", mrow[:], mrow_sp, s="LGM")
        wait("vector", "LGM", g2)
        opq("vector", lambda e: e.reduce_max(out=m11[:], in_=mrow[:], axis=AX.X))
        bc2 = op("vector", "CV",
                 lambda e: e.tensor_copy(out=gmax2[:], in_=m11[:].broadcast_to([1, 8])))
        wait("gpsimd", "CV", bc2)
        cci2 = dma("gpsimd", cc2_in.ap(), gmax2[:], s="LGC")
        wait("gpsimd", "LGC", cci2)
        cc2n = op("gpsimd", "CG", lambda e: e.collective_compute(
            "AllReduce", ALU.max, replica_groups=[list(range(NCORES))],
            ins=[cc2_in.ap().opt()], outs=[cc2_out.ap().opt()]))
        wait("gpsimd", "CG", cc2n)
        cco2 = dma("gpsimd", gmax2[:], cc2_out.ap(), s="LGC")
        wait("vector", "LGC", cco2)
        opw("vector", lambda e: e.tensor_tensor(out=sc_s[:, 4:5], in0=sc_s[:, 1:2],
                                                in1=gmax2[:, 0:1], op=ALU.mult))
        opw("vector", lambda e: e.tensor_scalar(out=sc_s[:, 4:5], in0=sc_s[:, 4:5],
                                                scalar1=1.0 / QMAX, scalar2=None, op0=ALU.mult))
        shv = op("vector", "CV", lambda e: e.reciprocal(out=sc_s[:, 5:6], in_=sc_s[:, 4:5]))
        wait("tensor", "CV", shv)
        bb2 = op("tensor", "CT", lambda e: e.matmul(pE[:, 6:8], lhsT=ones1f[:, :],
                                                    rhs=sc_s[:, 4:6], start=True, stop=True))
        wait("vector", "CT", bb2)
        opq("vector", lambda e: e.tensor_copy(out=sh_b[:], in_=pE[:, 6:7]))
        shb = op("vector", "CV", lambda e: e.tensor_copy(out=invsh_b[:], in_=pE[:, 7:8]))

        d2_cs_last = cnt["CS"]
        w2_drive(8)
        w2_flush()
        wait("tensor", "LTD", w2_state["wd_tr"])
        trd_free = [0, 0]
        mm_of = [0, 0]
        out_free = [0, 0]
        srd_free = [0, 0]
        ov_par = [0, 0]
        gn_prev = [d2_cs_last] * 4
        for s in range(nslabs):
            sl = slabs[s]
            ntt = len(tiles_of[s])
            mmd = 0
            for ch in range(NIC):
                tb = ch % 2
                wait("sync", "CV", trd_free[tb])
                wait("sync", f"LGT{ch % 2}", tp_sp_of[ch])
                ldt = dma("sync", trd[tb][:, 0:sl], t_sp[ch, :, s * SLAB:s * SLAB + sl], s=f"LXT{tb}")
                wait("scalar", f"LXT{tb}", ldt)
                if s == 0 and ch == 0:
                    wait("scalar", "CV", shb)
                un = op("scalar", "CS", lambda e, tb=tb, sl=sl: e.activation(
                    out=trd[tb][:, 0:sl], in_=trd[tb][:, 0:sl], func=ACT.Copy,
                    scale=invsh_b[:]))
                wait("vector", "CS", un)
                opw("vector", lambda e, tb=tb, sl=sl: e.tensor_scalar(
                    out=h_i8[:, 0:sl], in0=trd[tb][:, 0:sl], scalar1=-128.0,
                    scalar2=127.0, op0=ALU.max, op1=ALU.min))
                wait("vector", "CT", mm_of[tb])
                hq = op("vector", "CV", lambda e, tb=tb, sl=sl: e.tensor_copy(
                    out=h_cb[tb][:, 0:sl], in_=h_i8[:, 0:sl]))
                trd_free[tb] = hq
                wait("tensor", "CV", hq)
                nmm = 2 * ntt
                imm = 0
                for r in range(ntt):
                    qn = qts[s * 4 + r]
                    if ch == 0:
                        wait("tensor", "CS", gn_prev[r])
                    for half in range(2):
                        if ntt >= 3 and r == 2:
                            pdst = (pD if half == 0 else pE)[0:qn, 0:512]
                        else:
                            pt = (pC, pA, None, pB)[r]
                            pdst = pt[0:qn, half * 512:half * 512 + 512]
                        imm += 1
                        def mkd(e, ch=ch, r=r, qn=qn, tb=tb, pdst=pdst, half=half):
                            return e.matmul(
                                pdst,
                                lhsT=h_cb[tb][:, r * QT: r * QT + qn],
                                rhs=rhs_d[:, ch * C + half * 512:ch * C + half * 512 + 512],
                                start=(ch == 0), stop=(ch == NIC - 1))
                        if imm == nmm:
                            mmd = op("tensor", "CT", mkd)
                        else:
                            opq("tensor", mkd)
                mm_of[tb] = mmd if mmd else mm_of[tb]
            for r in range(ntt):
                jj = s * 4 + r
                qn = qts[jj]
                wait("scalar", "CT", mmd)
                wait("scalar", "CV", ov_par[jj % 2])
                gb = g_t[jj % 2]
                if ntt >= 3 and r == 2:
                    opq("scalar", lambda e, qn=qn, gb=gb: e.activation(
                        out=gb[0:qn, 0:512], in_=pD[0:qn, :], func=ACT.Silu,
                        scale=sh_b[0:qn, :]))
                    gn = op("scalar", "CS", lambda e, qn=qn, gb=gb: e.activation(
                        out=gb[0:qn, 512:1024], in_=pE[0:qn, :], func=ACT.Silu,
                        scale=sh_b[0:qn, :]))
                else:
                    pdn = (pC, pA, None, pB)[r]
                    gn = op("scalar", "CS", lambda e, qn=qn, pdn=pdn, gb=gb: e.activation(
                        out=gb[0:qn, :], in_=pdn[0:qn, :], func=ACT.Silu,
                        scale=sh_b[0:qn, :]))
                gn_prev[r] = gn
                rb = r % 2
                wait("scalar", "CV", srd_free[rb])
                dma("scalar", s_rd[rb][0:qn, :], s_sp[jj * QT:jj * QT + qn, :], s=f"LXF{rb}")
                ldx2 = dma("scalar", x_rd[rb][0:qn, :], xh[2 + jj * QT:2 + jj * QT + qn, :], s=f"LXF{rb}")
                wait("vector", "CS", gn)
                wait("vector", f"ST{rb}", out_free[rb])
                wait("vector", f"LXF{rb}", ldx2)
                opw("vector", lambda e, rb=rb, qn=qn, gb=gb: e.tensor_tensor(
                    out=out_t[rb][0:qn, :], in0=s_rd[rb][0:qn, :], in1=gb[0:qn, :],
                    op=ALU.mult))
                ov = op("vector", "CV", lambda e, rb=rb, qn=qn: e.tensor_tensor(
                    out=out_t[rb][0:qn, :], in0=out_t[rb][0:qn, :], in1=x_rd[rb][0:qn, :],
                    op=ALU.add))
                srd_free[rb] = ov
                ov_par[jj % 2] = ov
                wait("gpsimd", "CV", ov)
                out_free[rb] = dma("gpsimd", out[jj * QT:jj * QT + qn, :],
                                   out_t[rb][0:qn, :], s=f"ST{rb}")

        wait("sync", "ST0", cnt.get("ST0", 0))
        wait("sync", "ST1", cnt.get("ST1", 0))

        E(nc.allow_low_precision("bf16 softmax weights / quantized operands are sanctioned for this fp32 reference (bf16 matmul + fp32 accum)"))
        block = E(nc.Block())

        @block.sync
        def _(e):
            for f in prog["sync"]:
                f(e)

        @block.vector
        def _(e):
            for f in prog["vector"]:
                f(e)

        @block.scalar
        def _(e):
            for f in prog["scalar"]:
                f(e)

        @block.tensor
        def _(e):
            for f in prog["tensor"]:
                f(e)

        @block.gpsimd
        def _(e):
            for f in prog["gpsimd"]:
                f(e)

    return nc


def make_masks():
    k = np.arange(128)[:, None]
    q = np.arange(QT)[None, :]
    d = k - q - 2
    even = (q % 2) == 0
    m1 = np.where(even, (d >= 0) & (d <= 3), (d >= -1) & (d <= 2))
    m2 = np.where(even, (d >= -2) & (d <= 1), (d >= -3) & (d <= 0))

    def rep(m):
        a = np.full((128, 1024), NEG, np.float32)
        for h in range(HEADS):
            a[:, h * 128:h * 128 + QT] = np.where(m, 0.0, NEG)
        return a
    return rep(m1), rep(m2)


def kernel(L0_attn_x, L0_attn_w_qkv, L0_attn_w_out, L0_attn_b_out,
           L0_ffn_clip_x, L0_ffn_clip_down, L0_ffn_w_gate, L0_ffn_w_up,
           L0_ffn_w_down):
    x = np.ascontiguousarray(np.asarray(L0_attn_x, dtype=np.float32)[0])
    L = x.shape[0]
    T = L // NCORES
    nfull, rem = T // QT, T % QT
    ntiles = nfull + (1 if rem else 0)
    nxt = (T + 4 + 127) // 128
    XROWS = nxt * 128

    wqkv = np.ascontiguousarray(np.asarray(L0_attn_w_qkv, np.float32))
    woutw = np.ascontiguousarray(np.asarray(L0_attn_w_out, np.float32))
    boutv = np.ascontiguousarray(np.asarray(L0_attn_b_out, np.float32)).reshape(1, C)
    wgv = np.ascontiguousarray(np.asarray(L0_ffn_w_gate, np.float32))
    wuv = np.ascontiguousarray(np.asarray(L0_ffn_w_up, np.float32))
    wdv = np.ascontiguousarray(np.asarray(L0_ffn_w_down, np.float32))
    clip2 = np.zeros((1, 8), np.float32)
    clip2[0, 0] = np.asarray(L0_ffn_clip_x, np.float32).reshape(-1)[0]
    clip2[0, 1] = np.asarray(L0_ffn_clip_down, np.float32).reshape(-1)[0]

    mask1, mask2 = make_masks()
    lastq = rem if rem else QT
    mask2_first = mask2.copy()
    for h in range(HEADS):
        mask2_first[:, h * 128:h * 128 + 2] = NEG
    mask1_last = mask1.copy()
    for h in range(HEADS):
        mask1_last[:, h * 128 + lastq - 2:h * 128 + lastq] = NEG

    bmul_base = np.zeros((1, ntiles * QT), np.float32)
    for j in range(ntiles):
        qn = min(QT, T - j * QT)
        bmul_base[0, j * QT:j * QT + qn] = 2.0

    in_maps = []
    for c in range(NCORES):
        t0 = c * T
        xa = np.zeros((XROWS, C), np.float32)
        lo, hi = t0 - 2, t0 + T + 2
        src_lo, src_hi = max(0, lo), min(L, hi)
        xa[src_lo - lo: src_hi - lo] = x[src_lo:src_hi]
        bm = bmul_base.copy()
        m1zv, m2zv = mask1, mask2
        if c == 0:
            bm[0, 0:2] = 1.0
            m2zv = mask2_first
        if c == NCORES - 1:
            bm[0, (ntiles - 1) * QT + lastq - 2:(ntiles - 1) * QT + lastq] = 1.0
            m1zv = mask1_last
        in_maps.append(dict(
            xh=xa, wqkv=wqkv, wout=woutw, bout=boutv, clip2=clip2,
            wg=wgv, wu=wuv, wd=wdv,
            mask1a=mask1, mask2a=mask2, mask1z=m1zv, mask2z=m2zv, bmul=bm))

    nc = build_nc(T)
    res = run_bass_kernel_spmd(nc, in_maps, core_ids=list(range(NCORES)))
    if DEBUG_SPILLS:
        DEBUG_RES.clear()
        DEBUG_RES.append(res)
    outs = [np.asarray(res.results[c]["out"], np.float32) for c in range(NCORES)]
    return np.concatenate(outs, axis=0)[None, :, :]
